# version 05
# baseline (speedup 1.0000x reference)
import contextlib
import numpy as np
import concourse.bass as bass
import concourse.mybir as mybir
from concourse.bass_utils import run_bass_kernel_spmd

F32 = mybir.dt.float32
BF16 = mybir.dt.bfloat16
F8 = mybir.dt.float8e4
AF = mybir.ActivationFunctionType
ALU = mybir.AluOpType
AX = mybir.AxisListType

D = 2048
S = 8192
NC = 8
TT = 512
NTT = S // TT
TOWN = 1024
HID = 8192
NDCH = D // 128
NHCH = HID // 128
SC = float(1.0 / np.sqrt(128.0))


class Ctr:
    def __init__(self, es, nc, name):
        self.sem = es.enter_context(nc.semaphore(name))
        self.v = 0

    def inc(self, instr, n=1):
        instr.then_inc(self.sem, n)
        self.v += n
        return self.v


def build_nc():
    nc = bass.Bass("TRN2", target_bir_lowering=False, debug=False)

    xf = nc.dram_tensor("xf", [D, S], F32, kind="ExternalInput").ap()
    xo = nc.dram_tensor("xo", [D, TOWN], F32, kind="ExternalInput").ap()
    wqkvT = nc.dram_tensor("wqkvT", [D, 768], F32, kind="ExternalInput").ap()
    bqkv = nc.dram_tensor("bqkv", [768], F32, kind="ExternalInput").ap()
    woutT = nc.dram_tensor("woutT", [256, D], F32, kind="ExternalInput").ap()
    bout = nc.dram_tensor("bout", [D], F32, kind="ExternalInput").ap()
    wg0 = nc.dram_tensor("wg0", [D, HID], F32, kind="ExternalInput").ap()
    wg1 = nc.dram_tensor("wg1", [D, HID], F32, kind="ExternalInput").ap()
    wl = nc.dram_tensor("wl", [HID, D], F32, kind="ExternalInput").ap()
    outp = nc.dram_tensor("outp", [D, TOWN], F32, kind="ExternalOutput").ap()

    qdr = nc.dram_tensor("qdr", [2, 2, 128, S], BF16, kind="Internal").ap()
    kdr = nc.dram_tensor("kdr", [2, 2, 128, S], BF16, kind="Internal").ap()
    vdr = nc.dram_tensor("vdr", [2, 2, S, 128], BF16, kind="Internal").ap()
    spart = nc.dram_tensor("spart", [NC, D, TOWN], F32, kind="Internal").ap()
    rows_dr = nc.dram_tensor("rows_dr", [2, 18, TT], F32, kind="Internal").ap()
    sred = nc.dram_tensor("sred", [D, TOWN], F32, kind="Internal").ap()

    mask_np = np.zeros((128, 4 * TT), dtype=np.float32)
    for off in range(4):
        mask_np[:, off * TT:(off + 1) * TT] = (
            np.arange(128)[:, None] + 128 * off <= np.arange(TT)[None, :])
    masks_dr = nc.inline_tensor(mask_np, name="masks_c")
    ones_dr = nc.inline_tensor(np.ones((128, 128), dtype=np.float32), name="ones_c")
    ident_dr = nc.inline_tensor(np.eye(128, dtype=np.float32), name="ident_c")
    import ml_dtypes as _mld
    onesb_dr = nc.inline_tensor(np.ones((128, 128), dtype=_mld.bfloat16), name="onesb_c")
    identb_dr = nc.inline_tensor(np.eye(128).astype(_mld.bfloat16), name="identb_c")
    eps_dr = nc.inline_tensor(np.full((128, 1), 1e-5, dtype=np.float32), name="eps_c")

    es = contextlib.ExitStack()
    with es:
        MASKS = es.enter_context(nc.sbuf_tensor("MASKS", [128, 4 * TT], F32))
        ONES = es.enter_context(nc.sbuf_tensor("ONES", [128, 128], F32))
        ONESB = es.enter_context(nc.sbuf_tensor("ONESB", [128, 128], BF16))
        IDENTB = es.enter_context(nc.sbuf_tensor("IDENTB", [128, 128], BF16))
        IDENT = es.enter_context(nc.sbuf_tensor("IDENT", [128, 128], F32))
        EPS = es.enter_context(nc.sbuf_tensor("EPS", [128, 1], F32))
        BQKV = es.enter_context(nc.sbuf_tensor("BQKV", [128, 6], F32))
        BOUT = es.enter_context(nc.sbuf_tensor("BOUT", [128, 16], F32))
        NRM = es.enter_context(nc.sbuf_tensor("NRM", [1, 64], F32))
        TMPR = es.enter_context(nc.sbuf_tensor("TMPR", [1, TT], F32))
        TMPM = es.enter_context(nc.sbuf_tensor("TMPM", [1, TT], F32))
        BB = es.enter_context(nc.sbuf_tensor("BB", [128, 32], F32))
        TMP = es.enter_context(nc.sbuf_tensor("TMP", [1, 2 * TT], F32))
        BROW = es.enter_context(nc.sbuf_tensor("BROW", [1, 64], F32))
        BBPAD = es.enter_context(nc.sbuf_tensor("BBPAD", [128, 32], F32))

        s_ld = Ctr(es, nc, "s_ld")
        s_x0 = Ctr(es, nc, "s_x0")
        s_x1 = Ctr(es, nc, "s_x1")
        s_pd = Ctr(es, nc, "s_pd")
        s_wq = Ctr(es, nc, "s_wq")
        s_kv = Ctr(es, nc, "s_kv")
        s_qt0 = Ctr(es, nc, "s_qt0")
        s_qt1 = Ctr(es, nc, "s_qt1")
        s_wo = Ctr(es, nc, "s_wo")
        s_w0 = Ctr(es, nc, "s_w0")
        s_w1 = Ctr(es, nc, "s_w1")
        s_d0 = Ctr(es, nc, "s_d0")
        s_d1 = Ctr(es, nc, "s_d1")
        g_sp0 = Ctr(es, nc, "g_sp0")
        g_sp1 = Ctr(es, nc, "g_sp1")
        g_o0 = Ctr(es, nc, "g_o0")
        g_o1 = Ctr(es, nc, "g_o1")
        g_cst = Ctr(es, nc, "g_cst")
        g_sp = Ctr(es, nc, "g_sp")
        g_rows = Ctr(es, nc, "g_rows")
        g_out = Ctr(es, nc, "g_out")
        cc = Ctr(es, nc, "cc")
        t_p = Ctr(es, nc, "t_p")
        v_p = Ctr(es, nc, "v_p")
        a_p = Ctr(es, nc, "a_p")

        blk = es.enter_context(nc.Block())

        @blk.gpsimd
        def _(g):
            g_cst.inc(g.dma_start(out=MASKS[:], in_=masks_dr.ap()), 16)
            g_cst.inc(g.dma_start(out=ONES[:], in_=ones_dr.ap()), 16)
            g_cst.inc(g.dma_start(out=ONESB[:], in_=onesb_dr.ap()), 16)
            g_cst.inc(g.dma_start(out=IDENTB[:], in_=identb_dr.ap()), 16)
            g_cst.inc(g.dma_start(out=IDENT[:], in_=ident_dr.ap()), 16)
            g_cst.inc(g.dma_start(out=EPS[:], in_=eps_dr.ap()), 16)
            g_cst.inc(g.dma_start(out=BQKV[:], in_=bqkv.rearrange("(c p) -> p c", p=128), allow_slow_non_contiguous=True), 16)
            g_cst.inc(g.dma_start(out=BOUT[:], in_=bout.rearrange("(c p) -> p c", p=128), allow_slow_non_contiguous=True), 16)

        CST = g_cst.v

        def fence():
            snaps = [(c.sem, c.v) for c in
                     (t_p, v_p, a_p, g_sp, g_sp0, g_sp1, g_rows, g_out, g_cst,
                      s_x0, s_x1, s_pd, s_wq, s_kv, s_qt0, s_qt1, s_wo, s_w0,
                      s_w1, s_d0, s_d1)]

            def emit(eng):
                for sem, val in snaps:
                    if val:
                        eng.wait_ge(sem, val)

            @blk.tensor
            def _(t):
                emit(t)

            @blk.vector
            def _(v):
                emit(v)

            @blk.scalar
            def _(s):
                emit(s)

            @blk.sync
            def _(sy):
                emit(sy)

            @blk.gpsimd
            def _(g):
                emit(g)

        def emit_ln_stats(src, ntt, row_base, SUMS, SQ, XCH, X2):
            t0, v0, a0, gr0 = t_p.v, v_p.v, a_p.v, g_rows.v
            slots = [s_x0, s_x1]
            ld_vals = []

            def x2_idx(n):
                return 20 * (n // NDCH) + (n % NDCH) + 1

            @blk.sync
            def _(sy):
                for tt in range(ntt):
                    for dch in range(NDCH):
                        n = tt * NDCH + dch
                        b = n % 2
                        if n >= 2:
                            sy.wait_ge(t_p.sem, t0 + 2 * (n - 1))
                            sy.wait_ge(v_p.sem, v0 + x2_idx(n - 2))
                        ld_vals.append(slots[b].inc(sy.dma_start(
                            out=XCH[:, b * TT:(b + 1) * TT],
                            in_=src[dch * 128:(dch + 1) * 128, tt * TT:(tt + 1) * TT]), 16))

            @blk.vector
            def _(v):
                for tt in range(ntt):
                    for dch in range(NDCH):
                        n = tt * NDCH + dch
                        b = n % 2
                        v.wait_ge(slots[b].sem, ld_vals[n])
                        if n >= 2:
                            v.wait_ge(t_p.sem, t0 + 2 * (n - 1))
                        v_p.inc(v.tensor_tensor(
                            out=X2[:, b * TT:(b + 1) * TT],
                            in0=XCH[:, b * TT:(b + 1) * TT],
                            in1=XCH[:, b * TT:(b + 1) * TT], op=ALU.mult))
                    v.wait_ge(a_p.sem, a0 + 2 * tt + 1)
                    v.wait_ge(v_p.sem, v0 + 20 * tt + 16)
                    v_p.inc(v.tensor_tensor(out=TMP[0:1, TT:2 * TT], in0=TMP[0:1, 0:TT],
                                            in1=TMP[0:1, 0:TT], op=ALU.mult))
                    v.wait_ge(v_p.sem, v0 + 20 * tt + 17)
                    v.wait_ge(t_p.sem, t0 + 2 * NDCH * (tt + 1))
                    v_p.inc(v.scalar_tensor_tensor(
                        out=TMP[0:1, TT:2 * TT], in0=SQ[0:1, :], scalar=1.0 / D,
                        in1=TMP[0:1, TT:2 * TT], op0=ALU.mult, op1=ALU.subtract))
                    v.wait_ge(a_p.sem, a0 + 2 * tt + 2)
                    v.wait_ge(v_p.sem, v0 + 20 * tt + 18)
                    if tt > 0:
                        v.wait_ge(g_rows.sem, gr0 + 32 * tt)
                    v_p.inc(v.reciprocal(TMPR[0:1, :], TMP[0:1, TT:2 * TT]))
                    v.wait_ge(v_p.sem, v0 + 20 * tt + 19)
                    v_p.inc(v.tensor_tensor(out=TMPM[0:1, :], in0=TMP[0:1, 0:TT],
                                            in1=TMPR[0:1, :], op=ALU.mult))

            @blk.gpsimd
            def _(g):
                for tt in range(ntt):
                    g.wait_ge(v_p.sem, v0 + 20 * tt + 20)
                    g_rows.inc(g.dma_start(out=rows_dr[0, row_base + tt, :],
                                           in_=TMPR[0:1, :]), 16)
                    g_rows.inc(g.dma_start(out=rows_dr[1, row_base + tt, :],
                                           in_=TMPM[0:1, :]), 16)

            @blk.tensor
            def _(t):
                t.wait_ge(g_cst.sem, CST)
                t.wait_ge(v_p.sem, v0)
                t.wait_ge(a_p.sem, a0)
                for tt in range(ntt):
                    if tt > 0:
                        t.wait_ge(a_p.sem, a0 + 2 * tt - 1)
                        t.wait_ge(v_p.sem, v0 + 20 * tt - 2)
                    for dch in range(NDCH):
                        n = tt * NDCH + dch
                        b = n % 2
                        t.wait_ge(slots[b].sem, ld_vals[n])
                        t_p.inc(t.matmul(SUMS[:], ONES[:], XCH[:, b * TT:(b + 1) * TT],
                                         start=(dch == 0), stop=(dch == NDCH - 1)))
                        t.wait_ge(v_p.sem, v0 + x2_idx(n))
                        t_p.inc(t.matmul(SQ[:], ONES[:], X2[:, b * TT:(b + 1) * TT],
                                         start=(dch == 0), stop=(dch == NDCH - 1)))

            @blk.scalar
            def _(sc_):
                sc_.wait_ge(g_cst.sem, CST)
                for tt in range(ntt):
                    sc_.wait_ge(t_p.sem, t0 + 2 * NDCH * (tt + 1))
                    sc_.wait_ge(v_p.sem, v0 + 20 * tt)
                    a_p.inc(sc_.activation(out=TMP[0:1, 0:TT], in_=SUMS[0:1, :],
                                           func=AF.Identity, scale=1.0 / D))
                    sc_.wait_ge(v_p.sem, v0 + 20 * tt + 18)
                    a_p.inc(sc_.activation(out=TMP[0:1, TT:2 * TT], in_=TMP[0:1, TT:2 * TT],
                                           func=AF.Sqrt, bias=EPS[0:1, 0:1]))

        def emit_ln_tile(src, tt, row_slot, rows_g, XCH, T1, XN2, XNH, XNL, RBS, MRBS,
                         RP1, RP2, AUX1, AUX2,
                         pads_free_t, pads_free_v, first_waits, prev, xn_consumer):
            @blk.sync
            def _(sy):
                sy.wait_ge(g_rows.sem, rows_g)
                sy.wait_ge(v_p.sem, pads_free_v)
                if pads_free_t:
                    sy.wait_ge(t_p.sem, pads_free_t)
                s_pd.inc(sy.dma_start(out=RP1[0:1, :], in_=rows_dr[0, row_slot, :]), 16)
                s_pd.inc(sy.dma_start(out=RP2[0:1, :], in_=rows_dr[1, row_slot, :]), 16)

            pad_ld = s_pd.v
            a_at = a_p.v

            @blk.tensor
            def _(t):
                t.wait_ge(s_pd.sem, pad_ld)
                t.wait_ge(v_p.sem, pads_free_v)
                t.wait_ge(a_p.sem, a_at)
                t_p.inc(t.matmul(AUX1[:], ONES[:], RP1[:], start=True, stop=True))
                t_p.inc(t.matmul(AUX2[:], ONES[:], RP2[:], start=True, stop=True))

            t_bc = t_p.v

            @blk.vector
            def _(v):
                v.wait_ge(t_p.sem, t_bc)
                v.wait_ge(v_p.sem, v_p.v)
                v_p.inc(v.tensor_copy(RBS[:], AUX1[:]))
                v_p.inc(v.tensor_copy(MRBS[:], AUX2[:]))

            v_bc = v_p.v
            xn_done = []
            free_marks = []
            ld_vals = []
            slots = [s_x0, s_x1]

            for dch in range(NDCH):
                b = dch % 2

                @blk.sync
                def _(sy, dch=dch, b=b):
                    if dch >= 2:
                        sy.wait_ge(v_p.sem, xn_done[dch - 2])
                    else:
                        for sem, val in first_waits:
                            sy.wait_ge(sem, val)
                        if prev is not None:
                            sy.wait_ge(v_p.sem, prev[0][NDCH - 2 + dch])
                    ld_vals.append(slots[b].inc(sy.dma_start(
                        out=XCH[:, b * TT:(b + 1) * TT],
                        in_=src[dch * 128:(dch + 1) * 128, tt * TT:(tt + 1) * TT]), 16))

                @blk.vector
                def _(v, dch=dch, b=b):
                    v.wait_ge(slots[b].sem, ld_vals[dch])
                    v.wait_ge(v_p.sem, v_bc)
                    if dch >= 2:
                        sem, val = free_marks[dch - 2]
                        v.wait_ge(sem, val)
                        v.wait_ge(v_p.sem, xn_done[dch - 2])
                    elif prev is not None:
                        sem, val = prev[1][NDCH - 2 + dch]
                        v.wait_ge(sem, val)
                        v.wait_ge(v_p.sem, prev[0][NDCH - 2 + dch])
                    v_p.inc(v.tensor_tensor(out=T1[:, b * TT:(b + 1) * TT],
                                            in0=XCH[:, b * TT:(b + 1) * TT],
                                            in1=RBS[:], op=ALU.mult))
                    v.wait_ge(v_p.sem, v_p.v)
                    v_p.inc(v.tensor_tensor(out=XN2[:, b * TT:(b + 1) * TT],
                                            in0=T1[:, b * TT:(b + 1) * TT],
                                            in1=MRBS[:], op=ALU.subtract))
                    if XNH is not None:
                        v.wait_ge(v_p.sem, v_p.v)
                        v_p.inc(v.tensor_copy(XNH[:, b * TT:(b + 1) * TT],
                                              XN2[:, b * TT:(b + 1) * TT]))
                        v.wait_ge(v_p.sem, v_p.v)
                        v_p.inc(v.tensor_tensor(out=XNL[:, b * TT:(b + 1) * TT],
                                                in0=XN2[:, b * TT:(b + 1) * TT],
                                                in1=XNH[:, b * TT:(b + 1) * TT],
                                                op=ALU.subtract))

                xn_done.append(v_p.v)
                free_marks.append(xn_consumer(dch, b, v_p.v))

            return xn_done, free_marks

        with contextlib.ExitStack() as ea:
            WQ = ea.enter_context(nc.sbuf_tensor("WQ", [128, 6 * NDCH * 128], F32))
            WQH = ea.enter_context(nc.sbuf_tensor("WQH", [128, 6 * NDCH * 128], BF16))
            WQL = ea.enter_context(nc.sbuf_tensor("WQL", [128, 6 * NDCH * 128], BF16))
            XCH = ea.enter_context(nc.sbuf_tensor("XCH", [128, 2 * TT], F32))
            X2 = ea.enter_context(nc.sbuf_tensor("X2", [128, 2 * TT], F32))
            T1 = ea.enter_context(nc.sbuf_tensor("T1", [128, 2 * TT], F32))
            XN2 = ea.enter_context(nc.sbuf_tensor("XN2", [128, 2 * TT], F32))
            XNH = ea.enter_context(nc.sbuf_tensor("XNH", [128, 2 * TT], BF16))
            XNL = ea.enter_context(nc.sbuf_tensor("XNL", [128, 2 * TT], BF16))
            RBS = ea.enter_context(nc.sbuf_tensor("RBS", [128, TT], F32))
            MRBS = ea.enter_context(nc.sbuf_tensor("MRBS", [128, TT], F32))
            RP1 = ea.enter_context(nc.sbuf_tensor("RP1", [128, TT], F32))
            RP2 = ea.enter_context(nc.sbuf_tensor("RP2", [128, TT], F32))
            EV = ea.enter_context(nc.sbuf_tensor("EV", [128, 6 * TT], F32))
            EVH = ea.enter_context(nc.sbuf_tensor("EVH", [128, 6 * TT], BF16))
            EVL = ea.enter_context(nc.sbuf_tensor("EVL", [128, 6 * TT], BF16))
            VTT = ea.enter_context(nc.sbuf_tensor("VTT", [128, 4 * TT], BF16))
            KSQ = ea.enter_context(nc.sbuf_tensor("KSQ", [128, TT], F32))

            @blk.sync
            def _(sy):
                src = wqkvT.rearrange("(dch p) (ech j) -> p ech dch j", p=128, j=128)
                s_wq.inc(sy.dma_start(out=WQ.ap().rearrange(
                    "p (ech dch j) -> p ech dch j", ech=6, dch=NDCH), in_=src), 16)

            WQ_LD = s_wq.v

            @blk.vector
            def _(v):
                v.wait_ge(s_wq.sem, WQ_LD)
                v.wait_ge(v_p.sem, v_p.v)
                v_p.inc(v.tensor_copy(WQH[:], WQ[:]))
                v.wait_ge(v_p.sem, v_p.v)
                v_p.inc(v.tensor_tensor(out=WQL[:], in0=WQ[:], in1=WQH[:],
                                        op=ALU.subtract))

            WQHL_V = v_p.v

            with contextlib.ExitStack() as ea1:
                SUMS = ea1.enter_context(nc.psum_tensor("SUMS", [128, TT], F32))
                SQ = ea1.enter_context(nc.psum_tensor("SQ", [128, TT], F32))
                emit_ln_stats(xf, NTT, 0, SUMS, SQ, XCH, X2)

            STATS_T, STATS_V = t_p.v, v_p.v
            ROWS_A = g_rows.v

            QKV = ea.enter_context(nc.psum_tensor("QKV", [128, 6 * TT], F32))
            AUX1 = ea.enter_context(nc.psum_tensor("AUX1", [128, TT], F32))
            AUX2 = ea.enter_context(nc.psum_tensor("AUX2", [128, TT], F32))

            @blk.vector
            def _(v):
                v.wait_ge(v_p.sem, v_p.v)
                v_p.inc(v.memset(RP1[:], 0.0))
                v_p.inc(v.memset(RP2[:], 0.0))

            pads_free_t = 0
            pads_free_v = v_p.v
            first_waits = [(v_p.sem, STATS_V), (t_p.sem, STATS_T)]
            prevA = None
            ev_free = []
            vdr_g = 0

            for tt in range(NTT):
                def qkv_consume(dch, b, xn_v, a_pre=a_p.v):
                    @blk.tensor
                    def _(t, dch=dch, b=b, xn_v=xn_v, a_pre=a_pre):
                        t.wait_ge(v_p.sem, max(xn_v, WQHL_V))
                        if dch == 0:
                            t.wait_ge(a_p.sem, a_pre)
                        bs = b * TT
                        for ech in range(6):
                            w0 = (ech * NDCH + dch) * 128
                            t_p.inc(t.matmul(
                                QKV[:, ech * TT:(ech + 1) * TT],
                                WQH[:, w0:w0 + 128], XNH[:, bs:bs + TT],
                                start=(dch == 0), stop=False))
                            t_p.inc(t.matmul(
                                QKV[:, ech * TT:(ech + 1) * TT],
                                WQH[:, w0:w0 + 128], XNL[:, bs:bs + TT],
                                start=False, stop=False))
                            t_p.inc(t.matmul(
                                QKV[:, ech * TT:(ech + 1) * TT],
                                WQL[:, w0:w0 + 128], XNH[:, bs:bs + TT],
                                start=False, stop=(dch == NDCH - 1)))
                    return (t_p.sem, t_p.v)

                prevA = emit_ln_tile(xf, tt, tt, ROWS_A, XCH, T1, XN2, XNH, XNL, RBS, MRBS,
                                     RP1, RP2, AUX1, AUX2, pads_free_t, pads_free_v,
                                     first_waits, prevA, qkv_consume)
                first_waits = []
                t_qkv_done = t_p.v

                @blk.scalar
                def _(sc_, t_qkv_done=t_qkv_done, ev_free=list(ev_free)):
                    sc_.wait_ge(t_p.sem, t_qkv_done)
                    sc_.wait_ge(g_cst.sem, CST)
                    for sem, val in ev_free:
                        sc_.wait_ge(sem, val)
                    for ech in range(6):
                        a_p.inc(sc_.activation(out=EV[:, ech * TT:(ech + 1) * TT],
                                               in_=QKV[:, ech * TT:(ech + 1) * TT],
                                               func=AF.Identity, bias=BQKV[:, ech:ech + 1],
                                               scale=1.0))

                a_ev = a_p.v

                @blk.vector
                def _(v, a_ev=a_ev, gprev=g_sp.v, tprev=t_p.v):
                    v.wait_ge(a_p.sem, a_ev)
                    v.wait_ge(v_p.sem, v_p.v)
                    v.wait_ge(g_sp.sem, gprev)
                    v.wait_ge(t_p.sem, tprev)
                    v_p.inc(v.tensor_copy(EVH[:], EV[:]))
                    v.wait_ge(v_p.sem, v_p.v)
                    v_p.inc(v.tensor_tensor(out=EVL[:], in0=EV[:], in1=EVH[:],
                                            op=ALU.subtract))

                ev_hl = v_p.v

                @blk.gpsimd
                def _(g, tt=tt, ev_hl=ev_hl):
                    g.wait_ge(v_p.sem, ev_hl)
                    for h in range(2):
                        for lv, EVX in ((0, EVH), (1, EVL)):
                            g_sp.inc(g.dma_start(
                                out=qdr[h, lv, :, tt * TT:(tt + 1) * TT],
                                in_=EVX[:, h * TT:(h + 1) * TT]), 16)
                            g_sp.inc(g.dma_start(
                                out=kdr[h, lv, :, tt * TT:(tt + 1) * TT],
                                in_=EVX[:, (2 + h) * TT:(3 + h) * TT]), 16)

                for i, (ech, slot) in enumerate(
                        [(0, 0), (1, 16), (2, 32), (3, 48)]):
                    @blk.vector
                    def _(v, ech=ech, a_ev=a_ev, tprev=t_p.v, vprev=v_p.v):
                        v.wait_ge(a_p.sem, a_ev)
                        v.wait_ge(t_p.sem, tprev)
                        v.wait_ge(v_p.sem, vprev)
                        v_p.inc(v.tensor_tensor(out=KSQ[:],
                                                in0=EV[:, ech * TT:(ech + 1) * TT],
                                                in1=EV[:, ech * TT:(ech + 1) * TT],
                                                op=ALU.mult))

                    sq_v = v_p.v

                    @blk.tensor
                    def _(t, sq_v=sq_v, vfree=v_p.v):
                        t.wait_ge(v_p.sem, sq_v)
                        t_p.inc(t.matmul(AUX1[:], ONES[:], KSQ[:], start=True, stop=True))

                    mm_t = t_p.v

                    @blk.vector
                    def _(v, mm_t=mm_t, slot=slot, tt=tt):
                        v.wait_ge(t_p.sem, mm_t)
                        v.wait_ge(v_p.sem, v_p.v)
                        v_p.inc(v.reduce_max(NRM[0:1, slot + tt:slot + tt + 1],
                                             AUX1[0:1, :], axis=AX.X))

                for h in range(2):
                    @blk.tensor
                    def _(t, h=h, a_ev=a_ev, vfree=v_p.v):
                        t.wait_ge(a_p.sem, a_ev)
                        t.wait_ge(v_p.sem, vfree)
                        for i in range(4):
                            t_p.inc(t.transpose(
                                AUX2[:, i * 128:(i + 1) * 128],
                                EV[:, (4 + h) * TT + i * 128:(4 + h) * TT + (i + 1) * 128],
                                IDENT[:]))

                    tp_t = t_p.v

                    @blk.vector
                    def _(v, h=h, tp_t=tp_t, vdr_g=vdr_g):
                        v.wait_ge(t_p.sem, tp_t)
                        v.wait_ge(v_p.sem, v_p.v)
                        v.wait_ge(g_sp.sem, vdr_g)
                        hi = slice((2 * h) * TT, (2 * h + 1) * TT)
                        lo = slice((2 * h + 1) * TT, (2 * h + 2) * TT)
                        v_p.inc(v.tensor_copy(VTT[:, hi], AUX2[:]))
                        v.wait_ge(v_p.sem, v_p.v)
                        v_p.inc(v.tensor_tensor(out=VTT[:, lo], in0=AUX2[:],
                                                in1=VTT[:, hi], op=ALU.subtract))

                vtt_v = v_p.v

                @blk.gpsimd
                def _(g, tt=tt, vtt_v=vtt_v):
                    g.wait_ge(v_p.sem, vtt_v)
                    for hl in range(4):
                        h, lv = hl // 2, hl % 2
                        for i in range(4):
                            tk = tt * TT + i * 128
                            g_sp.inc(g.dma_start(
                                out=vdr[h, lv, tk:tk + 128, :],
                                in_=VTT[:, hl * TT + i * 128:hl * TT + (i + 1) * 128]), 16)

                vdr_g = g_sp.v
                pads_free_t = t_p.v
                pads_free_v = v_p.v
                ev_free = [(v_p.sem, v_p.v), (t_p.sem, t_p.v), (g_sp.sem, g_sp.v)]

        A_T, A_V, A_A, A_G = t_p.v, v_p.v, a_p.v, g_sp.v

        with contextlib.ExitStack() as eb0:
            BBPS = eb0.enter_context(nc.psum_tensor("BBPS", [128, TT], F32))

            @blk.vector
            def _(v):
                v.wait_ge(v_p.sem, A_V)
                v.wait_ge(g_cst.sem, CST)
                v_p.inc(v.memset(BBPAD[:], 0.0))
                for h in range(2):
                    v.wait_ge(v_p.sem, v_p.v)
                    v.wait_ge(a_p.sem, A_A + 2 * h)
                    v_p.inc(v.tensor_copy(BROW[0:1, 16:32], NRM[0:1, 32 + h * 16:48 + h * 16]))
                    for qt in range(1, NTT):
                        v.wait_ge(v_p.sem, v_p.v)
                        v_p.inc(v.tensor_tensor(out=BROW[0:1, 16 + qt:17 + qt],
                                                in0=BROW[0:1, 16 + qt:17 + qt],
                                                in1=BROW[0:1, 15 + qt:16 + qt], op=ALU.max))
                    v.wait_ge(v_p.sem, v_p.v)
                    v_p.inc(v.tensor_tensor(out=BROW[0:1, 32:48],
                                            in0=NRM[0:1, h * 16:(h + 1) * 16],
                                            in1=BROW[0:1, 16:32], op=ALU.mult))
                    v.wait_ge(a_p.sem, A_A + 2 * (h + 1))
                    v_p.inc(v.tensor_copy(BBPAD[0:1, h * 16:(h + 1) * 16],
                                          BROW[0:1, 32:48]))

            @blk.scalar
            def _(sc_):
                vv = A_V + 1
                for h in range(2):
                    vv += 17
                    sc_.wait_ge(v_p.sem, vv)
                    a_p.inc(sc_.activation(out=BROW[0:1, 32:48], in_=BROW[0:1, 32:48],
                                           func=AF.Sqrt))
                    sc_.wait_ge(a_p.sem, a_p.v)
                    a_p.inc(sc_.activation(out=BROW[0:1, 32:48], in_=BROW[0:1, 32:48],
                                           func=AF.Identity, scale=-SC))
                    vv += 1

            @blk.tensor
            def _(t, a_at=a_p.v):
                t.wait_ge(v_p.sem, v_p.v)
                t.wait_ge(a_p.sem, a_at)
                t_p.inc(t.matmul(BBPS[:, 0:32], ONES[:], BBPAD[:, 0:32],
                                 start=True, stop=True))

            @blk.vector
            def _(v):
                v.wait_ge(t_p.sem, t_p.v)
                v.wait_ge(v_p.sem, v_p.v)
                v_p.inc(v.tensor_copy(BB[:], BBPS[:, 0:32]))

        BB_V = v_p.v
        fence()

        with contextlib.ExitStack() as eb:
            KHH = eb.enter_context(nc.sbuf_tensor("KHH", [128, S], BF16))
            KHL = eb.enter_context(nc.sbuf_tensor("KHL", [128, S], BF16))
            VHH = eb.enter_context(nc.sbuf_tensor("VHH", [128, S], BF16))
            VHL = eb.enter_context(nc.sbuf_tensor("VHL", [128, S], BF16))
            QTH = eb.enter_context(nc.sbuf_tensor("QTH", [128, 2 * TT], BF16))
            QTL = eb.enter_context(nc.sbuf_tensor("QTL", [128, 2 * TT], BF16))
            P = eb.enter_context(nc.sbuf_tensor("P", [128, 2 * TT], F32))
            PH = eb.enter_context(nc.sbuf_tensor("PH", [128, 2 * TT], BF16))
            PL = eb.enter_context(nc.sbuf_tensor("PL", [128, 2 * TT], BF16))
            RBN = eb.enter_context(nc.sbuf_tensor("RBN", [128, TT], F32))
            AO = eb.enter_context(nc.sbuf_tensor("AO", [128, 2 * NTT * TT], F32))
            AOH = eb.enter_context(nc.sbuf_tensor("AOH", [128, 2 * TT], BF16))
            AOL = eb.enter_context(nc.sbuf_tensor("AOL", [128, 2 * TT], BF16))
            WO = eb.enter_context(nc.sbuf_tensor("WO", [128, 2 * NDCH * 128], F32))
            WOH = eb.enter_context(nc.sbuf_tensor("WOH", [128, 2 * NDCH * 128], BF16))
            WOL = eb.enter_context(nc.sbuf_tensor("WOL", [128, 2 * NDCH * 128], BF16))
            SP = eb.enter_context(nc.sbuf_tensor("SP", [128, 2 * TT], F32))
            STP = eb.enter_context(nc.psum_tensor("STP", [128, 2 * TT], F32))
            AVP = eb.enter_context(nc.psum_tensor("AVP", [128, TT], F32))
            SMP = eb.enter_context(nc.psum_tensor("SMP", [128, TT], F32))
            OPP = eb.enter_context(nc.psum_tensor("OPP", [128, 2 * TT], F32))

            @blk.sync
            def _(sy):
                src = woutT.rearrange("(dch p) (ech j) -> p dch ech j", p=128, j=128)
                s_wo.inc(sy.dma_start(out=WO.ap().rearrange(
                    "p (dch ech j) -> p dch ech j", dch=2, ech=NDCH), in_=src), 16)

            WO_LD = s_wo.v

            @blk.vector
            def _(v):
                v.wait_ge(s_wo.sem, WO_LD)
                v.wait_ge(v_p.sem, v_p.v)
                v_p.inc(v.tensor_copy(WOH[:], WO[:]))
                v.wait_ge(v_p.sem, v_p.v)
                v_p.inc(v.tensor_tensor(out=WOL[:], in0=WO[:], in1=WOH[:],
                                        op=ALU.subtract))

            WOHL_V = v_p.v
            ao_done_v = max(BB_V, WOHL_V)
            AB_A = a_p.v
            prev_pl_tail = []
            prev_ae_tail = []
            prev_tsum_tail = []
            qt_last_s = {}

            for h in range(2):
                @blk.sync
                def _(sy, h=h, t_all=t_p.v):
                    sy.wait_ge(g_sp.sem, A_G)
                    if h == 1:
                        sy.wait_ge(t_p.sem, t_all)
                    for lv, KHX in ((0, KHH), (1, KHL)):
                        for i in range(2):
                            s_kv.inc(sy.dma_start(
                                out=KHX[:, i * 4096:(i + 1) * 4096],
                                in_=kdr[h, lv, :, i * 4096:(i + 1) * 4096]), 16)
                    vsrc = vdr.rearrange("h lv (tb p) j -> h lv p tb j", p=128)
                    for lv, VHX in ((0, VHH), (1, VHL)):
                        for i in range(2):
                            s_kv.inc(sy.dma_start(
                                out=VHX.ap().rearrange("p (tb j) -> p tb j", j=128)[
                                    :, i * 32:(i + 1) * 32, :],
                                in_=vsrc[h, lv, :, i * 32:(i + 1) * 32, :]), 16)

                KV_LD = s_kv.v

                for qt in range(NTT):
                    nkt = (qt + 1) * 4
                    qb = qt % 2
                    gk = (h, qt)

                    qsem = s_qt0 if qb == 0 else s_qt1

                    @blk.sync
                    def _(sy, h=h, qt=qt, qb=qb, qsem=qsem):
                        key = (h, qt - 2)
                        if key in qt_last_s:
                            sy.wait_ge(t_p.sem, qt_last_s[key])
                        qsem.inc(sy.dma_start(out=QTH[:, qb * TT:(qb + 1) * TT],
                                              in_=qdr[h, 0, :, qt * TT:(qt + 1) * TT]), 16)
                        qsem.inc(sy.dma_start(out=QTL[:, qb * TT:(qb + 1) * TT],
                                              in_=qdr[h, 1, :, qt * TT:(qt + 1) * TT]), 16)

                    q_ld = qsem.v
                    ts_vals, ae_vals, vm_vals, tsum_vals, pl_vals = [], [], [], [], []

                    for kt in range(nkt):
                        pb = kt % 2

                        @blk.tensor
                        def _(t, kt=kt, pb=pb, qb=qb, q_ld=q_ld, KV_LD=KV_LD,
                              qsem=qsem, av0=ao_done_v, pae=list(prev_ae_tail)):
                            t.wait_ge(qsem.sem, q_ld)
                            t.wait_ge(s_kv.sem, KV_LD)
                            if kt >= 2:
                                t.wait_ge(a_p.sem, ae_vals[kt - 2])
                            elif len(pae) == 2:
                                t.wait_ge(a_p.sem, pae[kt])
                            else:
                                t.wait_ge(a_p.sem, AB_A)
                            if kt == 0:
                                t.wait_ge(v_p.sem, av0)
                            ks = slice(kt * 128, (kt + 1) * 128)
                            qs = slice(qb * TT, (qb + 1) * TT)
                            t_p.inc(t.matmul(STP[:, pb * TT:(pb + 1) * TT],
                                             KHH[:, ks], QTH[:, qs],
                                             start=True, stop=False))
                            t_p.inc(t.matmul(STP[:, pb * TT:(pb + 1) * TT],
                                             KHH[:, ks], QTL[:, qs],
                                             start=False, stop=False))
                            t_p.inc(t.matmul(STP[:, pb * TT:(pb + 1) * TT],
                                             KHL[:, ks], QTH[:, qs],
                                             start=False, stop=True))

                        ts_vals.append(t_p.v)
                        if kt == nkt - 1:
                            qt_last_s[gk] = t_p.v

                        @blk.scalar
                        def _(sc_, h=h, qt=qt, kt=kt, pb=pb, ppl=list(prev_pl_tail)):
                            sc_.wait_ge(t_p.sem, ts_vals[kt])
                            sc_.wait_ge(v_p.sem, BB_V)
                            if kt >= 2:
                                sc_.wait_ge(v_p.sem, pl_vals[kt - 2])
                            elif len(ppl) == 2:
                                sc_.wait_ge(v_p.sem, ppl[kt])
                            a_p.inc(sc_.activation(out=P[:, pb * TT:(pb + 1) * TT],
                                                   in_=STP[:, pb * TT:(pb + 1) * TT],
                                                   func=AF.Exp,
                                                   bias=BB[:, h * 16 + qt:h * 16 + qt + 1],
                                                   scale=SC))

                        ae_vals.append(a_p.v)

                        if kt >= nkt - 4:
                            off = kt - (nkt - 4)

                            @blk.vector
                            def _(v, kt=kt, pb=pb, off=off):
                                v.wait_ge(a_p.sem, ae_vals[kt])
                                v.wait_ge(v_p.sem, v_p.v)
                                v_p.inc(v.tensor_tensor(
                                    out=P[:, pb * TT:(pb + 1) * TT],
                                    in0=P[:, pb * TT:(pb + 1) * TT],
                                    in1=MASKS[:, off * TT:(off + 1) * TT], op=ALU.mult))

                            vm_vals.append(v_p.v)
                        else:
                            vm_vals.append(None)

                        @blk.vector
                        def _(v, kt=kt, pb=pb, ptt=list(prev_tsum_tail)):
                            if vm_vals[kt] is None:
                                v.wait_ge(a_p.sem, ae_vals[kt])
                            v.wait_ge(v_p.sem, v_p.v)
                            if kt >= 2:
                                v.wait_ge(t_p.sem, tsum_vals[kt - 2])
                            elif len(ptt) == 2:
                                v.wait_ge(t_p.sem, ptt[kt])
                            ps = slice(pb * TT, (pb + 1) * TT)
                            v_p.inc(v.tensor_copy(PH[:, ps], P[:, ps]))
                            v.wait_ge(v_p.sem, v_p.v)
                            v_p.inc(v.tensor_tensor(out=PL[:, ps], in0=P[:, ps],
                                                    in1=PH[:, ps], op=ALU.subtract))

                        pl_vals.append(v_p.v)

                        @blk.tensor
                        def _(t, kt=kt, pb=pb, nkt=nkt, av0=ao_done_v):
                            t.wait_ge(v_p.sem, pl_vals[kt])
                            if kt == 0:
                                t.wait_ge(v_p.sem, av0)
                            ks = slice(kt * 128, (kt + 1) * 128)
                            ps = slice(pb * TT, (pb + 1) * TT)
                            t_p.inc(t.matmul(AVP[:], VHH[:, ks], PH[:, ps],
                                             start=(kt == 0), stop=False))
                            t_p.inc(t.matmul(AVP[:], VHH[:, ks], PL[:, ps],
                                             start=False, stop=False))
                            t_p.inc(t.matmul(AVP[:], VHL[:, ks], PH[:, ps],
                                             start=False, stop=(kt == nkt - 1)))
                            t_p.inc(t.matmul(SMP[:], ONESB[:], PH[:, ps],
                                             start=(kt == 0), stop=False))
                            t_p.inc(t.matmul(SMP[:], ONESB[:], PL[:, ps],
                                             start=False, stop=(kt == nkt - 1)))

                        tsum_vals.append(t_p.v)

                    @blk.vector
                    def _(v, h=h, qt=qt, last_t=tsum_vals[-1]):
                        v.wait_ge(t_p.sem, last_t)
                        v.wait_ge(v_p.sem, v_p.v)
                        v_p.inc(v.reciprocal(RBN[:], SMP[:]))
                        v.wait_ge(v_p.sem, v_p.v)
                        v_p.inc(v.tensor_tensor(
                            out=AO[:, (h * NTT + qt) * TT:(h * NTT + qt + 1) * TT],
                            in0=AVP[:], in1=RBN[:], op=ALU.mult))

                    ao_done_v = v_p.v
                    prev_ae_tail = ae_vals[-2:]
                    prev_tsum_tail = tsum_vals[-2:]
                    prev_pl_tail = pl_vals[-2:]

            AO_DONE = v_p.v
            op_evac_a = {}
            sp_dma_g = {}
            ao_dec_v = {}
            qt_mm_t = {}

            for qt in range(NTT):
                @blk.vector
                def _(v, qt=qt, tprev=qt_mm_t.get(qt - 1, 0)):
                    v.wait_ge(v_p.sem, AO_DONE)
                    v.wait_ge(v_p.sem, v_p.v)
                    if tprev:
                        v.wait_ge(t_p.sem, tprev)
                    for dch in range(2):
                        ds = slice(dch * TT, (dch + 1) * TT)
                        aos = slice((dch * NTT + qt) * TT, (dch * NTT + qt + 1) * TT)
                        v_p.inc(v.tensor_copy(AOH[:, ds], AO[:, aos]))
                        v.wait_ge(v_p.sem, v_p.v)
                        v_p.inc(v.tensor_tensor(out=AOL[:, ds], in0=AO[:, aos],
                                                in1=AOH[:, ds], op=ALU.subtract))

                ao_dec_v[qt] = v_p.v

                for ech in range(NDCH):
                    n = qt * NDCH + ech
                    ob = n % 2

                    @blk.tensor
                    def _(t, qt=qt, ech=ech, ob=ob, n=n, a_at=a_p.v):
                        t.wait_ge(v_p.sem, ao_dec_v[qt])
                        if n >= 2:
                            t.wait_ge(a_p.sem, op_evac_a[n - 2])
                        else:
                            t.wait_ge(a_p.sem, a_at)
                        for dch in range(2):
                            ws = slice((dch * NDCH + ech) * 128, (dch * NDCH + ech + 1) * 128)
                            ds = slice(dch * TT, (dch + 1) * TT)
                            t_p.inc(t.matmul(OPP[:, ob * TT:(ob + 1) * TT],
                                             WOH[:, ws], AOH[:, ds],
                                             start=(dch == 0), stop=False))
                            t_p.inc(t.matmul(OPP[:, ob * TT:(ob + 1) * TT],
                                             WOH[:, ws], AOL[:, ds],
                                             start=False, stop=False))
                            t_p.inc(t.matmul(OPP[:, ob * TT:(ob + 1) * TT],
                                             WOL[:, ws], AOH[:, ds],
                                             start=False, stop=(dch == 1)))

                    mm_t = t_p.v
                    qt_mm_t[qt] = t_p.v

                    @blk.scalar
                    def _(sc_, mm_t=mm_t, n=n, ob=ob):
                        sc_.wait_ge(t_p.sem, mm_t)
                        if n >= 2:
                            sc_.wait_ge(sp_dma_g[n - 2][0].sem, sp_dma_g[n - 2][1])
                        a_p.inc(sc_.activation(out=SP[:, ob * TT:(ob + 1) * TT],
                                               in_=OPP[:, ob * TT:(ob + 1) * TT],
                                               func=AF.Identity))

                    op_evac_a[n] = a_p.v

                    @blk.gpsimd
                    def _(g, qt=qt, ech=ech, ob=ob, n=n):
                        g.wait_ge(a_p.sem, op_evac_a[n])
                        rank = qt if qt < 8 else 15 - qt
                        half = 0 if qt < 8 else 1
                        gsem = g_sp0 if ob == 0 else g_sp1
                        gsem.inc(g.dma_start(
                            out=spart[rank, ech * 128:(ech + 1) * 128,
                                      half * TT:(half + 1) * TT],
                            in_=SP[:, ob * TT:(ob + 1) * TT]), 16)

                    sp_dma_g[n] = (g_sp0 if ob == 0 else g_sp1, (g_sp0 if ob == 0 else g_sp1).v)

            fence()

            @blk.gpsimd
            def _(g, a0=g_sp0.v, b0=g_sp1.v):
                g.wait_ge(g_sp0.sem, a0)
                g.wait_ge(g_sp1.sem, b0)
                cc.inc(g.collective_compute(
                    "ReduceScatter", ALU.add, replica_groups=[list(range(NC))],
                    ins=[spart.opt()], outs=[sred.opt()]))

        G = es.enter_context(nc.sbuf_tensor("G", [128, NDCH * TOWN], BF16))
        with contextlib.ExitStack() as ec:
            XCHc = ec.enter_context(nc.sbuf_tensor("XCHc", [128, 2 * TT], F32))
            X2c = ec.enter_context(nc.sbuf_tensor("X2c", [128, 2 * TT], F32))
            T1c = ec.enter_context(nc.sbuf_tensor("T1c", [128, 2 * TT], F32))
            XN2c = ec.enter_context(nc.sbuf_tensor("XN2c", [128, 2 * TT], F32))
            RBSc = ec.enter_context(nc.sbuf_tensor("RBSc", [128, TT], F32))
            MRBSc = ec.enter_context(nc.sbuf_tensor("MRBSc", [128, TT], F32))
            RP1c = ec.enter_context(nc.sbuf_tensor("RP1c", [128, TT], F32))
            RP2c = ec.enter_context(nc.sbuf_tensor("RP2c", [128, TT], F32))
            XF8 = ec.enter_context(nc.sbuf_tensor("XF8", [128, NDCH * TOWN], F8))
            WST = ec.enter_context(nc.sbuf_tensor("WST", [128, 4 * 2048], F32))
            W8 = ec.enter_context(nc.sbuf_tensor("W8", [128, 4 * 2048], F8))
            GV = ec.enter_context(nc.sbuf_tensor("GV", [128, 2 * TT], BF16))
            F1 = ec.enter_context(nc.sbuf_tensor("F1", [128, 2 * TT], BF16))
            ACTB = ec.enter_context(nc.sbuf_tensor("ACTB", [128, 2 * TT], BF16))
            ACT8 = ec.enter_context(nc.sbuf_tensor("ACT8", [128, NHCH * TOWN], F8))

            with contextlib.ExitStack() as ec1:
                SUMSc = ec1.enter_context(nc.psum_tensor("SUMSc", [128, TT], F32))
                SQc = ec1.enter_context(nc.psum_tensor("SQc", [128, TT], F32))
                emit_ln_stats(xo, 2, 16, SUMSc, SQc, XCHc, X2c)

            CST_T, CST_V = t_p.v, v_p.v
            ROWS_C = g_rows.v
            AUX1c = ec.enter_context(nc.psum_tensor("AUX1c", [128, TT], F32))
            AUX2c = ec.enter_context(nc.psum_tensor("AUX2c", [128, TT], F32))
            FFP = ec.enter_context(nc.psum_tensor("FFP", [128, 4 * TT], F32))

            @blk.vector
            def _(v):
                v.wait_ge(v_p.sem, v_p.v)
                v_p.inc(v.memset(RP1c[:], 0.0))
                v_p.inc(v.memset(RP2c[:], 0.0))

            padc_t = 0
            padc_v = v_p.v
            firstc = [(v_p.sem, CST_V), (t_p.sem, CST_T)]
            prevC = None
            for tt in range(2):
                def f8_consume(dch, b, xn_v, tt=tt):
                    @blk.vector
                    def _(v, dch=dch, b=b, tt=tt):
                        v.wait_ge(v_p.sem, v_p.v)
                        v_p.inc(v.tensor_copy(
                            XF8[:, dch * TOWN + tt * TT:dch * TOWN + (tt + 1) * TT],
                            XN2c[:, b * TT:(b + 1) * TT]))
                    return (v_p.sem, v_p.v)

                prevC = emit_ln_tile(xo, tt, 16 + tt, ROWS_C, XCHc, T1c, XN2c,
                                     None, None, RBSc, MRBSc, RP1c, RP2c, AUX1c,
                                     AUX2c, padc_t, padc_v, firstc, prevC, f8_consume)
                firstc = []
                padc_t = t_p.v
                padc_v = v_p.v

            XF8_V = v_p.v

            st_ld, q8_v, ev_scl, act_v, ff_mm_t = {}, {}, {}, {}, {}
            for hch in range(NHCH):
                sb = hch % 2

                wsem = s_w0 if sb == 0 else s_w1

                @blk.sync
                def _(sy, hch=hch, sb=sb, wsem=wsem):
                    if hch >= 2:
                        sy.wait_ge(v_p.sem, q8_v[hch - 2])
                    for m, w in ((0, wg0), (1, wg1)):
                        src = w.rearrange("(dch p) (hch j) -> p hch dch j", p=128, j=128)
                        wsem.inc(sy.dma_start(
                            out=WST.ap().rearrange("p (s dch j) -> p s dch j", s=4, j=128)[
                                :, 2 * sb + m, :, :],
                            in_=src[:, hch, :, :]), 16)

                st_ld[hch] = wsem.v

                @blk.vector
                def _(v, hch=hch, sb=sb, wsem=wsem):
                    v.wait_ge(wsem.sem, st_ld[hch])
                    v.wait_ge(v_p.sem, v_p.v)
                    if hch >= 2:
                        v.wait_ge(t_p.sem, ff_mm_t[hch - 2])
                    v_p.inc(v.tensor_copy(W8[:, (2 * sb) * 2048:(2 * sb + 2) * 2048],
                                          WST[:, (2 * sb) * 2048:(2 * sb + 2) * 2048]))

                q8_v[hch] = v_p.v

                @blk.tensor
                def _(t, hch=hch, sb=sb, a_at=a_p.v):
                    t.wait_ge(v_p.sem, max(q8_v[hch], XF8_V))
                    if hch >= 1:
                        t.wait_ge(a_p.sem, ev_scl[hch - 1])
                    else:
                        t.wait_ge(a_p.sem, a_at)
                    for m in range(2):
                        for th in range(2):
                            for dch in range(NDCH):
                                t_p.inc(t.matmul(
                                    FFP[:, (2 * m + th) * TT:(2 * m + th + 1) * TT],
                                    W8[:, (2 * sb + m) * 2048 + dch * 128:
                                       (2 * sb + m) * 2048 + (dch + 1) * 128],
                                    XF8[:, dch * TOWN + th * TT:dch * TOWN + (th + 1) * TT],
                                    start=(dch == 0), stop=(dch == NDCH - 1)))

                ff_mm_t[hch] = t_p.v

                @blk.scalar
                def _(sc_, hch=hch):
                    sc_.wait_ge(t_p.sem, ff_mm_t[hch])
                    if hch >= 1:
                        sc_.wait_ge(v_p.sem, act_v[hch - 1])
                    for th in range(2):
                        a_p.inc(sc_.activation(out=GV[:, th * TT:(th + 1) * TT],
                                               in_=FFP[:, th * TT:(th + 1) * TT],
                                               func=AF.Gelu_apprx_tanh))

                ev_scl[hch] = a_p.v

                @blk.vector
                def _(v, hch=hch):
                    v.wait_ge(a_p.sem, ev_scl[hch])
                    v.wait_ge(t_p.sem, ff_mm_t[hch])
                    for th in range(2):
                        v.wait_ge(v_p.sem, v_p.v)
                        v_p.inc(v.tensor_copy(F1[:, th * TT:(th + 1) * TT],
                                              FFP[:, (2 + th) * TT:(3 + th) * TT]))
                        v.wait_ge(v_p.sem, v_p.v)
                        v_p.inc(v.tensor_tensor(out=ACTB[:, th * TT:(th + 1) * TT],
                                                in0=GV[:, th * TT:(th + 1) * TT],
                                                in1=F1[:, th * TT:(th + 1) * TT],
                                                op=ALU.mult))
                        v.wait_ge(v_p.sem, v_p.v)
                        v_p.inc(v.tensor_copy(
                            ACT8[:, hch * TOWN + th * TT:hch * TOWN + (th + 1) * TT],
                            ACTB[:, th * TT:(th + 1) * TT]))

                act_v[hch] = v_p.v

            ACT_V = v_p.v
            FF_LAST_T = ff_mm_t[NHCH - 1]
            FF_LAST_Q = q8_v[NHCH - 1]
            FF_LAST_E = ev_scl[NHCH - 1]

            wl_ld, wl_q8, wl_mm, g_evac = {}, {}, {}, {}
            for ech in range(NDCH):
                for grp in range(4):
                    n = ech * 4 + grp
                    sb = n % 2

                    wsem = s_w0 if sb == 0 else s_w1

                    @blk.sync
                    def _(sy, ech=ech, grp=grp, sb=sb, n=n, wsem=wsem):
                        if n >= 2:
                            sy.wait_ge(v_p.sem, wl_q8[n - 2])
                        else:
                            sy.wait_ge(v_p.sem, FF_LAST_Q)
                        src = wl.rearrange("(hch p) (ech j) -> p hch ech j", p=128, j=128)
                        wsem.inc(sy.dma_start(
                            out=WST.ap().rearrange("p (s dch j) -> p s dch j", s=4, j=128)[
                                :, 2 * sb, :, :],
                            in_=src[:, grp * 16:(grp + 1) * 16, ech, :]), 16)

                    wl_ld[n] = wsem.v

                    @blk.vector
                    def _(v, n=n, sb=sb, wsem=wsem):
                        v.wait_ge(wsem.sem, wl_ld[n])
                        v.wait_ge(v_p.sem, v_p.v)
                        if n >= 2:
                            v.wait_ge(t_p.sem, wl_mm[n - 2])
                        else:
                            v.wait_ge(t_p.sem, FF_LAST_T)
                        v_p.inc(v.tensor_copy(W8[:, (2 * sb) * 2048:(2 * sb) * 2048 + 2048],
                                              WST[:, (2 * sb) * 2048:(2 * sb) * 2048 + 2048]))

                    wl_q8[n] = v_p.v

                    @blk.tensor
                    def _(t, ech=ech, grp=grp, n=n, sb=sb):
                        t.wait_ge(v_p.sem, max(wl_q8[n], ACT_V))
                        if grp == 0:
                            if ech >= 1:
                                t.wait_ge(a_p.sem, g_evac[ech - 1])
                            else:
                                t.wait_ge(a_p.sem, FF_LAST_E)
                        for i in range(16):
                            hch = grp * 16 + i
                            for th in range(2):
                                t_p.inc(t.matmul(
                                    FFP[:, th * TT:(th + 1) * TT],
                                    W8[:, (2 * sb) * 2048 + i * 128:(2 * sb) * 2048 + (i + 1) * 128],
                                    ACT8[:, hch * TOWN + th * TT:hch * TOWN + (th + 1) * TT],
                                    start=(hch == 0), stop=(hch == NHCH - 1)))

                    wl_mm[n] = t_p.v

                @blk.scalar
                def _(sc_, ech=ech):
                    sc_.wait_ge(t_p.sem, wl_mm[ech * 4 + 3])
                    for th in range(2):
                        a_p.inc(sc_.activation(
                            out=G[:, ech * TOWN + th * TT:ech * TOWN + (th + 1) * TT],
                            in_=FFP[:, th * TT:(th + 1) * TT], func=AF.Silu))

                g_evac[ech] = a_p.v

        G_A = a_p.v
        fence()

        with contextlib.ExitStack() as ed:
            SR = ed.enter_context(nc.sbuf_tensor("SR", [128, 2 * TT], F32))
            XOT = ed.enter_context(nc.sbuf_tensor("XOT", [128, 2 * TT], F32))
            SBT = ed.enter_context(nc.sbuf_tensor("SBT", [128, 2 * TT], F32))
            OT = ed.enter_context(nc.sbuf_tensor("OT", [128, 2 * TT], F32))

            d_ld, d_sc, d_ot, d_st = {}, {}, {}, {}
            for th in range(2):
                for ech in range(NDCH):
                    n = th * NDCH + ech
                    b = n % 2

                    dsem = s_d0 if b == 0 else s_d1

                    @blk.sync
                    def _(sy, th=th, ech=ech, b=b, n=n, dsem=dsem):
                        sy.wait_ge(cc.sem, 1)
                        if n >= 2:
                            sy.wait_ge(a_p.sem, d_sc[n - 2])
                            sy.wait_ge(v_p.sem, d_ot[n - 2])
                        dsem.inc(sy.dma_start(
                            out=SR[:, b * TT:(b + 1) * TT],
                            in_=sred[ech * 128:(ech + 1) * 128, th * TT:(th + 1) * TT]), 16)
                        dsem.inc(sy.dma_start(
                            out=XOT[:, b * TT:(b + 1) * TT],
                            in_=xo[ech * 128:(ech + 1) * 128, th * TT:(th + 1) * TT]), 16)

                    d_ld[n] = (dsem, dsem.v)

                    @blk.scalar
                    def _(sc_, ech=ech, b=b, n=n):
                        sc_.wait_ge(d_ld[n][0].sem, d_ld[n][1])
                        sc_.wait_ge(a_p.sem, G_A)
                        if n >= 2:
                            sc_.wait_ge(v_p.sem, d_ot[n - 2])
                        a_p.inc(sc_.activation(out=SBT[:, b * TT:(b + 1) * TT],
                                               in_=SR[:, b * TT:(b + 1) * TT],
                                               func=AF.Identity, bias=BOUT[:, ech:ech + 1]))

                    d_sc[n] = a_p.v

                    @blk.vector
                    def _(v, th=th, ech=ech, b=b, n=n):
                        v.wait_ge(a_p.sem, d_sc[n])
                        v.wait_ge(v_p.sem, v_p.v)
                        if n >= 2:
                            gsem, gval = d_st[n - 2]
                            v.wait_ge(gsem.sem, gval)
                        v_p.inc(v.tensor_tensor(
                            out=SBT[:, b * TT:(b + 1) * TT],
                            in0=SBT[:, b * TT:(b + 1) * TT],
                            in1=G[:, ech * TOWN + th * TT:ech * TOWN + (th + 1) * TT],
                            op=ALU.mult))
                        v.wait_ge(v_p.sem, v_p.v)
                        v.wait_ge(d_ld[n][0].sem, d_ld[n][1])
                        v_p.inc(v.tensor_tensor(out=OT[:, b * TT:(b + 1) * TT],
                                                in0=SBT[:, b * TT:(b + 1) * TT],
                                                in1=XOT[:, b * TT:(b + 1) * TT],
                                                op=ALU.add))

                    d_ot[n] = v_p.v

                    gosem = g_o0 if b == 0 else g_o1

                    @blk.gpsimd
                    def _(g, th=th, ech=ech, b=b, n=n, gosem=gosem):
                        g.wait_ge(v_p.sem, d_ot[n])
                        gosem.inc(g.dma_start(
                            out=outp[ech * 128:(ech + 1) * 128, th * TT:(th + 1) * TT],
                            in_=OT[:, b * TT:(b + 1) * TT]), 16)

                    d_st[n] = (gosem, gosem.v)

            @blk.gpsimd
            def _(g, a0=g_o0.v, b0=g_o1.v):
                g.wait_ge(g_o0.sem, a0)
                g.wait_ge(g_o1.sem, b0)

    return nc


_NC_CACHE = [None]


def kernel(L0_attn_x, L0_attn_w_qkv, L0_attn_b_qkv, L0_attn_w_out, L0_attn_b_out,
           L0_ffn_w_gating, L0_ffn_w_linear):
    x = np.asarray(L0_attn_x)[0]
    w_qkv = np.asarray(L0_attn_w_qkv)
    b_qkv = np.asarray(L0_attn_b_qkv)
    w_out = np.asarray(L0_attn_w_out)
    b_out = np.asarray(L0_attn_b_out)
    w_g = np.asarray(L0_ffn_w_gating)
    w_l = np.asarray(L0_ffn_w_linear)

    xf = np.ascontiguousarray(x.T)
    woutT_full = np.ascontiguousarray(w_out.T)
    wg0 = np.ascontiguousarray(w_g[0])
    wg1 = np.ascontiguousarray(w_g[1])
    wl = np.ascontiguousarray(w_l)

    in_maps = []
    for c in range(NC):
        b0, b1 = c, 15 - c
        own = np.concatenate([x[b0 * TT:(b0 + 1) * TT], x[b1 * TT:(b1 + 1) * TT]], axis=0)
        xo_c = np.ascontiguousarray(own.T)
        h0 = 2 * c
        rows = np.concatenate([
            w_qkv[h0 * 128:(h0 + 2) * 128],
            w_qkv[2048 + h0 * 128:2048 + (h0 + 2) * 128],
            w_qkv[4096 + h0 * 128:4096 + (h0 + 2) * 128]], axis=0)
        wqkvT_c = np.ascontiguousarray(rows.T)
        bq_c = np.ascontiguousarray(np.concatenate([
            b_qkv[h0 * 128:(h0 + 2) * 128],
            b_qkv[2048 + h0 * 128:2048 + (h0 + 2) * 128],
            b_qkv[4096 + h0 * 128:4096 + (h0 + 2) * 128]], axis=0))
        woutT_c = np.ascontiguousarray(woutT_full[h0 * 128:(h0 + 2) * 128, :])
        in_maps.append({
            "xf": xf, "xo": xo_c, "wqkvT": wqkvT_c, "bqkv": bq_c,
            "woutT": woutT_c, "bout": b_out, "wg0": wg0, "wg1": wg1, "wl": wl,
        })

    if _NC_CACHE[0] is None:
        _NC_CACHE[0] = build_nc()
    res = run_bass_kernel_spmd(_NC_CACHE[0], in_maps, core_ids=list(range(NC)))

    out = np.empty((1, S, D), dtype=np.float32)
    for c in range(NC):
        o = np.asarray(res.results[c]["outp"]).T
        out[0, c * TT:(c + 1) * TT] = o[:TT]
        out[0, (15 - c) * TT:(16 - c) * TT] = o[TT:]
    return out


# version 06
# speedup vs baseline: 1.5519x; 1.5519x over previous
import contextlib
import numpy as np
import concourse.bass as bass
import concourse.mybir as mybir
from concourse.bass_utils import run_bass_kernel_spmd

F32 = mybir.dt.float32
BF16 = mybir.dt.bfloat16
F8 = mybir.dt.float8e4
AF = mybir.ActivationFunctionType
ALU = mybir.AluOpType
AX = mybir.AxisListType

D = 2048
S = 8192
NC = 8
TT = 512
NTT = S // TT
TOWN = 1024
HID = 8192
NDCH = D // 128
NHCH = HID // 128
SC = float(1.0 / np.sqrt(128.0))


class Ctr:
    def __init__(self, es, nc, name):
        self.sem = es.enter_context(nc.semaphore(name))
        self.v = 0

    def inc(self, instr, n=1):
        instr.then_inc(self.sem, n)
        self.v += n
        return self.v


def build_nc():
    nc = bass.Bass("TRN2", target_bir_lowering=False, debug=False)

    xf = nc.dram_tensor("xf", [D, S], F32, kind="ExternalInput").ap()
    xo = nc.dram_tensor("xo", [D, TOWN], F32, kind="ExternalInput").ap()
    wqkvT = nc.dram_tensor("wqkvT", [D, 768], F32, kind="ExternalInput").ap()
    bqkv = nc.dram_tensor("bqkv", [768], F32, kind="ExternalInput").ap()
    woutT = nc.dram_tensor("woutT", [256, D], F32, kind="ExternalInput").ap()
    bout = nc.dram_tensor("bout", [D], F32, kind="ExternalInput").ap()
    wg0 = nc.dram_tensor("wg0", [D, HID], F32, kind="ExternalInput").ap()
    wg1 = nc.dram_tensor("wg1", [D, HID], F32, kind="ExternalInput").ap()
    wl = nc.dram_tensor("wl", [HID, D], F32, kind="ExternalInput").ap()
    outp = nc.dram_tensor("outp", [D, TOWN], F32, kind="ExternalOutput").ap()

    qdr = nc.dram_tensor("qdr", [2, 2, 128, S], BF16, kind="Internal").ap()
    kdr = nc.dram_tensor("kdr", [2, 2, 128, S], BF16, kind="Internal").ap()
    vdr = nc.dram_tensor("vdr", [2, 2, S, 128], BF16, kind="Internal").ap()
    spart = nc.dram_tensor("spart", [NC, D, TOWN], F32, kind="Internal").ap()
    rows_dr = nc.dram_tensor("rows_dr", [2, 18, TT], F32, kind="Internal").ap()
    sred = nc.dram_tensor("sred", [D, TOWN], F32, kind="Internal").ap()

    mask_np = np.zeros((128, 4 * TT), dtype=np.float32)
    for off in range(4):
        mask_np[:, off * TT:(off + 1) * TT] = (
            np.arange(128)[:, None] + 128 * off <= np.arange(TT)[None, :])
    masks_dr = nc.inline_tensor(mask_np, name="masks_c")
    ones_dr = nc.inline_tensor(np.ones((128, 128), dtype=np.float32), name="ones_c")
    ident_dr = nc.inline_tensor(np.eye(128, dtype=np.float32), name="ident_c")
    import ml_dtypes as _mld
    onesb_dr = nc.inline_tensor(np.ones((128, 128), dtype=_mld.bfloat16), name="onesb_c")
    identb_dr = nc.inline_tensor(np.eye(128).astype(_mld.bfloat16), name="identb_c")
    eps_dr = nc.inline_tensor(np.full((128, 1), 1e-5, dtype=np.float32), name="eps_c")

    es = contextlib.ExitStack()
    with es:
        MASKS = es.enter_context(nc.sbuf_tensor("MASKS", [128, 4 * TT], F32))
        ONES = es.enter_context(nc.sbuf_tensor("ONES", [128, 128], F32))
        ONESB = es.enter_context(nc.sbuf_tensor("ONESB", [128, 128], BF16))
        IDENTB = es.enter_context(nc.sbuf_tensor("IDENTB", [128, 128], BF16))
        IDENT = es.enter_context(nc.sbuf_tensor("IDENT", [128, 128], F32))
        EPS = es.enter_context(nc.sbuf_tensor("EPS", [128, 1], F32))
        BQKV = es.enter_context(nc.sbuf_tensor("BQKV", [128, 6], F32))
        BOUT = es.enter_context(nc.sbuf_tensor("BOUT", [128, 16], F32))
        NRM = es.enter_context(nc.sbuf_tensor("NRM", [1, 64], F32))
        TMPR = es.enter_context(nc.sbuf_tensor("TMPR", [1, TT], F32))
        TMPM = es.enter_context(nc.sbuf_tensor("TMPM", [1, TT], F32))
        BB = es.enter_context(nc.sbuf_tensor("BB", [128, 32], F32))
        TMP = es.enter_context(nc.sbuf_tensor("TMP", [1, 2 * TT], F32))
        BROW = es.enter_context(nc.sbuf_tensor("BROW", [1, 64], F32))
        BBPAD = es.enter_context(nc.sbuf_tensor("BBPAD", [128, 32], F32))

        s_ld = Ctr(es, nc, "s_ld")
        s_x0 = Ctr(es, nc, "s_x0")
        s_x1 = Ctr(es, nc, "s_x1")
        s_pd = Ctr(es, nc, "s_pd")
        s_wq = Ctr(es, nc, "s_wq")
        s_kv = Ctr(es, nc, "s_kv")
        s_qt0 = Ctr(es, nc, "s_qt0")
        s_qt1 = Ctr(es, nc, "s_qt1")
        s_wo = Ctr(es, nc, "s_wo")
        s_w0 = Ctr(es, nc, "s_w0")
        s_w1 = Ctr(es, nc, "s_w1")
        s_d0 = Ctr(es, nc, "s_d0")
        s_d1 = Ctr(es, nc, "s_d1")
        g_sp0 = Ctr(es, nc, "g_sp0")
        g_sp1 = Ctr(es, nc, "g_sp1")
        g_o0 = Ctr(es, nc, "g_o0")
        g_o1 = Ctr(es, nc, "g_o1")
        g_cst = Ctr(es, nc, "g_cst")
        g_sp = Ctr(es, nc, "g_sp")
        g_rows = Ctr(es, nc, "g_rows")
        g_out = Ctr(es, nc, "g_out")
        cc = Ctr(es, nc, "cc")
        t_p = Ctr(es, nc, "t_p")
        v_p = Ctr(es, nc, "v_p")
        a_p = Ctr(es, nc, "a_p")

        blk = es.enter_context(nc.Block())

        @blk.gpsimd
        def _(g):
            g_cst.inc(g.dma_start(out=MASKS[:], in_=masks_dr.ap()), 16)
            g_cst.inc(g.dma_start(out=ONES[:], in_=ones_dr.ap()), 16)
            g_cst.inc(g.dma_start(out=ONESB[:], in_=onesb_dr.ap()), 16)
            g_cst.inc(g.dma_start(out=IDENTB[:], in_=identb_dr.ap()), 16)
            g_cst.inc(g.dma_start(out=IDENT[:], in_=ident_dr.ap()), 16)
            g_cst.inc(g.dma_start(out=EPS[:], in_=eps_dr.ap()), 16)
            g_cst.inc(g.dma_start(out=BQKV[:], in_=bqkv.rearrange("(c p) -> p c", p=128), allow_slow_non_contiguous=True), 16)
            g_cst.inc(g.dma_start(out=BOUT[:], in_=bout.rearrange("(c p) -> p c", p=128), allow_slow_non_contiguous=True), 16)

        CST = g_cst.v

        def fence():
            snaps = [(c.sem, c.v) for c in
                     (t_p, v_p, a_p, g_sp, g_sp0, g_sp1, g_rows, g_out, g_cst,
                      s_x0, s_x1, s_pd, s_wq, s_kv, s_qt0, s_qt1, s_wo, s_w0,
                      s_w1, s_d0, s_d1)]

            def emit(eng):
                for sem, val in snaps:
                    if val:
                        eng.wait_ge(sem, val)

            @blk.tensor
            def _(t):
                emit(t)

            @blk.vector
            def _(v):
                emit(v)

            @blk.scalar
            def _(s):
                emit(s)

            @blk.sync
            def _(sy):
                emit(sy)

            @blk.gpsimd
            def _(g):
                emit(g)

        def emit_ln_stats(src, ntt, row_base, SUMS, SQ, XCH, X2):
            t0, v0, a0, gr0 = t_p.v, v_p.v, a_p.v, g_rows.v
            slots = [s_x0, s_x1]
            ld_vals = []

            def x2_idx(n):
                return 20 * (n // NDCH) + (n % NDCH) + 1

            @blk.sync
            def _(sy):
                for tt in range(ntt):
                    for dch in range(NDCH):
                        n = tt * NDCH + dch
                        b = n % 2
                        if n >= 2:
                            sy.wait_ge(t_p.sem, t0 + 2 * (n - 1))
                            sy.wait_ge(v_p.sem, v0 + x2_idx(n - 2))
                        ld_vals.append(slots[b].inc(sy.dma_start(
                            out=XCH[:, b * TT:(b + 1) * TT],
                            in_=src[dch * 128:(dch + 1) * 128, tt * TT:(tt + 1) * TT]), 16))

            @blk.vector
            def _(v):
                for tt in range(ntt):
                    for dch in range(NDCH):
                        n = tt * NDCH + dch
                        b = n % 2
                        v.wait_ge(slots[b].sem, ld_vals[n])
                        if n >= 2:
                            v.wait_ge(t_p.sem, t0 + 2 * (n - 1))
                        v_p.inc(v.tensor_tensor(
                            out=X2[:, b * TT:(b + 1) * TT],
                            in0=XCH[:, b * TT:(b + 1) * TT],
                            in1=XCH[:, b * TT:(b + 1) * TT], op=ALU.mult))
                    v.wait_ge(a_p.sem, a0 + 2 * tt + 1)
                    v.wait_ge(v_p.sem, v0 + 20 * tt + 16)
                    v_p.inc(v.tensor_tensor(out=TMP[0:1, TT:2 * TT], in0=TMP[0:1, 0:TT],
                                            in1=TMP[0:1, 0:TT], op=ALU.mult))
                    v.wait_ge(v_p.sem, v0 + 20 * tt + 17)
                    v.wait_ge(t_p.sem, t0 + 2 * NDCH * (tt + 1))
                    v_p.inc(v.scalar_tensor_tensor(
                        out=TMP[0:1, TT:2 * TT], in0=SQ[0:1, :], scalar=1.0 / D,
                        in1=TMP[0:1, TT:2 * TT], op0=ALU.mult, op1=ALU.subtract))
                    v.wait_ge(a_p.sem, a0 + 2 * tt + 2)
                    v.wait_ge(v_p.sem, v0 + 20 * tt + 18)
                    if tt > 0:
                        v.wait_ge(g_rows.sem, gr0 + 32 * tt)
                    v_p.inc(v.reciprocal(TMPR[0:1, :], TMP[0:1, TT:2 * TT]))
                    v.wait_ge(v_p.sem, v0 + 20 * tt + 19)
                    v_p.inc(v.tensor_tensor(out=TMPM[0:1, :], in0=TMP[0:1, 0:TT],
                                            in1=TMPR[0:1, :], op=ALU.mult))

            @blk.gpsimd
            def _(g):
                for tt in range(ntt):
                    g.wait_ge(v_p.sem, v0 + 20 * tt + 20)
                    g_rows.inc(g.dma_start(out=rows_dr[0, row_base + tt, :],
                                           in_=TMPR[0:1, :]), 16)
                    g_rows.inc(g.dma_start(out=rows_dr[1, row_base + tt, :],
                                           in_=TMPM[0:1, :]), 16)

            @blk.tensor
            def _(t):
                t.wait_ge(g_cst.sem, CST)
                t.wait_ge(v_p.sem, v0)
                t.wait_ge(a_p.sem, a0)
                for tt in range(ntt):
                    if tt > 0:
                        t.wait_ge(a_p.sem, a0 + 2 * tt - 1)
                        t.wait_ge(v_p.sem, v0 + 20 * tt - 2)
                    for dch in range(NDCH):
                        n = tt * NDCH + dch
                        b = n % 2
                        t.wait_ge(slots[b].sem, ld_vals[n])
                        t_p.inc(t.matmul(SUMS[:], ONES[:], XCH[:, b * TT:(b + 1) * TT],
                                         start=(dch == 0), stop=(dch == NDCH - 1)))
                        t.wait_ge(v_p.sem, v0 + x2_idx(n))
                        t_p.inc(t.matmul(SQ[:], ONES[:], X2[:, b * TT:(b + 1) * TT],
                                         start=(dch == 0), stop=(dch == NDCH - 1)))

            @blk.scalar
            def _(sc_):
                sc_.wait_ge(g_cst.sem, CST)
                for tt in range(ntt):
                    sc_.wait_ge(t_p.sem, t0 + 2 * NDCH * (tt + 1))
                    sc_.wait_ge(v_p.sem, v0 + 20 * tt)
                    a_p.inc(sc_.activation(out=TMP[0:1, 0:TT], in_=SUMS[0:1, :],
                                           func=AF.Identity, scale=1.0 / D))
                    sc_.wait_ge(v_p.sem, v0 + 20 * tt + 18)
                    a_p.inc(sc_.activation(out=TMP[0:1, TT:2 * TT], in_=TMP[0:1, TT:2 * TT],
                                           func=AF.Sqrt, bias=EPS[0:1, 0:1]))

        def emit_ln_tile(src, tt, row_slot, rows_g, XCH, T1, XN2, XNH, XNL, RBS, MRBS,
                         RP1, RP2, AUX1, AUX2,
                         pads_free_t, pads_free_v, first_waits, prev, xn_consumer):
            @blk.sync
            def _(sy):
                sy.wait_ge(g_rows.sem, rows_g)
                sy.wait_ge(v_p.sem, pads_free_v)
                if pads_free_t:
                    sy.wait_ge(t_p.sem, pads_free_t)
                s_pd.inc(sy.dma_start(out=RP1[0:1, :], in_=rows_dr[0, row_slot, :]), 16)
                s_pd.inc(sy.dma_start(out=RP2[0:1, :], in_=rows_dr[1, row_slot, :]), 16)

            pad_ld = s_pd.v
            a_at = a_p.v

            @blk.tensor
            def _(t):
                t.wait_ge(s_pd.sem, pad_ld)
                t.wait_ge(v_p.sem, pads_free_v)
                t.wait_ge(a_p.sem, a_at)
                t_p.inc(t.matmul(AUX1[:], ONES[:], RP1[:], start=True, stop=True))
                t_p.inc(t.matmul(AUX2[:], ONES[:], RP2[:], start=True, stop=True))

            t_bc = t_p.v

            @blk.vector
            def _(v):
                v.wait_ge(t_p.sem, t_bc)
                v.wait_ge(v_p.sem, v_p.v)
                v_p.inc(v.tensor_copy(RBS[:], AUX1[:]))
                v_p.inc(v.tensor_copy(MRBS[:], AUX2[:]))

            v_bc = v_p.v
            xn_done = []
            free_marks = []
            ld_vals = []
            slots = [s_x0, s_x1]

            for dch in range(NDCH):
                b = dch % 2

                @blk.sync
                def _(sy, dch=dch, b=b):
                    if dch >= 2:
                        sy.wait_ge(v_p.sem, xn_done[dch - 2])
                    else:
                        for sem, val in first_waits:
                            sy.wait_ge(sem, val)
                        if prev is not None:
                            sy.wait_ge(v_p.sem, prev[0][NDCH - 2 + dch])
                    ld_vals.append(slots[b].inc(sy.dma_start(
                        out=XCH[:, b * TT:(b + 1) * TT],
                        in_=src[dch * 128:(dch + 1) * 128, tt * TT:(tt + 1) * TT]), 16))

                @blk.vector
                def _(v, dch=dch, b=b):
                    v.wait_ge(slots[b].sem, ld_vals[dch])
                    v.wait_ge(v_p.sem, v_bc)
                    if dch >= 2:
                        sem, val = free_marks[dch - 2]
                        v.wait_ge(sem, val)
                        v.wait_ge(v_p.sem, xn_done[dch - 2])
                    elif prev is not None:
                        sem, val = prev[1][NDCH - 2 + dch]
                        v.wait_ge(sem, val)
                        v.wait_ge(v_p.sem, prev[0][NDCH - 2 + dch])
                    v_p.inc(v.tensor_tensor(out=T1[:, b * TT:(b + 1) * TT],
                                            in0=XCH[:, b * TT:(b + 1) * TT],
                                            in1=RBS[:], op=ALU.mult))
                    v.wait_ge(v_p.sem, v_p.v)
                    v_p.inc(v.tensor_tensor(out=XN2[:, b * TT:(b + 1) * TT],
                                            in0=T1[:, b * TT:(b + 1) * TT],
                                            in1=MRBS[:], op=ALU.subtract))
                    if XNH is not None:
                        v.wait_ge(v_p.sem, v_p.v)
                        v_p.inc(v.tensor_copy(XNH[:, b * TT:(b + 1) * TT],
                                              XN2[:, b * TT:(b + 1) * TT]))
                        v.wait_ge(v_p.sem, v_p.v)
                        v_p.inc(v.tensor_tensor(out=XNL[:, b * TT:(b + 1) * TT],
                                                in0=XN2[:, b * TT:(b + 1) * TT],
                                                in1=XNH[:, b * TT:(b + 1) * TT],
                                                op=ALU.subtract))

                xn_done.append(v_p.v)
                free_marks.append(xn_consumer(dch, b, v_p.v))

            return xn_done, free_marks

        with contextlib.ExitStack() as ea:
            WQ = ea.enter_context(nc.sbuf_tensor("WQ", [128, 6 * NDCH * 128], F32))
            WQH = ea.enter_context(nc.sbuf_tensor("WQH", [128, 6 * NDCH * 128], BF16))
            WQL = ea.enter_context(nc.sbuf_tensor("WQL", [128, 6 * NDCH * 128], BF16))
            XCH = ea.enter_context(nc.sbuf_tensor("XCH", [128, 2 * TT], F32))
            X2 = ea.enter_context(nc.sbuf_tensor("X2", [128, 2 * TT], F32))
            T1 = ea.enter_context(nc.sbuf_tensor("T1", [128, 2 * TT], F32))
            XN2 = ea.enter_context(nc.sbuf_tensor("XN2", [128, 2 * TT], F32))
            XNH = ea.enter_context(nc.sbuf_tensor("XNH", [128, 2 * TT], BF16))
            XNL = ea.enter_context(nc.sbuf_tensor("XNL", [128, 2 * TT], BF16))
            RBS = ea.enter_context(nc.sbuf_tensor("RBS", [128, TT], F32))
            MRBS = ea.enter_context(nc.sbuf_tensor("MRBS", [128, TT], F32))
            RP1 = ea.enter_context(nc.sbuf_tensor("RP1", [128, TT], F32))
            RP2 = ea.enter_context(nc.sbuf_tensor("RP2", [128, TT], F32))
            EV = ea.enter_context(nc.sbuf_tensor("EV", [128, 6 * TT], F32))
            EVH = ea.enter_context(nc.sbuf_tensor("EVH", [128, 6 * TT], BF16))
            EVL = ea.enter_context(nc.sbuf_tensor("EVL", [128, 6 * TT], BF16))
            VTT = ea.enter_context(nc.sbuf_tensor("VTT", [128, 4 * TT], BF16))
            KSQ = ea.enter_context(nc.sbuf_tensor("KSQ", [128, TT], F32))

            @blk.sync
            def _(sy):
                src = wqkvT.rearrange("(dch p) (ech j) -> p ech dch j", p=128, j=128)
                s_wq.inc(sy.dma_start(out=WQ.ap().rearrange(
                    "p (ech dch j) -> p ech dch j", ech=6, dch=NDCH), in_=src), 16)

            WQ_LD = s_wq.v

            @blk.vector
            def _(v):
                v.wait_ge(s_wq.sem, WQ_LD)
                v.wait_ge(v_p.sem, v_p.v)
                v_p.inc(v.tensor_copy(WQH[:], WQ[:]))
                v.wait_ge(v_p.sem, v_p.v)
                v_p.inc(v.tensor_tensor(out=WQL[:], in0=WQ[:], in1=WQH[:],
                                        op=ALU.subtract))

            WQHL_V = v_p.v

            with contextlib.ExitStack() as ea1:
                SUMS = ea1.enter_context(nc.psum_tensor("SUMS", [128, TT], F32))
                SQ = ea1.enter_context(nc.psum_tensor("SQ", [128, TT], F32))
                emit_ln_stats(xf, NTT, 0, SUMS, SQ, XCH, X2)

            STATS_T, STATS_V = t_p.v, v_p.v
            ROWS_A = g_rows.v

            QKV = ea.enter_context(nc.psum_tensor("QKV", [128, 6 * TT], F32))
            AUX1 = ea.enter_context(nc.psum_tensor("AUX1", [128, TT], F32))
            AUX2 = ea.enter_context(nc.psum_tensor("AUX2", [128, TT], F32))

            @blk.vector
            def _(v):
                v.wait_ge(v_p.sem, v_p.v)
                v_p.inc(v.memset(RP1[:], 0.0))
                v_p.inc(v.memset(RP2[:], 0.0))

            pads_free_t = 0
            pads_free_v = v_p.v
            first_waits = [(v_p.sem, STATS_V), (t_p.sem, STATS_T)]
            prevA = None
            ev_free = []
            vdr_g = 0

            for tt in range(NTT):
                def qkv_consume(dch, b, xn_v, a_pre=a_p.v):
                    @blk.tensor
                    def _(t, dch=dch, b=b, xn_v=xn_v, a_pre=a_pre):
                        t.wait_ge(v_p.sem, max(xn_v, WQHL_V))
                        if dch == 0:
                            t.wait_ge(a_p.sem, a_pre)
                        bs = b * TT
                        for ech in range(6):
                            w0 = (ech * NDCH + dch) * 128
                            t_p.inc(t.matmul(
                                QKV[:, ech * TT:(ech + 1) * TT],
                                WQH[:, w0:w0 + 128], XNH[:, bs:bs + TT],
                                start=(dch == 0), stop=False))
                            t_p.inc(t.matmul(
                                QKV[:, ech * TT:(ech + 1) * TT],
                                WQH[:, w0:w0 + 128], XNL[:, bs:bs + TT],
                                start=False, stop=False))
                            t_p.inc(t.matmul(
                                QKV[:, ech * TT:(ech + 1) * TT],
                                WQL[:, w0:w0 + 128], XNH[:, bs:bs + TT],
                                start=False, stop=(dch == NDCH - 1)))
                    return (t_p.sem, t_p.v)

                prevA = emit_ln_tile(xf, tt, tt, ROWS_A, XCH, T1, XN2, XNH, XNL, RBS, MRBS,
                                     RP1, RP2, AUX1, AUX2, pads_free_t, pads_free_v,
                                     first_waits, prevA, qkv_consume)
                first_waits = []
                t_qkv_done = t_p.v

                @blk.scalar
                def _(sc_, t_qkv_done=t_qkv_done, ev_free=list(ev_free)):
                    sc_.wait_ge(t_p.sem, t_qkv_done)
                    sc_.wait_ge(g_cst.sem, CST)
                    for sem, val in ev_free:
                        sc_.wait_ge(sem, val)
                    for ech in range(6):
                        a_p.inc(sc_.activation(out=EV[:, ech * TT:(ech + 1) * TT],
                                               in_=QKV[:, ech * TT:(ech + 1) * TT],
                                               func=AF.Identity, bias=BQKV[:, ech:ech + 1],
                                               scale=1.0))

                a_ev = a_p.v

                @blk.vector
                def _(v, a_ev=a_ev, gprev=g_sp.v, tprev=t_p.v):
                    v.wait_ge(a_p.sem, a_ev)
                    v.wait_ge(v_p.sem, v_p.v)
                    v.wait_ge(g_sp.sem, gprev)
                    v.wait_ge(t_p.sem, tprev)
                    v_p.inc(v.tensor_copy(EVH[:], EV[:]))
                    v.wait_ge(v_p.sem, v_p.v)
                    v_p.inc(v.tensor_tensor(out=EVL[:], in0=EV[:], in1=EVH[:],
                                            op=ALU.subtract))

                ev_hl = v_p.v

                @blk.gpsimd
                def _(g, tt=tt, ev_hl=ev_hl):
                    g.wait_ge(v_p.sem, ev_hl)
                    for h in range(2):
                        for lv, EVX in ((0, EVH), (1, EVL)):
                            g_sp.inc(g.dma_start(
                                out=qdr[h, lv, :, tt * TT:(tt + 1) * TT],
                                in_=EVX[:, h * TT:(h + 1) * TT]), 16)
                            g_sp.inc(g.dma_start(
                                out=kdr[h, lv, :, tt * TT:(tt + 1) * TT],
                                in_=EVX[:, (2 + h) * TT:(3 + h) * TT]), 16)

                for i, (ech, slot) in enumerate(
                        [(0, 0), (1, 16), (2, 32), (3, 48)]):
                    @blk.vector
                    def _(v, ech=ech, a_ev=a_ev, tprev=t_p.v, vprev=v_p.v):
                        v.wait_ge(a_p.sem, a_ev)
                        v.wait_ge(t_p.sem, tprev)
                        v.wait_ge(v_p.sem, vprev)
                        v_p.inc(v.tensor_tensor(out=KSQ[:],
                                                in0=EV[:, ech * TT:(ech + 1) * TT],
                                                in1=EV[:, ech * TT:(ech + 1) * TT],
                                                op=ALU.mult))

                    sq_v = v_p.v

                    @blk.tensor
                    def _(t, sq_v=sq_v, vfree=v_p.v):
                        t.wait_ge(v_p.sem, sq_v)
                        t_p.inc(t.matmul(AUX1[:], ONES[:], KSQ[:], start=True, stop=True))

                    mm_t = t_p.v

                    @blk.vector
                    def _(v, mm_t=mm_t, slot=slot, tt=tt):
                        v.wait_ge(t_p.sem, mm_t)
                        v.wait_ge(v_p.sem, v_p.v)
                        v_p.inc(v.reduce_max(NRM[0:1, slot + tt:slot + tt + 1],
                                             AUX1[0:1, :], axis=AX.X))

                for h in range(2):
                    @blk.tensor
                    def _(t, h=h, a_ev=a_ev, vfree=v_p.v):
                        t.wait_ge(a_p.sem, a_ev)
                        t.wait_ge(v_p.sem, vfree)
                        for i in range(4):
                            t_p.inc(t.transpose(
                                AUX2[:, i * 128:(i + 1) * 128],
                                EV[:, (4 + h) * TT + i * 128:(4 + h) * TT + (i + 1) * 128],
                                IDENT[:]))

                    tp_t = t_p.v

                    @blk.vector
                    def _(v, h=h, tp_t=tp_t, vdr_g=vdr_g):
                        v.wait_ge(t_p.sem, tp_t)
                        v.wait_ge(v_p.sem, v_p.v)
                        v.wait_ge(g_sp.sem, vdr_g)
                        hi = slice((2 * h) * TT, (2 * h + 1) * TT)
                        lo = slice((2 * h + 1) * TT, (2 * h + 2) * TT)
                        v_p.inc(v.tensor_copy(VTT[:, hi], AUX2[:]))
                        v.wait_ge(v_p.sem, v_p.v)
                        v_p.inc(v.tensor_tensor(out=VTT[:, lo], in0=AUX2[:],
                                                in1=VTT[:, hi], op=ALU.subtract))

                vtt_v = v_p.v

                @blk.gpsimd
                def _(g, tt=tt, vtt_v=vtt_v):
                    g.wait_ge(v_p.sem, vtt_v)
                    for hl in range(4):
                        h, lv = hl // 2, hl % 2
                        for i in range(4):
                            tk = tt * TT + i * 128
                            g_sp.inc(g.dma_start(
                                out=vdr[h, lv, tk:tk + 128, :],
                                in_=VTT[:, hl * TT + i * 128:hl * TT + (i + 1) * 128]), 16)

                vdr_g = g_sp.v
                pads_free_t = t_p.v
                pads_free_v = v_p.v
                ev_free = [(v_p.sem, v_p.v), (t_p.sem, t_p.v), (g_sp.sem, g_sp.v)]

        A_T, A_V, A_A, A_G = t_p.v, v_p.v, a_p.v, g_sp.v

        with contextlib.ExitStack() as eb0:
            BBPS = eb0.enter_context(nc.psum_tensor("BBPS", [128, TT], F32))

            @blk.vector
            def _(v):
                v.wait_ge(v_p.sem, A_V)
                v.wait_ge(g_cst.sem, CST)
                v_p.inc(v.memset(BBPAD[:], 0.0))
                for h in range(2):
                    v.wait_ge(v_p.sem, v_p.v)
                    v.wait_ge(a_p.sem, A_A + 2 * h)
                    v_p.inc(v.tensor_copy(BROW[0:1, 16:32], NRM[0:1, 32 + h * 16:48 + h * 16]))
                    for qt in range(1, NTT):
                        v.wait_ge(v_p.sem, v_p.v)
                        v_p.inc(v.tensor_tensor(out=BROW[0:1, 16 + qt:17 + qt],
                                                in0=BROW[0:1, 16 + qt:17 + qt],
                                                in1=BROW[0:1, 15 + qt:16 + qt], op=ALU.max))
                    v.wait_ge(v_p.sem, v_p.v)
                    v_p.inc(v.tensor_tensor(out=BROW[0:1, 32:48],
                                            in0=NRM[0:1, h * 16:(h + 1) * 16],
                                            in1=BROW[0:1, 16:32], op=ALU.mult))
                    v.wait_ge(a_p.sem, A_A + 2 * (h + 1))
                    v_p.inc(v.tensor_copy(BBPAD[0:1, h * 16:(h + 1) * 16],
                                          BROW[0:1, 32:48]))

            @blk.scalar
            def _(sc_):
                vv = A_V + 1
                for h in range(2):
                    vv += 17
                    sc_.wait_ge(v_p.sem, vv)
                    a_p.inc(sc_.activation(out=BROW[0:1, 32:48], in_=BROW[0:1, 32:48],
                                           func=AF.Sqrt))
                    sc_.wait_ge(a_p.sem, a_p.v)
                    a_p.inc(sc_.activation(out=BROW[0:1, 32:48], in_=BROW[0:1, 32:48],
                                           func=AF.Identity, scale=-SC))
                    vv += 1

            @blk.tensor
            def _(t, a_at=a_p.v):
                t.wait_ge(v_p.sem, v_p.v)
                t.wait_ge(a_p.sem, a_at)
                t_p.inc(t.matmul(BBPS[:, 0:32], ONES[:], BBPAD[:, 0:32],
                                 start=True, stop=True))

            @blk.vector
            def _(v):
                v.wait_ge(t_p.sem, t_p.v)
                v.wait_ge(v_p.sem, v_p.v)
                v_p.inc(v.tensor_copy(BB[:], BBPS[:, 0:32]))

        BB_V = v_p.v
        fence()

        with contextlib.ExitStack() as eb:
            KHH = eb.enter_context(nc.sbuf_tensor("KHH", [128, S], BF16))
            KHL = eb.enter_context(nc.sbuf_tensor("KHL", [128, S], BF16))
            VHH = eb.enter_context(nc.sbuf_tensor("VHH", [128, S], BF16))
            VHL = eb.enter_context(nc.sbuf_tensor("VHL", [128, S], BF16))
            QTH = eb.enter_context(nc.sbuf_tensor("QTH", [128, 2 * TT], BF16))
            QTL = eb.enter_context(nc.sbuf_tensor("QTL", [128, 2 * TT], BF16))
            P = eb.enter_context(nc.sbuf_tensor("P", [128, 2 * TT], F32))
            PH = eb.enter_context(nc.sbuf_tensor("PH", [128, 2 * TT], BF16))
            PL = eb.enter_context(nc.sbuf_tensor("PL", [128, 2 * TT], BF16))
            RBN = eb.enter_context(nc.sbuf_tensor("RBN", [128, TT], F32))
            AO = eb.enter_context(nc.sbuf_tensor("AO", [128, 2 * NTT * TT], F32))
            AOH = eb.enter_context(nc.sbuf_tensor("AOH", [128, 2 * TT], BF16))
            AOL = eb.enter_context(nc.sbuf_tensor("AOL", [128, 2 * TT], BF16))
            WO = eb.enter_context(nc.sbuf_tensor("WO", [128, 2 * NDCH * 128], F32))
            WOH = eb.enter_context(nc.sbuf_tensor("WOH", [128, 2 * NDCH * 128], BF16))
            WOL = eb.enter_context(nc.sbuf_tensor("WOL", [128, 2 * NDCH * 128], BF16))
            SP = eb.enter_context(nc.sbuf_tensor("SP", [128, 2 * TT], F32))
            STP = eb.enter_context(nc.psum_tensor("STP", [128, 2 * TT], F32))
            AVP = eb.enter_context(nc.psum_tensor("AVP", [128, TT], F32))
            SMP = eb.enter_context(nc.psum_tensor("SMP", [128, TT], F32))
            OPP = eb.enter_context(nc.psum_tensor("OPP", [128, 2 * TT], F32))

            @blk.sync
            def _(sy):
                src = woutT.rearrange("(dch p) (ech j) -> p dch ech j", p=128, j=128)
                s_wo.inc(sy.dma_start(out=WO.ap().rearrange(
                    "p (dch ech j) -> p dch ech j", dch=2, ech=NDCH), in_=src), 16)

            WO_LD = s_wo.v

            @blk.vector
            def _(v):
                v.wait_ge(s_wo.sem, WO_LD)
                v.wait_ge(v_p.sem, v_p.v)
                v_p.inc(v.tensor_copy(WOH[:], WO[:]))
                v.wait_ge(v_p.sem, v_p.v)
                v_p.inc(v.tensor_tensor(out=WOL[:], in0=WO[:], in1=WOH[:],
                                        op=ALU.subtract))

            WOHL_V = v_p.v
            ao_done_v = max(BB_V, WOHL_V)
            AB_A = a_p.v
            prev_pl_tail = []
            prev_ae_tail = []
            prev_tsum_tail = []
            qt_last_s = {}

            for h in range(2):
                @blk.sync
                def _(sy, h=h, t_all=t_p.v):
                    sy.wait_ge(g_sp.sem, A_G)
                    if h == 1:
                        sy.wait_ge(t_p.sem, t_all)
                    for lv, KHX in ((0, KHH), (1, KHL)):
                        for i in range(2):
                            s_kv.inc(sy.dma_start(
                                out=KHX[:, i * 4096:(i + 1) * 4096],
                                in_=kdr[h, lv, :, i * 4096:(i + 1) * 4096]), 16)
                    vsrc = vdr.rearrange("h lv (tb p) j -> h lv p tb j", p=128)
                    for lv, VHX in ((0, VHH), (1, VHL)):
                        for i in range(2):
                            s_kv.inc(sy.dma_start(
                                out=VHX.ap().rearrange("p (tb j) -> p tb j", j=128)[
                                    :, i * 32:(i + 1) * 32, :],
                                in_=vsrc[h, lv, :, i * 32:(i + 1) * 32, :]), 16)

                KV_LD = s_kv.v

                for qt in range(NTT):
                    nkt = (qt + 1) * 4
                    qb = qt % 2
                    gk = (h, qt)
                    qsem = s_qt0 if qb == 0 else s_qt1

                    @blk.sync
                    def _(sy, h=h, qt=qt, qb=qb, qsem=qsem):
                        key = (h, qt - 2)
                        if key in qt_last_s:
                            sy.wait_ge(t_p.sem, qt_last_s[key])
                        qsem.inc(sy.dma_start(out=QTH[:, qb * TT:(qb + 1) * TT],
                                              in_=qdr[h, 0, :, qt * TT:(qt + 1) * TT]), 16)
                        qsem.inc(sy.dma_start(out=QTL[:, qb * TT:(qb + 1) * TT],
                                              in_=qdr[h, 1, :, qt * TT:(qt + 1) * TT]), 16)

                    q_ld = qsem.v

                    T0, A0, V0 = t_p.v, a_p.v, v_p.v
                    ts_vals = [0] * nkt
                    tsum_vals = [0] * nkt
                    ae_vals = [A0 + kt + 1 for kt in range(nkt)]
                    pl_vals = [0] * nkt
                    tv = T0
                    for kt in range(nkt):
                        tv += 3
                        ts_vals[kt] = tv
                        if kt >= 1:
                            tv += 5
                            tsum_vals[kt - 1] = tv
                    tv += 5
                    tsum_vals[nkt - 1] = tv
                    vv = V0
                    for kt in range(nkt):
                        if kt >= nkt - 4:
                            vv += 1
                        vv += 2
                        pl_vals[kt] = vv
                    qt_last_s[gk] = ts_vals[nkt - 1]

                    @blk.tensor
                    def _(t, h=h, qt=qt, qb=qb, nkt=nkt, q_ld=q_ld, KV_LD=KV_LD,
                          qsem=qsem, av0=ao_done_v, pae=list(prev_ae_tail)):
                        t.wait_ge(qsem.sem, q_ld)
                        t.wait_ge(s_kv.sem, KV_LD)
                        qs = slice(qb * TT, (qb + 1) * TT)

                        def emit_s(kt):
                            if kt >= 2:
                                t.wait_ge(a_p.sem, ae_vals[kt - 2])
                            elif len(pae) == 2:
                                t.wait_ge(a_p.sem, pae[kt])
                            else:
                                t.wait_ge(a_p.sem, AB_A)
                            ks = slice(kt * 128, (kt + 1) * 128)
                            pb = kt % 2
                            t_p.inc(t.matmul(STP[:, pb * TT:(pb + 1) * TT],
                                             KHH[:, ks], QTH[:, qs], start=True, stop=False))
                            t_p.inc(t.matmul(STP[:, pb * TT:(pb + 1) * TT],
                                             KHH[:, ks], QTL[:, qs], start=False, stop=False))
                            t_p.inc(t.matmul(STP[:, pb * TT:(pb + 1) * TT],
                                             KHL[:, ks], QTH[:, qs], start=False, stop=True))

                        def emit_av(kt):
                            t.wait_ge(v_p.sem, pl_vals[kt])
                            if kt == 0:
                                t.wait_ge(v_p.sem, av0)
                            ks = slice(kt * 128, (kt + 1) * 128)
                            pb = kt % 2
                            ps = slice(pb * TT, (pb + 1) * TT)
                            t_p.inc(t.matmul(AVP[:], VHH[:, ks], PH[:, ps],
                                             start=(kt == 0), stop=False))
                            t_p.inc(t.matmul(AVP[:], VHH[:, ks], PL[:, ps],
                                             start=False, stop=False))
                            t_p.inc(t.matmul(AVP[:], VHL[:, ks], PH[:, ps],
                                             start=False, stop=(kt == nkt - 1)))
                            t_p.inc(t.matmul(SMP[:], ONESB[:], PH[:, ps],
                                             start=(kt == 0), stop=False))
                            t_p.inc(t.matmul(SMP[:], ONESB[:], PL[:, ps],
                                             start=False, stop=(kt == nkt - 1)))

                        for kt in range(nkt):
                            emit_s(kt)
                            if kt >= 1:
                                emit_av(kt - 1)
                        emit_av(nkt - 1)
                        assert t_p.v == tsum_vals[nkt - 1], (t_p.v, tsum_vals[nkt - 1])

                    @blk.scalar
                    def _(sc_, h=h, qt=qt, nkt=nkt, ppl=list(prev_pl_tail)):
                        for kt in range(nkt):
                            pb = kt % 2
                            sc_.wait_ge(t_p.sem, ts_vals[kt])
                            sc_.wait_ge(v_p.sem, BB_V)
                            if kt >= 2:
                                sc_.wait_ge(v_p.sem, pl_vals[kt - 2])
                            elif len(ppl) == 2:
                                sc_.wait_ge(v_p.sem, ppl[kt])
                            a_p.inc(sc_.activation(out=P[:, pb * TT:(pb + 1) * TT],
                                                   in_=STP[:, pb * TT:(pb + 1) * TT],
                                                   func=AF.Exp,
                                                   bias=BB[:, h * 16 + qt:h * 16 + qt + 1],
                                                   scale=SC))
                        assert a_p.v == ae_vals[-1]

                    @blk.vector
                    def _(v, h=h, qt=qt, nkt=nkt, ptt=list(prev_tsum_tail)):
                        for kt in range(nkt):
                            pb = kt % 2
                            ps = slice(pb * TT, (pb + 1) * TT)
                            v.wait_ge(a_p.sem, ae_vals[kt])
                            if kt >= nkt - 4:
                                off = kt - (nkt - 4)
                                v.wait_ge(v_p.sem, v_p.v)
                                v_p.inc(v.tensor_tensor(
                                    out=P[:, ps], in0=P[:, ps],
                                    in1=MASKS[:, off * TT:(off + 1) * TT], op=ALU.mult))
                            v.wait_ge(v_p.sem, v_p.v)
                            if kt >= 2:
                                v.wait_ge(t_p.sem, tsum_vals[kt - 2])
                            elif len(ptt) == 2:
                                v.wait_ge(t_p.sem, ptt[kt])
                            v_p.inc(v.tensor_copy(PH[:, ps], P[:, ps]))
                            v.wait_ge(v_p.sem, v_p.v)
                            v_p.inc(v.tensor_tensor(out=PL[:, ps], in0=P[:, ps],
                                                    in1=PH[:, ps], op=ALU.subtract))
                        assert v_p.v == pl_vals[-1]
                        v.wait_ge(t_p.sem, tsum_vals[nkt - 1])
                        v.wait_ge(v_p.sem, v_p.v)
                        v_p.inc(v.reciprocal(RBN[:], SMP[:]))
                        v.wait_ge(v_p.sem, v_p.v)
                        v_p.inc(v.tensor_tensor(
                            out=AO[:, (h * NTT + qt) * TT:(h * NTT + qt + 1) * TT],
                            in0=AVP[:], in1=RBN[:], op=ALU.mult))

                    ao_done_v = v_p.v
                    prev_ae_tail = ae_vals[-2:]
                    prev_tsum_tail = tsum_vals[-2:]
                    prev_pl_tail = pl_vals[-2:]

            AO_DONE = v_p.v
            op_evac_a = {}
            sp_dma_g = {}
            ao_dec_v = {}
            qt_mm_t = {}

            for qt in range(NTT):
                @blk.vector
                def _(v, qt=qt, tprev=qt_mm_t.get(qt - 1, 0)):
                    v.wait_ge(v_p.sem, AO_DONE)
                    v.wait_ge(v_p.sem, v_p.v)
                    if tprev:
                        v.wait_ge(t_p.sem, tprev)
                    for dch in range(2):
                        ds = slice(dch * TT, (dch + 1) * TT)
                        aos = slice((dch * NTT + qt) * TT, (dch * NTT + qt + 1) * TT)
                        v_p.inc(v.tensor_copy(AOH[:, ds], AO[:, aos]))
                        v.wait_ge(v_p.sem, v_p.v)
                        v_p.inc(v.tensor_tensor(out=AOL[:, ds], in0=AO[:, aos],
                                                in1=AOH[:, ds], op=ALU.subtract))

                ao_dec_v[qt] = v_p.v

                for ech in range(NDCH):
                    n = qt * NDCH + ech
                    ob = n % 2

                    @blk.tensor
                    def _(t, qt=qt, ech=ech, ob=ob, n=n, a_at=a_p.v):
                        t.wait_ge(v_p.sem, ao_dec_v[qt])
                        if n >= 2:
                            t.wait_ge(a_p.sem, op_evac_a[n - 2])
                        else:
                            t.wait_ge(a_p.sem, a_at)
                        for dch in range(2):
                            ws = slice((dch * NDCH + ech) * 128, (dch * NDCH + ech + 1) * 128)
                            ds = slice(dch * TT, (dch + 1) * TT)
                            t_p.inc(t.matmul(OPP[:, ob * TT:(ob + 1) * TT],
                                             WOH[:, ws], AOH[:, ds],
                                             start=(dch == 0), stop=False))
                            t_p.inc(t.matmul(OPP[:, ob * TT:(ob + 1) * TT],
                                             WOH[:, ws], AOL[:, ds],
                                             start=False, stop=False))
                            t_p.inc(t.matmul(OPP[:, ob * TT:(ob + 1) * TT],
                                             WOL[:, ws], AOH[:, ds],
                                             start=False, stop=(dch == 1)))

                    mm_t = t_p.v
                    qt_mm_t[qt] = t_p.v

                    @blk.scalar
                    def _(sc_, mm_t=mm_t, n=n, ob=ob):
                        sc_.wait_ge(t_p.sem, mm_t)
                        if n >= 2:
                            sc_.wait_ge(sp_dma_g[n - 2][0].sem, sp_dma_g[n - 2][1])
                        a_p.inc(sc_.activation(out=SP[:, ob * TT:(ob + 1) * TT],
                                               in_=OPP[:, ob * TT:(ob + 1) * TT],
                                               func=AF.Identity))

                    op_evac_a[n] = a_p.v

                    @blk.gpsimd
                    def _(g, qt=qt, ech=ech, ob=ob, n=n):
                        g.wait_ge(a_p.sem, op_evac_a[n])
                        rank = qt if qt < 8 else 15 - qt
                        half = 0 if qt < 8 else 1
                        gsem = g_sp0 if ob == 0 else g_sp1
                        gsem.inc(g.dma_start(
                            out=spart[rank, ech * 128:(ech + 1) * 128,
                                      half * TT:(half + 1) * TT],
                            in_=SP[:, ob * TT:(ob + 1) * TT]), 16)

                    sp_dma_g[n] = (g_sp0 if ob == 0 else g_sp1, (g_sp0 if ob == 0 else g_sp1).v)

            fence()

            @blk.gpsimd
            def _(g, a0=g_sp0.v, b0=g_sp1.v):
                g.wait_ge(g_sp0.sem, a0)
                g.wait_ge(g_sp1.sem, b0)
                cc.inc(g.collective_compute(
                    "ReduceScatter", ALU.add, replica_groups=[list(range(NC))],
                    ins=[spart.opt()], outs=[sred.opt()]))

        G = es.enter_context(nc.sbuf_tensor("G", [128, NDCH * TOWN], BF16))
        with contextlib.ExitStack() as ec:
            XCHc = ec.enter_context(nc.sbuf_tensor("XCHc", [128, 2 * TT], F32))
            X2c = ec.enter_context(nc.sbuf_tensor("X2c", [128, 2 * TT], F32))
            T1c = ec.enter_context(nc.sbuf_tensor("T1c", [128, 2 * TT], F32))
            XN2c = ec.enter_context(nc.sbuf_tensor("XN2c", [128, 2 * TT], F32))
            RBSc = ec.enter_context(nc.sbuf_tensor("RBSc", [128, TT], F32))
            MRBSc = ec.enter_context(nc.sbuf_tensor("MRBSc", [128, TT], F32))
            RP1c = ec.enter_context(nc.sbuf_tensor("RP1c", [128, TT], F32))
            RP2c = ec.enter_context(nc.sbuf_tensor("RP2c", [128, TT], F32))
            XF8 = ec.enter_context(nc.sbuf_tensor("XF8", [128, NDCH * TOWN], F8))
            WST = ec.enter_context(nc.sbuf_tensor("WST", [128, 4 * 2048], F32))
            W8 = ec.enter_context(nc.sbuf_tensor("W8", [128, 4 * 2048], F8))
            GV = ec.enter_context(nc.sbuf_tensor("GV", [128, 2 * TT], BF16))
            F1 = ec.enter_context(nc.sbuf_tensor("F1", [128, 2 * TT], BF16))
            ACTB = ec.enter_context(nc.sbuf_tensor("ACTB", [128, 2 * TT], BF16))
            ACT8 = ec.enter_context(nc.sbuf_tensor("ACT8", [128, NHCH * TOWN], F8))

            with contextlib.ExitStack() as ec1:
                SUMSc = ec1.enter_context(nc.psum_tensor("SUMSc", [128, TT], F32))
                SQc = ec1.enter_context(nc.psum_tensor("SQc", [128, TT], F32))
                emit_ln_stats(xo, 2, 16, SUMSc, SQc, XCHc, X2c)

            CST_T, CST_V = t_p.v, v_p.v
            ROWS_C = g_rows.v
            AUX1c = ec.enter_context(nc.psum_tensor("AUX1c", [128, TT], F32))
            AUX2c = ec.enter_context(nc.psum_tensor("AUX2c", [128, TT], F32))
            FFP = ec.enter_context(nc.psum_tensor("FFP", [128, 4 * TT], F32))

            @blk.vector
            def _(v):
                v.wait_ge(v_p.sem, v_p.v)
                v_p.inc(v.memset(RP1c[:], 0.0))
                v_p.inc(v.memset(RP2c[:], 0.0))

            padc_t = 0
            padc_v = v_p.v
            firstc = [(v_p.sem, CST_V), (t_p.sem, CST_T)]
            prevC = None
            for tt in range(2):
                def f8_consume(dch, b, xn_v, tt=tt):
                    @blk.vector
                    def _(v, dch=dch, b=b, tt=tt):
                        v.wait_ge(v_p.sem, v_p.v)
                        v_p.inc(v.tensor_copy(
                            XF8[:, dch * TOWN + tt * TT:dch * TOWN + (tt + 1) * TT],
                            XN2c[:, b * TT:(b + 1) * TT]))
                    return (v_p.sem, v_p.v)

                prevC = emit_ln_tile(xo, tt, 16 + tt, ROWS_C, XCHc, T1c, XN2c,
                                     None, None, RBSc, MRBSc, RP1c, RP2c, AUX1c,
                                     AUX2c, padc_t, padc_v, firstc, prevC, f8_consume)
                firstc = []
                padc_t = t_p.v
                padc_v = v_p.v

            XF8_V = v_p.v

            st_ld, q8_v, ev_scl, act_v, ff_mm_t = {}, {}, {}, {}, {}
            for hch in range(NHCH):
                sb = hch % 2

                wsem = s_w0 if sb == 0 else s_w1

                @blk.sync
                def _(sy, hch=hch, sb=sb, wsem=wsem):
                    if hch >= 2:
                        sy.wait_ge(v_p.sem, q8_v[hch - 2])
                    for m, w in ((0, wg0), (1, wg1)):
                        src = w.rearrange("(dch p) (hch j) -> p hch dch j", p=128, j=128)
                        wsem.inc(sy.dma_start(
                            out=WST.ap().rearrange("p (s dch j) -> p s dch j", s=4, j=128)[
                                :, 2 * sb + m, :, :],
                            in_=src[:, hch, :, :]), 16)

                st_ld[hch] = wsem.v

                @blk.vector
                def _(v, hch=hch, sb=sb, wsem=wsem):
                    v.wait_ge(wsem.sem, st_ld[hch])
                    v.wait_ge(v_p.sem, v_p.v)
                    if hch >= 2:
                        v.wait_ge(t_p.sem, ff_mm_t[hch - 2])
                    v_p.inc(v.tensor_copy(W8[:, (2 * sb) * 2048:(2 * sb + 2) * 2048],
                                          WST[:, (2 * sb) * 2048:(2 * sb + 2) * 2048]))

                q8_v[hch] = v_p.v

                @blk.tensor
                def _(t, hch=hch, sb=sb, a_at=a_p.v):
                    t.wait_ge(v_p.sem, max(q8_v[hch], XF8_V))
                    if hch >= 1:
                        t.wait_ge(a_p.sem, ev_scl[hch - 1])
                    else:
                        t.wait_ge(a_p.sem, a_at)
                    for m in range(2):
                        for th in range(2):
                            for dch in range(NDCH):
                                t_p.inc(t.matmul(
                                    FFP[:, (2 * m + th) * TT:(2 * m + th + 1) * TT],
                                    W8[:, (2 * sb + m) * 2048 + dch * 128:
                                       (2 * sb + m) * 2048 + (dch + 1) * 128],
                                    XF8[:, dch * TOWN + th * TT:dch * TOWN + (th + 1) * TT],
                                    start=(dch == 0), stop=(dch == NDCH - 1)))

                ff_mm_t[hch] = t_p.v

                @blk.scalar
                def _(sc_, hch=hch):
                    sc_.wait_ge(t_p.sem, ff_mm_t[hch])
                    if hch >= 1:
                        sc_.wait_ge(v_p.sem, act_v[hch - 1])
                    for th in range(2):
                        a_p.inc(sc_.activation(out=GV[:, th * TT:(th + 1) * TT],
                                               in_=FFP[:, th * TT:(th + 1) * TT],
                                               func=AF.Gelu_apprx_tanh))

                ev_scl[hch] = a_p.v

                @blk.vector
                def _(v, hch=hch):
                    v.wait_ge(a_p.sem, ev_scl[hch])
                    v.wait_ge(t_p.sem, ff_mm_t[hch])
                    for th in range(2):
                        v.wait_ge(v_p.sem, v_p.v)
                        v_p.inc(v.tensor_copy(F1[:, th * TT:(th + 1) * TT],
                                              FFP[:, (2 + th) * TT:(3 + th) * TT]))
                        v.wait_ge(v_p.sem, v_p.v)
                        v_p.inc(v.tensor_tensor(out=ACTB[:, th * TT:(th + 1) * TT],
                                                in0=GV[:, th * TT:(th + 1) * TT],
                                                in1=F1[:, th * TT:(th + 1) * TT],
                                                op=ALU.mult))
                        v.wait_ge(v_p.sem, v_p.v)
                        v_p.inc(v.tensor_copy(
                            ACT8[:, hch * TOWN + th * TT:hch * TOWN + (th + 1) * TT],
                            ACTB[:, th * TT:(th + 1) * TT]))

                act_v[hch] = v_p.v

            ACT_V = v_p.v
            FF_LAST_T = ff_mm_t[NHCH - 1]
            FF_LAST_Q = q8_v[NHCH - 1]
            FF_LAST_E = ev_scl[NHCH - 1]

            wl_ld, wl_q8, wl_mm, g_evac = {}, {}, {}, {}
            for ech in range(NDCH):
                for grp in range(4):
                    n = ech * 4 + grp
                    sb = n % 2

                    wsem = s_w0 if sb == 0 else s_w1

                    @blk.sync
                    def _(sy, ech=ech, grp=grp, sb=sb, n=n, wsem=wsem):
                        if n >= 2:
                            sy.wait_ge(v_p.sem, wl_q8[n - 2])
                        else:
                            sy.wait_ge(v_p.sem, FF_LAST_Q)
                        src = wl.rearrange("(hch p) (ech j) -> p hch ech j", p=128, j=128)
                        wsem.inc(sy.dma_start(
                            out=WST.ap().rearrange("p (s dch j) -> p s dch j", s=4, j=128)[
                                :, 2 * sb, :, :],
                            in_=src[:, grp * 16:(grp + 1) * 16, ech, :]), 16)

                    wl_ld[n] = wsem.v

                    @blk.vector
                    def _(v, n=n, sb=sb, wsem=wsem):
                        v.wait_ge(wsem.sem, wl_ld[n])
                        v.wait_ge(v_p.sem, v_p.v)
                        if n >= 2:
                            v.wait_ge(t_p.sem, wl_mm[n - 2])
                        else:
                            v.wait_ge(t_p.sem, FF_LAST_T)
                        v_p.inc(v.tensor_copy(W8[:, (2 * sb) * 2048:(2 * sb) * 2048 + 2048],
                                              WST[:, (2 * sb) * 2048:(2 * sb) * 2048 + 2048]))

                    wl_q8[n] = v_p.v

                    @blk.tensor
                    def _(t, ech=ech, grp=grp, n=n, sb=sb):
                        t.wait_ge(v_p.sem, max(wl_q8[n], ACT_V))
                        if grp == 0:
                            if ech >= 1:
                                t.wait_ge(a_p.sem, g_evac[ech - 1])
                            else:
                                t.wait_ge(a_p.sem, FF_LAST_E)
                        for i in range(16):
                            hch = grp * 16 + i
                            for th in range(2):
                                t_p.inc(t.matmul(
                                    FFP[:, th * TT:(th + 1) * TT],
                                    W8[:, (2 * sb) * 2048 + i * 128:(2 * sb) * 2048 + (i + 1) * 128],
                                    ACT8[:, hch * TOWN + th * TT:hch * TOWN + (th + 1) * TT],
                                    start=(hch == 0), stop=(hch == NHCH - 1)))

                    wl_mm[n] = t_p.v

                @blk.scalar
                def _(sc_, ech=ech):
                    sc_.wait_ge(t_p.sem, wl_mm[ech * 4 + 3])
                    for th in range(2):
                        a_p.inc(sc_.activation(
                            out=G[:, ech * TOWN + th * TT:ech * TOWN + (th + 1) * TT],
                            in_=FFP[:, th * TT:(th + 1) * TT], func=AF.Silu))

                g_evac[ech] = a_p.v

        G_A = a_p.v
        fence()

        with contextlib.ExitStack() as ed:
            SR = ed.enter_context(nc.sbuf_tensor("SR", [128, 2 * TT], F32))
            XOT = ed.enter_context(nc.sbuf_tensor("XOT", [128, 2 * TT], F32))
            SBT = ed.enter_context(nc.sbuf_tensor("SBT", [128, 2 * TT], F32))
            OT = ed.enter_context(nc.sbuf_tensor("OT", [128, 2 * TT], F32))

            d_ld, d_sc, d_ot, d_st = {}, {}, {}, {}
            for th in range(2):
                for ech in range(NDCH):
                    n = th * NDCH + ech
                    b = n % 2

                    dsem = s_d0 if b == 0 else s_d1

                    @blk.sync
                    def _(sy, th=th, ech=ech, b=b, n=n, dsem=dsem):
                        sy.wait_ge(cc.sem, 1)
                        if n >= 2:
                            sy.wait_ge(a_p.sem, d_sc[n - 2])
                            sy.wait_ge(v_p.sem, d_ot[n - 2])
                        dsem.inc(sy.dma_start(
                            out=SR[:, b * TT:(b + 1) * TT],
                            in_=sred[ech * 128:(ech + 1) * 128, th * TT:(th + 1) * TT]), 16)
                        dsem.inc(sy.dma_start(
                            out=XOT[:, b * TT:(b + 1) * TT],
                            in_=xo[ech * 128:(ech + 1) * 128, th * TT:(th + 1) * TT]), 16)

                    d_ld[n] = (dsem, dsem.v)

                    @blk.scalar
                    def _(sc_, ech=ech, b=b, n=n):
                        sc_.wait_ge(d_ld[n][0].sem, d_ld[n][1])
                        sc_.wait_ge(a_p.sem, G_A)
                        if n >= 2:
                            sc_.wait_ge(v_p.sem, d_ot[n - 2])
                        a_p.inc(sc_.activation(out=SBT[:, b * TT:(b + 1) * TT],
                                               in_=SR[:, b * TT:(b + 1) * TT],
                                               func=AF.Identity, bias=BOUT[:, ech:ech + 1]))

                    d_sc[n] = a_p.v

                    @blk.vector
                    def _(v, th=th, ech=ech, b=b, n=n):
                        v.wait_ge(a_p.sem, d_sc[n])
                        v.wait_ge(v_p.sem, v_p.v)
                        if n >= 2:
                            gsem, gval = d_st[n - 2]
                            v.wait_ge(gsem.sem, gval)
                        v_p.inc(v.tensor_tensor(
                            out=SBT[:, b * TT:(b + 1) * TT],
                            in0=SBT[:, b * TT:(b + 1) * TT],
                            in1=G[:, ech * TOWN + th * TT:ech * TOWN + (th + 1) * TT],
                            op=ALU.mult))
                        v.wait_ge(v_p.sem, v_p.v)
                        v.wait_ge(d_ld[n][0].sem, d_ld[n][1])
                        v_p.inc(v.tensor_tensor(out=OT[:, b * TT:(b + 1) * TT],
                                                in0=SBT[:, b * TT:(b + 1) * TT],
                                                in1=XOT[:, b * TT:(b + 1) * TT],
                                                op=ALU.add))

                    d_ot[n] = v_p.v

                    gosem = g_o0 if b == 0 else g_o1

                    @blk.gpsimd
                    def _(g, th=th, ech=ech, b=b, n=n, gosem=gosem):
                        g.wait_ge(v_p.sem, d_ot[n])
                        gosem.inc(g.dma_start(
                            out=outp[ech * 128:(ech + 1) * 128, th * TT:(th + 1) * TT],
                            in_=OT[:, b * TT:(b + 1) * TT]), 16)

                    d_st[n] = (gosem, gosem.v)

            @blk.gpsimd
            def _(g, a0=g_o0.v, b0=g_o1.v):
                g.wait_ge(g_o0.sem, a0)
                g.wait_ge(g_o1.sem, b0)

    return nc


_NC_CACHE = [None]


def kernel(L0_attn_x, L0_attn_w_qkv, L0_attn_b_qkv, L0_attn_w_out, L0_attn_b_out,
           L0_ffn_w_gating, L0_ffn_w_linear):
    x = np.asarray(L0_attn_x)[0]
    w_qkv = np.asarray(L0_attn_w_qkv)
    b_qkv = np.asarray(L0_attn_b_qkv)
    w_out = np.asarray(L0_attn_w_out)
    b_out = np.asarray(L0_attn_b_out)
    w_g = np.asarray(L0_ffn_w_gating)
    w_l = np.asarray(L0_ffn_w_linear)

    xf = np.ascontiguousarray(x.T)
    woutT_full = np.ascontiguousarray(w_out.T)
    wg0 = np.ascontiguousarray(w_g[0])
    wg1 = np.ascontiguousarray(w_g[1])
    wl = np.ascontiguousarray(w_l)

    in_maps = []
    for c in range(NC):
        b0, b1 = c, 15 - c
        own = np.concatenate([x[b0 * TT:(b0 + 1) * TT], x[b1 * TT:(b1 + 1) * TT]], axis=0)
        xo_c = np.ascontiguousarray(own.T)
        h0 = 2 * c
        rows = np.concatenate([
            w_qkv[h0 * 128:(h0 + 2) * 128],
            w_qkv[2048 + h0 * 128:2048 + (h0 + 2) * 128],
            w_qkv[4096 + h0 * 128:4096 + (h0 + 2) * 128]], axis=0)
        wqkvT_c = np.ascontiguousarray(rows.T)
        bq_c = np.ascontiguousarray(np.concatenate([
            b_qkv[h0 * 128:(h0 + 2) * 128],
            b_qkv[2048 + h0 * 128:2048 + (h0 + 2) * 128],
            b_qkv[4096 + h0 * 128:4096 + (h0 + 2) * 128]], axis=0))
        woutT_c = np.ascontiguousarray(woutT_full[h0 * 128:(h0 + 2) * 128, :])
        in_maps.append({
            "xf": xf, "xo": xo_c, "wqkvT": wqkvT_c, "bqkv": bq_c,
            "woutT": woutT_c, "bout": b_out, "wg0": wg0, "wg1": wg1, "wl": wl,
        })

    if _NC_CACHE[0] is None:
        _NC_CACHE[0] = build_nc()
    res = run_bass_kernel_spmd(_NC_CACHE[0], in_maps, core_ids=list(range(NC)))

    out = np.empty((1, S, D), dtype=np.float32)
    for c in range(NC):
        o = np.asarray(res.results[c]["outp"]).T
        out[0, c * TT:(c + 1) * TT] = o[:TT]
        out[0, (15 - c) * TT:(16 - c) * TT] = o[TT:]
    return out


# version 07
# speedup vs baseline: 1.6910x; 1.0896x over previous
import contextlib
import numpy as np
import concourse.bass as bass
import concourse.mybir as mybir
from concourse.bass_utils import run_bass_kernel_spmd

F32 = mybir.dt.float32
BF16 = mybir.dt.bfloat16
F8 = mybir.dt.float8e4
AF = mybir.ActivationFunctionType
ALU = mybir.AluOpType
AX = mybir.AxisListType

D = 2048
S = 8192
NC = 8
TT = 512
NTT = S // TT
TOWN = 1024
HID = 8192
NDCH = D // 128
NHCH = HID // 128
SC = float(1.0 / np.sqrt(128.0))


class Ctr:
    def __init__(self, es, nc, name):
        self.sem = es.enter_context(nc.semaphore(name))
        self.v = 0

    def inc(self, instr, n=1):
        instr.then_inc(self.sem, n)
        self.v += n
        return self.v


def build_nc():
    nc = bass.Bass("TRN2", target_bir_lowering=False, debug=False)

    xf = nc.dram_tensor("xf", [D, S], F32, kind="ExternalInput").ap()
    xo = nc.dram_tensor("xo", [D, TOWN], F32, kind="ExternalInput").ap()
    wqkvT = nc.dram_tensor("wqkvT", [D, 768], F32, kind="ExternalInput").ap()
    bqkv = nc.dram_tensor("bqkv", [768], F32, kind="ExternalInput").ap()
    woutT = nc.dram_tensor("woutT", [256, D], F32, kind="ExternalInput").ap()
    bout = nc.dram_tensor("bout", [D], F32, kind="ExternalInput").ap()
    wg0 = nc.dram_tensor("wg0", [D, HID], F32, kind="ExternalInput").ap()
    wg1 = nc.dram_tensor("wg1", [D, HID], F32, kind="ExternalInput").ap()
    wl = nc.dram_tensor("wl", [HID, D], F32, kind="ExternalInput").ap()
    outp = nc.dram_tensor("outp", [D, TOWN], F32, kind="ExternalOutput").ap()

    qdr = nc.dram_tensor("qdr", [2, 2, 128, S], BF16, kind="Internal").ap()
    kdr = nc.dram_tensor("kdr", [2, 2, 128, S], BF16, kind="Internal").ap()
    vdr = nc.dram_tensor("vdr", [2, 2, S, 128], BF16, kind="Internal").ap()
    spart = nc.dram_tensor("spart", [NC, D, TOWN], F32, kind="Internal").ap()
    rows_dr = nc.dram_tensor("rows_dr", [2, 18, TT], F32, kind="Internal").ap()
    sred = nc.dram_tensor("sred", [D, TOWN], F32, kind="Internal").ap()

    mask_np = np.zeros((128, 4 * TT), dtype=np.float32)
    for off in range(4):
        mask_np[:, off * TT:(off + 1) * TT] = (
            np.arange(128)[:, None] + 128 * off <= np.arange(TT)[None, :])
    masks_dr = nc.inline_tensor(mask_np, name="masks_c")
    ones_dr = nc.inline_tensor(np.ones((128, 128), dtype=np.float32), name="ones_c")
    ident_dr = nc.inline_tensor(np.eye(128, dtype=np.float32), name="ident_c")
    import ml_dtypes as _mld
    onesb_dr = nc.inline_tensor(np.ones((128, 128), dtype=_mld.bfloat16), name="onesb_c")
    identb_dr = nc.inline_tensor(np.eye(128).astype(_mld.bfloat16), name="identb_c")
    eps_dr = nc.inline_tensor(np.full((128, 1), 1e-5, dtype=np.float32), name="eps_c")

    es = contextlib.ExitStack()
    with es:
        MASKS = es.enter_context(nc.sbuf_tensor("MASKS", [128, 4 * TT], F32))
        ONES = es.enter_context(nc.sbuf_tensor("ONES", [128, 128], F32))
        ONESB = es.enter_context(nc.sbuf_tensor("ONESB", [128, 128], BF16))
        IDENTB = es.enter_context(nc.sbuf_tensor("IDENTB", [128, 128], BF16))
        IDENT = es.enter_context(nc.sbuf_tensor("IDENT", [128, 128], F32))
        EPS = es.enter_context(nc.sbuf_tensor("EPS", [128, 1], F32))
        BQKV = es.enter_context(nc.sbuf_tensor("BQKV", [128, 6], F32))
        BOUT = es.enter_context(nc.sbuf_tensor("BOUT", [128, 16], F32))
        NRM = es.enter_context(nc.sbuf_tensor("NRM", [1, 64], F32))
        BB = es.enter_context(nc.sbuf_tensor("BB", [128, 32], F32))
        BROW = es.enter_context(nc.sbuf_tensor("BROW", [1, 64], F32))
        BBPAD = es.enter_context(nc.sbuf_tensor("BBPAD", [128, 32], F32))

        s_ld = Ctr(es, nc, "s_ld")
        s_x0 = Ctr(es, nc, "s_x0")
        s_x1 = Ctr(es, nc, "s_x1")
        s_pd = Ctr(es, nc, "s_pd")
        s_wq = Ctr(es, nc, "s_wq")
        s_kv = Ctr(es, nc, "s_kv")
        s_qt0 = Ctr(es, nc, "s_qt0")
        s_qt1 = Ctr(es, nc, "s_qt1")
        s_wo = Ctr(es, nc, "s_wo")
        s_w0 = Ctr(es, nc, "s_w0")
        s_w1 = Ctr(es, nc, "s_w1")
        s_d0 = Ctr(es, nc, "s_d0")
        s_d1 = Ctr(es, nc, "s_d1")
        g_sp0 = Ctr(es, nc, "g_sp0")
        g_sp1 = Ctr(es, nc, "g_sp1")
        g_o0 = Ctr(es, nc, "g_o0")
        g_o1 = Ctr(es, nc, "g_o1")
        g_cst = Ctr(es, nc, "g_cst")
        g_sp = Ctr(es, nc, "g_sp")
        g_rows = Ctr(es, nc, "g_rows")
        g_out = Ctr(es, nc, "g_out")
        cc = Ctr(es, nc, "cc")
        t_p = Ctr(es, nc, "t_p")
        v_p = Ctr(es, nc, "v_p")
        a_p = Ctr(es, nc, "a_p")

        blk = es.enter_context(nc.Block())

        @blk.gpsimd
        def _(g):
            g_cst.inc(g.dma_start(out=MASKS[:], in_=masks_dr.ap()), 16)
            g_cst.inc(g.dma_start(out=ONES[:], in_=ones_dr.ap()), 16)
            g_cst.inc(g.dma_start(out=ONESB[:], in_=onesb_dr.ap()), 16)
            g_cst.inc(g.dma_start(out=IDENTB[:], in_=identb_dr.ap()), 16)
            g_cst.inc(g.dma_start(out=IDENT[:], in_=ident_dr.ap()), 16)
            g_cst.inc(g.dma_start(out=EPS[:], in_=eps_dr.ap()), 16)
            g_cst.inc(g.dma_start(out=BQKV[:], in_=bqkv.rearrange("(c p) -> p c", p=128), allow_slow_non_contiguous=True), 16)
            g_cst.inc(g.dma_start(out=BOUT[:], in_=bout.rearrange("(c p) -> p c", p=128), allow_slow_non_contiguous=True), 16)

        CST = g_cst.v

        def fence():
            snaps = [(c.sem, c.v) for c in
                     (t_p, v_p, a_p, g_sp, g_sp0, g_sp1, g_rows, g_out, g_cst,
                      s_x0, s_x1, s_pd, s_wq, s_kv, s_qt0, s_qt1, s_wo, s_w0,
                      s_w1, s_d0, s_d1)]

            def emit(eng):
                for sem, val in snaps:
                    if val:
                        eng.wait_ge(sem, val)

            @blk.tensor
            def _(t):
                emit(t)

            @blk.vector
            def _(v):
                emit(v)

            @blk.scalar
            def _(s):
                emit(s)

            @blk.sync
            def _(sy):
                emit(sy)

            @blk.gpsimd
            def _(g):
                emit(g)

        def emit_ln_stats(src, ntt, row_base, SUMS, SQ, XCH, X2):
            t0, v0, a0, gr0 = t_p.v, v_p.v, a_p.v, g_rows.v
            slots = [s_x0, s_x1]
            ld_vals = []

            def x2_idx(n):
                return 20 * (n // NDCH) + (n % NDCH) + 1

            @blk.sync
            def _(sy):
                for tt in range(ntt):
                    for dch in range(NDCH):
                        n = tt * NDCH + dch
                        b = n % 2
                        if n >= 2:
                            sy.wait_ge(t_p.sem, t0 + 2 * (n - 1))
                            sy.wait_ge(v_p.sem, v0 + x2_idx(n - 2))
                        else:
                            sy.wait_ge(t_p.sem, t0)
                            sy.wait_ge(v_p.sem, v0)
                        ld_vals.append(slots[b].inc(sy.dma_start(
                            out=XCH[:, b * TT:(b + 1) * TT],
                            in_=src[dch * 128:(dch + 1) * 128, tt * TT:(tt + 1) * TT]), 16))

            @blk.vector
            def _(v):
                for tt in range(ntt):
                    for dch in range(NDCH):
                        n = tt * NDCH + dch
                        b = n % 2
                        v.wait_ge(slots[b].sem, ld_vals[n])
                        if n >= 2:
                            v.wait_ge(t_p.sem, t0 + 2 * (n - 1))
                        v_p.inc(v.tensor_tensor(
                            out=X2[:, b * TT:(b + 1) * TT],
                            in0=XCH[:, b * TT:(b + 1) * TT],
                            in1=XCH[:, b * TT:(b + 1) * TT], op=ALU.mult))
                    v.wait_ge(a_p.sem, a0 + 2 * tt + 1)
                    v.wait_ge(v_p.sem, v0 + 20 * tt + 16)
                    v_p.inc(v.tensor_tensor(out=TMP[0:1, TT:2 * TT], in0=TMP[0:1, 0:TT],
                                            in1=TMP[0:1, 0:TT], op=ALU.mult))
                    v.wait_ge(v_p.sem, v0 + 20 * tt + 17)
                    v.wait_ge(t_p.sem, t0 + 2 * NDCH * (tt + 1))
                    v_p.inc(v.scalar_tensor_tensor(
                        out=TMP[0:1, TT:2 * TT], in0=SQ[0:1, :], scalar=1.0 / D,
                        in1=TMP[0:1, TT:2 * TT], op0=ALU.mult, op1=ALU.subtract))
                    v.wait_ge(a_p.sem, a0 + 2 * tt + 2)
                    v.wait_ge(v_p.sem, v0 + 20 * tt + 18)
                    v.wait_ge(g_rows.sem, gr0 + 32 * tt)
                    v_p.inc(v.reciprocal(TMPR[0:1, :], TMP[0:1, TT:2 * TT]))
                    v.wait_ge(v_p.sem, v0 + 20 * tt + 19)
                    v_p.inc(v.tensor_tensor(out=TMPM[0:1, :], in0=TMP[0:1, 0:TT],
                                            in1=TMPR[0:1, :], op=ALU.mult))

            @blk.gpsimd
            def _(g):
                for tt in range(ntt):
                    g.wait_ge(v_p.sem, v0 + 20 * tt + 20)
                    g_rows.inc(g.dma_start(out=rows_dr[0, row_base + tt, :],
                                           in_=TMPR[0:1, :]), 16)
                    g_rows.inc(g.dma_start(out=rows_dr[1, row_base + tt, :],
                                           in_=TMPM[0:1, :]), 16)

            @blk.tensor
            def _(t):
                t.wait_ge(g_cst.sem, CST)
                t.wait_ge(v_p.sem, v0)
                t.wait_ge(a_p.sem, a0)
                for tt in range(ntt):
                    if tt > 0:
                        t.wait_ge(a_p.sem, a0 + 2 * tt - 1)
                        t.wait_ge(v_p.sem, v0 + 20 * tt - 2)
                    for dch in range(NDCH):
                        n = tt * NDCH + dch
                        b = n % 2
                        t.wait_ge(slots[b].sem, ld_vals[n])
                        t_p.inc(t.matmul(SUMS[:], ONES[:], XCH[:, b * TT:(b + 1) * TT],
                                         start=(dch == 0), stop=(dch == NDCH - 1)))
                        t.wait_ge(v_p.sem, v0 + x2_idx(n))
                        t_p.inc(t.matmul(SQ[:], ONES[:], X2[:, b * TT:(b + 1) * TT],
                                         start=(dch == 0), stop=(dch == NDCH - 1)))

            @blk.scalar
            def _(sc_):
                sc_.wait_ge(g_cst.sem, CST)
                for tt in range(ntt):
                    sc_.wait_ge(t_p.sem, t0 + 2 * NDCH * (tt + 1))
                    sc_.wait_ge(v_p.sem, v0 + 20 * tt)
                    a_p.inc(sc_.activation(out=TMP[0:1, 0:TT], in_=SUMS[0:1, :],
                                           func=AF.Identity, scale=1.0 / D))
                    sc_.wait_ge(v_p.sem, v0 + 20 * tt + 18)
                    a_p.inc(sc_.activation(out=TMP[0:1, TT:2 * TT], in_=TMP[0:1, TT:2 * TT],
                                           func=AF.Sqrt, bias=EPS[0:1, 0:1]))

        def emit_ln_tile(src, tt, row_slot, rows_g, XCH, T1, XN2, XNH, XNL, RBS, MRBS,
                         RP1, RP2, AUX1, AUX2,
                         pads_free_t, pads_free_v, first_waits, prev, xn_consumer):
            @blk.sync
            def _(sy):
                sy.wait_ge(g_rows.sem, rows_g)
                sy.wait_ge(v_p.sem, pads_free_v)
                if pads_free_t:
                    sy.wait_ge(t_p.sem, pads_free_t)
                s_pd.inc(sy.dma_start(out=RP1[0:1, :], in_=rows_dr[0, row_slot, :]), 16)
                s_pd.inc(sy.dma_start(out=RP2[0:1, :], in_=rows_dr[1, row_slot, :]), 16)

            pad_ld = s_pd.v
            a_at = a_p.v

            @blk.tensor
            def _(t):
                t.wait_ge(s_pd.sem, pad_ld)
                t.wait_ge(v_p.sem, pads_free_v)
                t.wait_ge(a_p.sem, a_at)
                t_p.inc(t.matmul(AUX1[:], ONES[:], RP1[:], start=True, stop=True))
                t_p.inc(t.matmul(AUX2[:], ONES[:], RP2[:], start=True, stop=True))

            t_bc = t_p.v

            @blk.vector
            def _(v):
                v.wait_ge(t_p.sem, t_bc)
                v.wait_ge(v_p.sem, v_p.v)
                v_p.inc(v.tensor_copy(RBS[:], AUX1[:]))
                v_p.inc(v.tensor_copy(MRBS[:], AUX2[:]))

            v_bc = v_p.v
            xn_done = []
            free_marks = []
            ld_vals = []
            slots = [s_x0, s_x1]

            for dch in range(NDCH):
                b = dch % 2

                @blk.sync
                def _(sy, dch=dch, b=b):
                    if dch >= 2:
                        sy.wait_ge(v_p.sem, xn_done[dch - 2])
                    else:
                        for sem, val in first_waits:
                            sy.wait_ge(sem, val)
                        if prev is not None:
                            sy.wait_ge(v_p.sem, prev[0][NDCH - 2 + dch])
                    ld_vals.append(slots[b].inc(sy.dma_start(
                        out=XCH[:, b * TT:(b + 1) * TT],
                        in_=src[dch * 128:(dch + 1) * 128, tt * TT:(tt + 1) * TT]), 16))

                @blk.vector
                def _(v, dch=dch, b=b):
                    v.wait_ge(slots[b].sem, ld_vals[dch])
                    v.wait_ge(v_p.sem, v_bc)
                    if dch >= 2:
                        sem, val = free_marks[dch - 2]
                        v.wait_ge(sem, val)
                        v.wait_ge(v_p.sem, xn_done[dch - 2])
                    elif prev is not None:
                        sem, val = prev[1][NDCH - 2 + dch]
                        v.wait_ge(sem, val)
                        v.wait_ge(v_p.sem, prev[0][NDCH - 2 + dch])
                    v_p.inc(v.tensor_tensor(out=T1[:, b * TT:(b + 1) * TT],
                                            in0=XCH[:, b * TT:(b + 1) * TT],
                                            in1=RBS[:], op=ALU.mult))
                    v.wait_ge(v_p.sem, v_p.v)
                    v_p.inc(v.tensor_tensor(out=XN2[:, b * TT:(b + 1) * TT],
                                            in0=T1[:, b * TT:(b + 1) * TT],
                                            in1=MRBS[:], op=ALU.subtract))
                    if XNH is not None:
                        v.wait_ge(v_p.sem, v_p.v)
                        v_p.inc(v.tensor_copy(XNH[:, b * TT:(b + 1) * TT],
                                              XN2[:, b * TT:(b + 1) * TT]))
                        v.wait_ge(v_p.sem, v_p.v)
                        v_p.inc(v.tensor_tensor(out=XNL[:, b * TT:(b + 1) * TT],
                                                in0=XN2[:, b * TT:(b + 1) * TT],
                                                in1=XNH[:, b * TT:(b + 1) * TT],
                                                op=ALU.subtract))

                xn_done.append(v_p.v)
                free_marks.append(xn_consumer(dch, b, v_p.v))

            return xn_done, free_marks

        XF8 = es.enter_context(nc.sbuf_tensor("XF8", [128, NDCH * TOWN], F8))
        XF8V = [0]

        with contextlib.ExitStack() as ea:
            TMP = ea.enter_context(nc.sbuf_tensor("TMP", [1, 2 * TT], F32))
            TMPR = ea.enter_context(nc.sbuf_tensor("TMPR", [1, TT], F32))
            TMPM = ea.enter_context(nc.sbuf_tensor("TMPM", [1, TT], F32))
            WQ = ea.enter_context(nc.sbuf_tensor("WQ", [128, 6 * NDCH * 128], F32))
            WQH = ea.enter_context(nc.sbuf_tensor("WQH", [128, 6 * NDCH * 128], BF16))
            WQL = ea.enter_context(nc.sbuf_tensor("WQL", [128, 6 * NDCH * 128], BF16))
            XCH = ea.enter_context(nc.sbuf_tensor("XCH", [128, 2 * TT], F32))
            X2 = ea.enter_context(nc.sbuf_tensor("X2", [128, 2 * TT], F32))
            T1 = ea.enter_context(nc.sbuf_tensor("T1", [128, 2 * TT], F32))
            XN2 = ea.enter_context(nc.sbuf_tensor("XN2", [128, 2 * TT], F32))
            XNH = ea.enter_context(nc.sbuf_tensor("XNH", [128, 2 * TT], BF16))
            XNL = ea.enter_context(nc.sbuf_tensor("XNL", [128, 2 * TT], BF16))
            RBS = ea.enter_context(nc.sbuf_tensor("RBS", [128, TT], F32))
            MRBS = ea.enter_context(nc.sbuf_tensor("MRBS", [128, TT], F32))
            RP1 = ea.enter_context(nc.sbuf_tensor("RP1", [128, TT], F32))
            RP2 = ea.enter_context(nc.sbuf_tensor("RP2", [128, TT], F32))
            EV = ea.enter_context(nc.sbuf_tensor("EV", [128, 6 * TT], F32))
            EVH = ea.enter_context(nc.sbuf_tensor("EVH", [128, 6 * TT], BF16))
            EVL = ea.enter_context(nc.sbuf_tensor("EVL", [128, 6 * TT], BF16))
            VTT = ea.enter_context(nc.sbuf_tensor("VTT", [128, 4 * TT], BF16))
            KSQ = ea.enter_context(nc.sbuf_tensor("KSQ", [128, TT], F32))

            @blk.sync
            def _(sy):
                src = wqkvT.rearrange("(dch p) (ech j) -> p ech dch j", p=128, j=128)
                s_wq.inc(sy.dma_start(out=WQ.ap().rearrange(
                    "p (ech dch j) -> p ech dch j", ech=6, dch=NDCH), in_=src), 16)

            WQ_LD = s_wq.v

            @blk.vector
            def _(v):
                v.wait_ge(s_wq.sem, WQ_LD)
                v.wait_ge(v_p.sem, v_p.v)
                v_p.inc(v.tensor_copy(WQH[:], WQ[:]))
                v.wait_ge(v_p.sem, v_p.v)
                v_p.inc(v.tensor_tensor(out=WQL[:], in0=WQ[:], in1=WQH[:],
                                        op=ALU.subtract))

            WQHL_V = v_p.v

            with contextlib.ExitStack() as ea1:
                SUMS = ea1.enter_context(nc.psum_tensor("SUMS", [128, TT], F32))
                SQ = ea1.enter_context(nc.psum_tensor("SQ", [128, TT], F32))
                emit_ln_stats(xf, NTT, 0, SUMS, SQ, XCH, X2)
                emit_ln_stats(xo, 2, 16, SUMS, SQ, XCH, X2)

            STATS_T, STATS_V = t_p.v, v_p.v
            ROWS_A = g_rows.v

            QKV = ea.enter_context(nc.psum_tensor("QKV", [128, 6 * TT], F32))
            AUX1 = ea.enter_context(nc.psum_tensor("AUX1", [128, TT], F32))
            AUX2 = ea.enter_context(nc.psum_tensor("AUX2", [128, TT], F32))

            @blk.vector
            def _(v):
                v.wait_ge(v_p.sem, v_p.v)
                v_p.inc(v.memset(RP1[:], 0.0))
                v_p.inc(v.memset(RP2[:], 0.0))

            pads_free_t = 0
            pads_free_v = v_p.v
            first_waits = [(v_p.sem, STATS_V), (t_p.sem, STATS_T)]
            prevA = None
            ev_free = []
            vdr_g = 0

            for tt in range(NTT):
                def qkv_consume(dch, b, xn_v, a_pre=a_p.v):
                    @blk.tensor
                    def _(t, dch=dch, b=b, xn_v=xn_v, a_pre=a_pre):
                        t.wait_ge(v_p.sem, max(xn_v, WQHL_V))
                        if dch == 0:
                            t.wait_ge(a_p.sem, a_pre)
                        bs = b * TT
                        for ech in range(6):
                            w0 = (ech * NDCH + dch) * 128
                            t_p.inc(t.matmul(
                                QKV[:, ech * TT:(ech + 1) * TT],
                                WQH[:, w0:w0 + 128], XNH[:, bs:bs + TT],
                                start=(dch == 0), stop=False))
                            t_p.inc(t.matmul(
                                QKV[:, ech * TT:(ech + 1) * TT],
                                WQH[:, w0:w0 + 128], XNL[:, bs:bs + TT],
                                start=False, stop=False))
                            t_p.inc(t.matmul(
                                QKV[:, ech * TT:(ech + 1) * TT],
                                WQL[:, w0:w0 + 128], XNH[:, bs:bs + TT],
                                start=False, stop=(dch == NDCH - 1)))
                    return (t_p.sem, t_p.v)

                prevA = emit_ln_tile(xf, tt, tt, ROWS_A, XCH, T1, XN2, XNH, XNL, RBS, MRBS,
                                     RP1, RP2, AUX1, AUX2, pads_free_t, pads_free_v,
                                     first_waits, prevA, qkv_consume)
                first_waits = []
                t_qkv_done = t_p.v

                @blk.scalar
                def _(sc_, t_qkv_done=t_qkv_done, ev_free=list(ev_free)):
                    sc_.wait_ge(t_p.sem, t_qkv_done)
                    sc_.wait_ge(g_cst.sem, CST)
                    for sem, val in ev_free:
                        sc_.wait_ge(sem, val)
                    for ech in range(6):
                        a_p.inc(sc_.activation(out=EV[:, ech * TT:(ech + 1) * TT],
                                               in_=QKV[:, ech * TT:(ech + 1) * TT],
                                               func=AF.Identity, bias=BQKV[:, ech:ech + 1],
                                               scale=1.0))

                a_ev = a_p.v

                @blk.vector
                def _(v, a_ev=a_ev, gprev=g_sp.v, tprev=t_p.v):
                    v.wait_ge(a_p.sem, a_ev)
                    v.wait_ge(v_p.sem, v_p.v)
                    v.wait_ge(g_sp.sem, gprev)
                    v.wait_ge(t_p.sem, tprev)
                    v_p.inc(v.tensor_copy(EVH[:], EV[:]))
                    v.wait_ge(v_p.sem, v_p.v)
                    v_p.inc(v.tensor_tensor(out=EVL[:], in0=EV[:], in1=EVH[:],
                                            op=ALU.subtract))

                ev_hl = v_p.v

                @blk.gpsimd
                def _(g, tt=tt, ev_hl=ev_hl):
                    g.wait_ge(v_p.sem, ev_hl)
                    for h in range(2):
                        for lv, EVX in ((0, EVH), (1, EVL)):
                            g_sp.inc(g.dma_start(
                                out=qdr[h, lv, :, tt * TT:(tt + 1) * TT],
                                in_=EVX[:, h * TT:(h + 1) * TT]), 16)
                            g_sp.inc(g.dma_start(
                                out=kdr[h, lv, :, tt * TT:(tt + 1) * TT],
                                in_=EVX[:, (2 + h) * TT:(3 + h) * TT]), 16)

                for i, (ech, slot) in enumerate(
                        [(0, 0), (1, 16), (2, 32), (3, 48)]):
                    @blk.vector
                    def _(v, ech=ech, a_ev=a_ev, tprev=t_p.v, vprev=v_p.v):
                        v.wait_ge(a_p.sem, a_ev)
                        v.wait_ge(t_p.sem, tprev)
                        v.wait_ge(v_p.sem, vprev)
                        v_p.inc(v.tensor_tensor(out=KSQ[:],
                                                in0=EV[:, ech * TT:(ech + 1) * TT],
                                                in1=EV[:, ech * TT:(ech + 1) * TT],
                                                op=ALU.mult))

                    sq_v = v_p.v

                    @blk.tensor
                    def _(t, sq_v=sq_v, vfree=v_p.v):
                        t.wait_ge(v_p.sem, sq_v)
                        t_p.inc(t.matmul(AUX1[:], ONES[:], KSQ[:], start=True, stop=True))

                    mm_t = t_p.v

                    @blk.vector
                    def _(v, mm_t=mm_t, slot=slot, tt=tt):
                        v.wait_ge(t_p.sem, mm_t)
                        v.wait_ge(v_p.sem, v_p.v)
                        v_p.inc(v.reduce_max(NRM[0:1, slot + tt:slot + tt + 1],
                                             AUX1[0:1, :], axis=AX.X))

                for h in range(2):
                    @blk.tensor
                    def _(t, h=h, a_ev=a_ev, vfree=v_p.v):
                        t.wait_ge(a_p.sem, a_ev)
                        t.wait_ge(v_p.sem, vfree)
                        for i in range(4):
                            t_p.inc(t.transpose(
                                AUX2[:, i * 128:(i + 1) * 128],
                                EV[:, (4 + h) * TT + i * 128:(4 + h) * TT + (i + 1) * 128],
                                IDENT[:]))

                    tp_t = t_p.v

                    @blk.vector
                    def _(v, h=h, tp_t=tp_t, vdr_g=vdr_g):
                        v.wait_ge(t_p.sem, tp_t)
                        v.wait_ge(v_p.sem, v_p.v)
                        v.wait_ge(g_sp.sem, vdr_g)
                        hi = slice((2 * h) * TT, (2 * h + 1) * TT)
                        lo = slice((2 * h + 1) * TT, (2 * h + 2) * TT)
                        v_p.inc(v.tensor_copy(VTT[:, hi], AUX2[:]))
                        v.wait_ge(v_p.sem, v_p.v)
                        v_p.inc(v.tensor_tensor(out=VTT[:, lo], in0=AUX2[:],
                                                in1=VTT[:, hi], op=ALU.subtract))

                vtt_v = v_p.v

                @blk.gpsimd
                def _(g, tt=tt, vtt_v=vtt_v):
                    g.wait_ge(v_p.sem, vtt_v)
                    for hl in range(4):
                        h, lv = hl // 2, hl % 2
                        for i in range(4):
                            tk = tt * TT + i * 128
                            g_sp.inc(g.dma_start(
                                out=vdr[h, lv, tk:tk + 128, :],
                                in_=VTT[:, hl * TT + i * 128:hl * TT + (i + 1) * 128]), 16)

                vdr_g = g_sp.v
                pads_free_t = t_p.v
                pads_free_v = v_p.v
                ev_free = [(v_p.sem, v_p.v), (t_p.sem, t_p.v), (g_sp.sem, g_sp.v)]

            for tt in range(2):
                def f8_consume(dch, b, xn_v, tt=tt):
                    @blk.vector
                    def _(v, dch=dch, b=b, tt=tt):
                        v.wait_ge(v_p.sem, v_p.v)
                        v_p.inc(v.tensor_copy(
                            XF8[:, dch * TOWN + tt * TT:dch * TOWN + (tt + 1) * TT],
                            XN2[:, b * TT:(b + 1) * TT]))
                    return (v_p.sem, v_p.v)

                prevA = emit_ln_tile(xo, tt, 16 + tt, ROWS_A, XCH, T1, XN2,
                                     None, None, RBS, MRBS, RP1, RP2, AUX1, AUX2,
                                     pads_free_t, pads_free_v, [], prevA, f8_consume)
                pads_free_t = t_p.v
                pads_free_v = v_p.v

            XF8V[0] = v_p.v

        A_T, A_V, A_A, A_G = t_p.v, v_p.v, a_p.v, g_sp.v

        with contextlib.ExitStack() as eb0:
            BBPS = eb0.enter_context(nc.psum_tensor("BBPS", [128, TT], F32))

            @blk.vector
            def _(v):
                v.wait_ge(v_p.sem, A_V)
                v.wait_ge(g_cst.sem, CST)
                v_p.inc(v.memset(BBPAD[:], 0.0))
                for h in range(2):
                    v.wait_ge(v_p.sem, v_p.v)
                    v.wait_ge(a_p.sem, A_A + 2 * h)
                    v_p.inc(v.tensor_copy(BROW[0:1, 16:32], NRM[0:1, 32 + h * 16:48 + h * 16]))
                    for qt in range(1, NTT):
                        v.wait_ge(v_p.sem, v_p.v)
                        v_p.inc(v.tensor_tensor(out=BROW[0:1, 16 + qt:17 + qt],
                                                in0=BROW[0:1, 16 + qt:17 + qt],
                                                in1=BROW[0:1, 15 + qt:16 + qt], op=ALU.max))
                    v.wait_ge(v_p.sem, v_p.v)
                    v_p.inc(v.tensor_tensor(out=BROW[0:1, 32:48],
                                            in0=NRM[0:1, h * 16:(h + 1) * 16],
                                            in1=BROW[0:1, 16:32], op=ALU.mult))
                    v.wait_ge(a_p.sem, A_A + 2 * (h + 1))
                    v_p.inc(v.tensor_copy(BBPAD[0:1, h * 16:(h + 1) * 16],
                                          BROW[0:1, 32:48]))

            @blk.scalar
            def _(sc_):
                vv = A_V + 1
                for h in range(2):
                    vv += 17
                    sc_.wait_ge(v_p.sem, vv)
                    a_p.inc(sc_.activation(out=BROW[0:1, 32:48], in_=BROW[0:1, 32:48],
                                           func=AF.Sqrt))
                    sc_.wait_ge(a_p.sem, a_p.v)
                    a_p.inc(sc_.activation(out=BROW[0:1, 32:48], in_=BROW[0:1, 32:48],
                                           func=AF.Identity, scale=-SC))
                    vv += 1

            @blk.tensor
            def _(t, a_at=a_p.v):
                t.wait_ge(v_p.sem, v_p.v)
                t.wait_ge(a_p.sem, a_at)
                t_p.inc(t.matmul(BBPS[:, 0:32], ONES[:], BBPAD[:, 0:32],
                                 start=True, stop=True))

            @blk.vector
            def _(v):
                v.wait_ge(t_p.sem, t_p.v)
                v.wait_ge(v_p.sem, v_p.v)
                v_p.inc(v.tensor_copy(BB[:], BBPS[:, 0:32]))

        BB_V = v_p.v
        fence()

        with contextlib.ExitStack() as eb:
            KHH = eb.enter_context(nc.sbuf_tensor("KHH", [128, S], BF16))
            KHL = eb.enter_context(nc.sbuf_tensor("KHL", [128, S], BF16))
            VHH = eb.enter_context(nc.sbuf_tensor("VHH", [128, S], BF16))
            VHL = eb.enter_context(nc.sbuf_tensor("VHL", [128, S], BF16))
            QTH = eb.enter_context(nc.sbuf_tensor("QTH", [128, 2 * TT], BF16))
            QTL = eb.enter_context(nc.sbuf_tensor("QTL", [128, 2 * TT], BF16))
            P = eb.enter_context(nc.sbuf_tensor("P", [128, 2 * TT], F32))
            PH = eb.enter_context(nc.sbuf_tensor("PH", [128, 2 * TT], BF16))
            PL = eb.enter_context(nc.sbuf_tensor("PL", [128, 2 * TT], BF16))
            RBN = eb.enter_context(nc.sbuf_tensor("RBN", [128, TT], F32))
            AO = eb.enter_context(nc.sbuf_tensor("AO", [128, 2 * NTT * TT], F32))
            AOH = eb.enter_context(nc.sbuf_tensor("AOH", [128, 2 * TT], BF16))
            AOL = eb.enter_context(nc.sbuf_tensor("AOL", [128, 2 * TT], BF16))
            WO = eb.enter_context(nc.sbuf_tensor("WO", [128, NDCH * 128], F32))
            WOH = eb.enter_context(nc.sbuf_tensor("WOH", [128, 2 * NDCH * 128], BF16))
            WOL = eb.enter_context(nc.sbuf_tensor("WOL", [128, 2 * NDCH * 128], BF16))
            SP = eb.enter_context(nc.sbuf_tensor("SP", [128, 2 * TT], F32))
            STP = eb.enter_context(nc.psum_tensor("STP", [128, 2 * TT], F32))
            AVP = eb.enter_context(nc.psum_tensor("AVP", [128, TT], F32))
            SMP = eb.enter_context(nc.psum_tensor("SMP", [128, TT], F32))
            OPP = eb.enter_context(nc.psum_tensor("OPP", [128, 2 * TT], F32))

            wsrc = woutT.rearrange("(dch p) (ech j) -> p dch ech j", p=128, j=128)
            for dch in range(2):
                @blk.sync
                def _(sy, dch=dch, vprev=v_p.v):
                    if dch == 1:
                        sy.wait_ge(v_p.sem, vprev)
                    s_wo.inc(sy.dma_start(out=WO.ap().rearrange(
                        "p (ech j) -> p ech j", ech=NDCH), in_=wsrc[:, dch, :, :]), 16)

                wo_ld = s_wo.v

                @blk.vector
                def _(v, dch=dch, wo_ld=wo_ld):
                    v.wait_ge(s_wo.sem, wo_ld)
                    v.wait_ge(v_p.sem, v_p.v)
                    half = slice(dch * NDCH * 128, (dch + 1) * NDCH * 128)
                    v_p.inc(v.tensor_copy(WOH[:, half], WO[:]))
                    v.wait_ge(v_p.sem, v_p.v)
                    v_p.inc(v.tensor_tensor(out=WOL[:, half], in0=WO[:],
                                            in1=WOH[:, half], op=ALU.subtract))

            WOHL_V = v_p.v
            ao_done_v = max(BB_V, WOHL_V)
            AB_A = a_p.v
            prev_pl_tail = []
            prev_ae_tail = []
            prev_tsum_tail = []
            qt_last_s = {}

            for h in range(2):
                @blk.sync
                def _(sy, h=h, t_all=t_p.v):
                    sy.wait_ge(g_sp.sem, A_G)
                    if h == 1:
                        sy.wait_ge(t_p.sem, t_all)
                    for lv, KHX in ((0, KHH), (1, KHL)):
                        for i in range(2):
                            s_kv.inc(sy.dma_start(
                                out=KHX[:, i * 4096:(i + 1) * 4096],
                                in_=kdr[h, lv, :, i * 4096:(i + 1) * 4096]), 16)
                    vsrc = vdr.rearrange("h lv (tb p) j -> h lv p tb j", p=128)
                    for lv, VHX in ((0, VHH), (1, VHL)):
                        for i in range(2):
                            s_kv.inc(sy.dma_start(
                                out=VHX.ap().rearrange("p (tb j) -> p tb j", j=128)[
                                    :, i * 32:(i + 1) * 32, :],
                                in_=vsrc[h, lv, :, i * 32:(i + 1) * 32, :]), 16)

                KV_LD = s_kv.v

                for qt in range(NTT):
                    nkt = (qt + 1) * 4
                    qb = qt % 2
                    gk = (h, qt)
                    qsem = s_qt0 if qb == 0 else s_qt1

                    @blk.sync
                    def _(sy, h=h, qt=qt, qb=qb, qsem=qsem):
                        key = (h, qt - 2)
                        if key in qt_last_s:
                            sy.wait_ge(t_p.sem, qt_last_s[key])
                        qsem.inc(sy.dma_start(out=QTH[:, qb * TT:(qb + 1) * TT],
                                              in_=qdr[h, 0, :, qt * TT:(qt + 1) * TT]), 16)
                        qsem.inc(sy.dma_start(out=QTL[:, qb * TT:(qb + 1) * TT],
                                              in_=qdr[h, 1, :, qt * TT:(qt + 1) * TT]), 16)

                    q_ld = qsem.v

                    T0, A0, V0 = t_p.v, a_p.v, v_p.v
                    ts_vals = [0] * nkt
                    tsum_vals = [0] * nkt
                    ae_vals = [A0 + kt + 1 for kt in range(nkt)]
                    pl_vals = [0] * nkt
                    tv = T0
                    for kt in range(nkt):
                        tv += 3
                        ts_vals[kt] = tv
                        if kt >= 1:
                            tv += 5
                            tsum_vals[kt - 1] = tv
                    tv += 5
                    tsum_vals[nkt - 1] = tv
                    vv = V0
                    for kt in range(nkt):
                        if kt >= nkt - 4:
                            vv += 1
                        vv += 2
                        pl_vals[kt] = vv
                    qt_last_s[gk] = ts_vals[nkt - 1]

                    @blk.tensor
                    def _(t, h=h, qt=qt, qb=qb, nkt=nkt, q_ld=q_ld, KV_LD=KV_LD,
                          qsem=qsem, av0=ao_done_v, pae=list(prev_ae_tail)):
                        t.wait_ge(qsem.sem, q_ld)
                        t.wait_ge(s_kv.sem, KV_LD)
                        qs = slice(qb * TT, (qb + 1) * TT)

                        def emit_s(kt):
                            if kt >= 2:
                                t.wait_ge(a_p.sem, ae_vals[kt - 2])
                            elif len(pae) == 2:
                                t.wait_ge(a_p.sem, pae[kt])
                            else:
                                t.wait_ge(a_p.sem, AB_A)
                            ks = slice(kt * 128, (kt + 1) * 128)
                            pb = kt % 2
                            t_p.inc(t.matmul(STP[:, pb * TT:(pb + 1) * TT],
                                             KHH[:, ks], QTH[:, qs], start=True, stop=False))
                            t_p.inc(t.matmul(STP[:, pb * TT:(pb + 1) * TT],
                                             KHH[:, ks], QTL[:, qs], start=False, stop=False))
                            t_p.inc(t.matmul(STP[:, pb * TT:(pb + 1) * TT],
                                             KHL[:, ks], QTH[:, qs], start=False, stop=True))

                        def emit_av(kt):
                            t.wait_ge(v_p.sem, pl_vals[kt])
                            if kt == 0:
                                t.wait_ge(v_p.sem, av0)
                            ks = slice(kt * 128, (kt + 1) * 128)
                            pb = kt % 2
                            ps = slice(pb * TT, (pb + 1) * TT)
                            t_p.inc(t.matmul(AVP[:], VHH[:, ks], PH[:, ps],
                                             start=(kt == 0), stop=False))
                            t_p.inc(t.matmul(AVP[:], VHH[:, ks], PL[:, ps],
                                             start=False, stop=False))
                            t_p.inc(t.matmul(AVP[:], VHL[:, ks], PH[:, ps],
                                             start=False, stop=(kt == nkt - 1)))
                            t_p.inc(t.matmul(SMP[:], ONESB[:], PH[:, ps],
                                             start=(kt == 0), stop=False))
                            t_p.inc(t.matmul(SMP[:], ONESB[:], PL[:, ps],
                                             start=False, stop=(kt == nkt - 1)))

                        for kt in range(nkt):
                            emit_s(kt)
                            if kt >= 1:
                                emit_av(kt - 1)
                        emit_av(nkt - 1)
                        assert t_p.v == tsum_vals[nkt - 1], (t_p.v, tsum_vals[nkt - 1])

                    @blk.scalar
                    def _(sc_, h=h, qt=qt, nkt=nkt, ppl=list(prev_pl_tail)):
                        for kt in range(nkt):
                            pb = kt % 2
                            sc_.wait_ge(t_p.sem, ts_vals[kt])
                            sc_.wait_ge(v_p.sem, BB_V)
                            if kt >= 2:
                                sc_.wait_ge(v_p.sem, pl_vals[kt - 2])
                            elif len(ppl) == 2:
                                sc_.wait_ge(v_p.sem, ppl[kt])
                            a_p.inc(sc_.activation(out=P[:, pb * TT:(pb + 1) * TT],
                                                   in_=STP[:, pb * TT:(pb + 1) * TT],
                                                   func=AF.Exp,
                                                   bias=BB[:, h * 16 + qt:h * 16 + qt + 1],
                                                   scale=SC))
                        assert a_p.v == ae_vals[-1]

                    @blk.vector
                    def _(v, h=h, qt=qt, nkt=nkt, ptt=list(prev_tsum_tail)):
                        for kt in range(nkt):
                            pb = kt % 2
                            ps = slice(pb * TT, (pb + 1) * TT)
                            v.wait_ge(a_p.sem, ae_vals[kt])
                            if kt >= nkt - 4:
                                off = kt - (nkt - 4)
                                v.wait_ge(v_p.sem, v_p.v)
                                v_p.inc(v.tensor_tensor(
                                    out=P[:, ps], in0=P[:, ps],
                                    in1=MASKS[:, off * TT:(off + 1) * TT], op=ALU.mult))
                            v.wait_ge(v_p.sem, v_p.v)
                            if kt >= 2:
                                v.wait_ge(t_p.sem, tsum_vals[kt - 2])
                            elif len(ptt) == 2:
                                v.wait_ge(t_p.sem, ptt[kt])
                            v_p.inc(v.tensor_copy(PH[:, ps], P[:, ps]))
                            v.wait_ge(v_p.sem, v_p.v)
                            v_p.inc(v.tensor_tensor(out=PL[:, ps], in0=P[:, ps],
                                                    in1=PH[:, ps], op=ALU.subtract))
                        assert v_p.v == pl_vals[-1]
                        v.wait_ge(t_p.sem, tsum_vals[nkt - 1])
                        v.wait_ge(v_p.sem, v_p.v)
                        v_p.inc(v.reciprocal(RBN[:], SMP[:]))
                        v.wait_ge(v_p.sem, v_p.v)
                        v_p.inc(v.tensor_tensor(
                            out=AO[:, (h * NTT + qt) * TT:(h * NTT + qt + 1) * TT],
                            in0=AVP[:], in1=RBN[:], op=ALU.mult))

                    ao_done_v = v_p.v
                    prev_ae_tail = ae_vals[-2:]
                    prev_tsum_tail = tsum_vals[-2:]
                    prev_pl_tail = pl_vals[-2:]

            AO_DONE = v_p.v
            op_evac_a = {}
            sp_dma_g = {}
            ao_dec_v = {}
            qt_mm_t = {}

            for qt in range(NTT):
                @blk.vector
                def _(v, qt=qt, tprev=qt_mm_t.get(qt - 1, 0)):
                    v.wait_ge(v_p.sem, AO_DONE)
                    v.wait_ge(v_p.sem, v_p.v)
                    if tprev:
                        v.wait_ge(t_p.sem, tprev)
                    for dch in range(2):
                        ds = slice(dch * TT, (dch + 1) * TT)
                        aos = slice((dch * NTT + qt) * TT, (dch * NTT + qt + 1) * TT)
                        v_p.inc(v.tensor_copy(AOH[:, ds], AO[:, aos]))
                        v.wait_ge(v_p.sem, v_p.v)
                        v_p.inc(v.tensor_tensor(out=AOL[:, ds], in0=AO[:, aos],
                                                in1=AOH[:, ds], op=ALU.subtract))

                ao_dec_v[qt] = v_p.v

                for ech in range(NDCH):
                    n = qt * NDCH + ech
                    ob = n % 2

                    @blk.tensor
                    def _(t, qt=qt, ech=ech, ob=ob, n=n, a_at=a_p.v):
                        t.wait_ge(v_p.sem, ao_dec_v[qt])
                        if n >= 2:
                            t.wait_ge(a_p.sem, op_evac_a[n - 2])
                        else:
                            t.wait_ge(a_p.sem, a_at)
                        for dch in range(2):
                            ws = slice((dch * NDCH + ech) * 128, (dch * NDCH + ech + 1) * 128)
                            ds = slice(dch * TT, (dch + 1) * TT)
                            t_p.inc(t.matmul(OPP[:, ob * TT:(ob + 1) * TT],
                                             WOH[:, ws], AOH[:, ds],
                                             start=(dch == 0), stop=False))
                            t_p.inc(t.matmul(OPP[:, ob * TT:(ob + 1) * TT],
                                             WOH[:, ws], AOL[:, ds],
                                             start=False, stop=False))
                            t_p.inc(t.matmul(OPP[:, ob * TT:(ob + 1) * TT],
                                             WOL[:, ws], AOH[:, ds],
                                             start=False, stop=(dch == 1)))

                    mm_t = t_p.v
                    qt_mm_t[qt] = t_p.v

                    @blk.scalar
                    def _(sc_, mm_t=mm_t, n=n, ob=ob):
                        sc_.wait_ge(t_p.sem, mm_t)
                        if n >= 2:
                            sc_.wait_ge(sp_dma_g[n - 2][0].sem, sp_dma_g[n - 2][1])
                        a_p.inc(sc_.activation(out=SP[:, ob * TT:(ob + 1) * TT],
                                               in_=OPP[:, ob * TT:(ob + 1) * TT],
                                               func=AF.Identity))

                    op_evac_a[n] = a_p.v

                    @blk.gpsimd
                    def _(g, qt=qt, ech=ech, ob=ob, n=n):
                        g.wait_ge(a_p.sem, op_evac_a[n])
                        rank = qt if qt < 8 else 15 - qt
                        half = 0 if qt < 8 else 1
                        gsem = g_sp0 if ob == 0 else g_sp1
                        gsem.inc(g.dma_start(
                            out=spart[rank, ech * 128:(ech + 1) * 128,
                                      half * TT:(half + 1) * TT],
                            in_=SP[:, ob * TT:(ob + 1) * TT]), 16)

                    sp_dma_g[n] = (g_sp0 if ob == 0 else g_sp1, (g_sp0 if ob == 0 else g_sp1).v)

            fence()

            @blk.gpsimd
            def _(g, a0=g_sp0.v, b0=g_sp1.v):
                g.wait_ge(g_sp0.sem, a0)
                g.wait_ge(g_sp1.sem, b0)
                cc.inc(g.collective_compute(
                    "ReduceScatter", ALU.add, replica_groups=[list(range(NC))],
                    ins=[spart.opt()], outs=[sred.opt()]))

        G = es.enter_context(nc.sbuf_tensor("G", [128, NDCH * TOWN], BF16))
        with contextlib.ExitStack() as ec:
            WST = ec.enter_context(nc.sbuf_tensor("WST", [128, 4 * 2048], F32))
            W8 = ec.enter_context(nc.sbuf_tensor("W8", [128, 4 * 2048], F8))
            GV = ec.enter_context(nc.sbuf_tensor("GV", [128, 2 * TT], BF16))
            F1 = ec.enter_context(nc.sbuf_tensor("F1", [128, 2 * TT], BF16))
            ACTB = ec.enter_context(nc.sbuf_tensor("ACTB", [128, 2 * TT], BF16))
            ACT8 = ec.enter_context(nc.sbuf_tensor("ACT8", [128, NHCH * TOWN], F8))

            FFP = ec.enter_context(nc.psum_tensor("FFP", [128, 4 * TT], F32))
            XF8_V = XF8V[0]

            st_ld, q8_v, ev_scl, act_v, ff_mm_t = {}, {}, {}, {}, {}
            for hch in range(NHCH):
                sb = hch % 2

                wsem = s_w0 if sb == 0 else s_w1

                @blk.sync
                def _(sy, hch=hch, sb=sb, wsem=wsem):
                    if hch >= 2:
                        sy.wait_ge(v_p.sem, q8_v[hch - 2])
                    for m, w in ((0, wg0), (1, wg1)):
                        src = w.rearrange("(dch p) (hch j) -> p hch dch j", p=128, j=128)
                        wsem.inc(sy.dma_start(
                            out=WST.ap().rearrange("p (s dch j) -> p s dch j", s=4, j=128)[
                                :, 2 * sb + m, :, :],
                            in_=src[:, hch, :, :]), 16)

                st_ld[hch] = wsem.v

                @blk.vector
                def _(v, hch=hch, sb=sb, wsem=wsem):
                    v.wait_ge(wsem.sem, st_ld[hch])
                    v.wait_ge(v_p.sem, v_p.v)
                    if hch >= 2:
                        v.wait_ge(t_p.sem, ff_mm_t[hch - 2])
                    v_p.inc(v.tensor_copy(W8[:, (2 * sb) * 2048:(2 * sb + 2) * 2048],
                                          WST[:, (2 * sb) * 2048:(2 * sb + 2) * 2048]))

                q8_v[hch] = v_p.v

                @blk.tensor
                def _(t, hch=hch, sb=sb, a_at=a_p.v):
                    t.wait_ge(v_p.sem, max(q8_v[hch], XF8_V))
                    if hch >= 1:
                        t.wait_ge(a_p.sem, ev_scl[hch - 1])
                    else:
                        t.wait_ge(a_p.sem, a_at)
                    for m in range(2):
                        for th in range(2):
                            for c2 in range(NDCH // 2):
                                w0 = (2 * sb + m) * 2048 + (2 * c2) * 128
                                lhs = W8[:, w0:w0 + 256].rearrange(
                                    "p (j m2) -> p j m2", j=2)
                                rhs = XF8[:, (2 * c2) * TOWN:(2 * c2 + 2) * TOWN].rearrange(
                                    "p (j t) -> p j t", j=2)[:, :, th * TT:(th + 1) * TT]
                                t_p.inc(t.matmul(
                                    FFP[:, (2 * m + th) * TT:(2 * m + th + 1) * TT],
                                    lhs, rhs,
                                    start=(c2 == 0), stop=(c2 == NDCH // 2 - 1),
                                    perf_mode=mybir.MatmulPerfMode.DoubleRow))

                ff_mm_t[hch] = t_p.v

                @blk.scalar
                def _(sc_, hch=hch):
                    sc_.wait_ge(t_p.sem, ff_mm_t[hch])
                    if hch >= 1:
                        sc_.wait_ge(v_p.sem, act_v[hch - 1])
                    for th in range(2):
                        a_p.inc(sc_.activation(out=GV[:, th * TT:(th + 1) * TT],
                                               in_=FFP[:, th * TT:(th + 1) * TT],
                                               func=AF.Gelu_apprx_tanh))

                ev_scl[hch] = a_p.v

                @blk.vector
                def _(v, hch=hch):
                    v.wait_ge(a_p.sem, ev_scl[hch])
                    v.wait_ge(t_p.sem, ff_mm_t[hch])
                    for th in range(2):
                        v.wait_ge(v_p.sem, v_p.v)
                        v_p.inc(v.tensor_copy(F1[:, th * TT:(th + 1) * TT],
                                              FFP[:, (2 + th) * TT:(3 + th) * TT]))
                        v.wait_ge(v_p.sem, v_p.v)
                        v_p.inc(v.tensor_tensor(out=ACTB[:, th * TT:(th + 1) * TT],
                                                in0=GV[:, th * TT:(th + 1) * TT],
                                                in1=F1[:, th * TT:(th + 1) * TT],
                                                op=ALU.mult))
                        v.wait_ge(v_p.sem, v_p.v)
                        v_p.inc(v.tensor_copy(
                            ACT8[:, hch * TOWN + th * TT:hch * TOWN + (th + 1) * TT],
                            ACTB[:, th * TT:(th + 1) * TT]))

                act_v[hch] = v_p.v

            ACT_V = v_p.v
            FF_LAST_T = ff_mm_t[NHCH - 1]
            FF_LAST_Q = q8_v[NHCH - 1]
            FF_LAST_E = ev_scl[NHCH - 1]

            wl_ld, wl_q8, wl_mm, g_evac = {}, {}, {}, {}
            for ech in range(NDCH):
                for grp in range(4):
                    n = ech * 4 + grp
                    sb = n % 2

                    wsem = s_w0 if sb == 0 else s_w1

                    @blk.sync
                    def _(sy, ech=ech, grp=grp, sb=sb, n=n, wsem=wsem):
                        if n >= 2:
                            sy.wait_ge(v_p.sem, wl_q8[n - 2])
                        else:
                            sy.wait_ge(v_p.sem, FF_LAST_Q)
                        src = wl.rearrange("(hch p) (ech j) -> p hch ech j", p=128, j=128)
                        wsem.inc(sy.dma_start(
                            out=WST.ap().rearrange("p (s dch j) -> p s dch j", s=4, j=128)[
                                :, 2 * sb, :, :],
                            in_=src[:, grp * 16:(grp + 1) * 16, ech, :]), 16)

                    wl_ld[n] = wsem.v

                    @blk.vector
                    def _(v, n=n, sb=sb, wsem=wsem):
                        v.wait_ge(wsem.sem, wl_ld[n])
                        v.wait_ge(v_p.sem, v_p.v)
                        if n >= 2:
                            v.wait_ge(t_p.sem, wl_mm[n - 2])
                        else:
                            v.wait_ge(t_p.sem, FF_LAST_T)
                        v_p.inc(v.tensor_copy(W8[:, (2 * sb) * 2048:(2 * sb) * 2048 + 2048],
                                              WST[:, (2 * sb) * 2048:(2 * sb) * 2048 + 2048]))

                    wl_q8[n] = v_p.v

                    @blk.tensor
                    def _(t, ech=ech, grp=grp, n=n, sb=sb):
                        t.wait_ge(v_p.sem, max(wl_q8[n], ACT_V))
                        if grp == 0:
                            if ech >= 1:
                                t.wait_ge(a_p.sem, g_evac[ech - 1])
                            else:
                                t.wait_ge(a_p.sem, FF_LAST_E)
                        for i2 in range(8):
                            hch = grp * 16 + 2 * i2
                            for th in range(2):
                                w0 = (2 * sb) * 2048 + (2 * i2) * 128
                                lhs = W8[:, w0:w0 + 256].rearrange(
                                    "p (j m2) -> p j m2", j=2)
                                rhs = ACT8[:, hch * TOWN:(hch + 2) * TOWN].rearrange(
                                    "p (j t) -> p j t", j=2)[:, :, th * TT:(th + 1) * TT]
                                t_p.inc(t.matmul(
                                    FFP[:, th * TT:(th + 1) * TT],
                                    lhs, rhs,
                                    start=(hch == 0), stop=(hch == NHCH - 2),
                                    perf_mode=mybir.MatmulPerfMode.DoubleRow))

                    wl_mm[n] = t_p.v

                @blk.scalar
                def _(sc_, ech=ech):
                    sc_.wait_ge(t_p.sem, wl_mm[ech * 4 + 3])
                    for th in range(2):
                        a_p.inc(sc_.activation(
                            out=G[:, ech * TOWN + th * TT:ech * TOWN + (th + 1) * TT],
                            in_=FFP[:, th * TT:(th + 1) * TT], func=AF.Silu))

                g_evac[ech] = a_p.v

        G_A = a_p.v
        fence()

        with contextlib.ExitStack() as ed:
            SR = ed.enter_context(nc.sbuf_tensor("SR", [128, 2 * TT], F32))
            XOT = ed.enter_context(nc.sbuf_tensor("XOT", [128, 2 * TT], F32))
            SBT = ed.enter_context(nc.sbuf_tensor("SBT", [128, 2 * TT], F32))
            OT = ed.enter_context(nc.sbuf_tensor("OT", [128, 2 * TT], F32))

            d_ld, d_sc, d_ot, d_st = {}, {}, {}, {}
            for th in range(2):
                for ech in range(NDCH):
                    n = th * NDCH + ech
                    b = n % 2

                    dsem = s_d0 if b == 0 else s_d1

                    @blk.sync
                    def _(sy, th=th, ech=ech, b=b, n=n, dsem=dsem):
                        sy.wait_ge(cc.sem, 1)
                        if n >= 2:
                            sy.wait_ge(a_p.sem, d_sc[n - 2])
                            sy.wait_ge(v_p.sem, d_ot[n - 2])
                        dsem.inc(sy.dma_start(
                            out=SR[:, b * TT:(b + 1) * TT],
                            in_=sred[ech * 128:(ech + 1) * 128, th * TT:(th + 1) * TT]), 16)
                        dsem.inc(sy.dma_start(
                            out=XOT[:, b * TT:(b + 1) * TT],
                            in_=xo[ech * 128:(ech + 1) * 128, th * TT:(th + 1) * TT]), 16)

                    d_ld[n] = (dsem, dsem.v)

                    @blk.scalar
                    def _(sc_, ech=ech, b=b, n=n):
                        sc_.wait_ge(d_ld[n][0].sem, d_ld[n][1])
                        sc_.wait_ge(a_p.sem, G_A)
                        if n >= 2:
                            sc_.wait_ge(v_p.sem, d_ot[n - 2])
                        a_p.inc(sc_.activation(out=SBT[:, b * TT:(b + 1) * TT],
                                               in_=SR[:, b * TT:(b + 1) * TT],
                                               func=AF.Identity, bias=BOUT[:, ech:ech + 1]))

                    d_sc[n] = a_p.v

                    @blk.vector
                    def _(v, th=th, ech=ech, b=b, n=n):
                        v.wait_ge(a_p.sem, d_sc[n])
                        v.wait_ge(v_p.sem, v_p.v)
                        if n >= 2:
                            gsem, gval = d_st[n - 2]
                            v.wait_ge(gsem.sem, gval)
                        v_p.inc(v.tensor_tensor(
                            out=SBT[:, b * TT:(b + 1) * TT],
                            in0=SBT[:, b * TT:(b + 1) * TT],
                            in1=G[:, ech * TOWN + th * TT:ech * TOWN + (th + 1) * TT],
                            op=ALU.mult))
                        v.wait_ge(v_p.sem, v_p.v)
                        v.wait_ge(d_ld[n][0].sem, d_ld[n][1])
                        v_p.inc(v.tensor_tensor(out=OT[:, b * TT:(b + 1) * TT],
                                                in0=SBT[:, b * TT:(b + 1) * TT],
                                                in1=XOT[:, b * TT:(b + 1) * TT],
                                                op=ALU.add))

                    d_ot[n] = v_p.v

                    gosem = g_o0 if b == 0 else g_o1

                    @blk.gpsimd
                    def _(g, th=th, ech=ech, b=b, n=n, gosem=gosem):
                        g.wait_ge(v_p.sem, d_ot[n])
                        gosem.inc(g.dma_start(
                            out=outp[ech * 128:(ech + 1) * 128, th * TT:(th + 1) * TT],
                            in_=OT[:, b * TT:(b + 1) * TT]), 16)

                    d_st[n] = (gosem, gosem.v)

            @blk.gpsimd
            def _(g, a0=g_o0.v, b0=g_o1.v):
                g.wait_ge(g_o0.sem, a0)
                g.wait_ge(g_o1.sem, b0)

    return nc


_NC_CACHE = [None]


def kernel(L0_attn_x, L0_attn_w_qkv, L0_attn_b_qkv, L0_attn_w_out, L0_attn_b_out,
           L0_ffn_w_gating, L0_ffn_w_linear):
    x = np.asarray(L0_attn_x)[0]
    w_qkv = np.asarray(L0_attn_w_qkv)
    b_qkv = np.asarray(L0_attn_b_qkv)
    w_out = np.asarray(L0_attn_w_out)
    b_out = np.asarray(L0_attn_b_out)
    w_g = np.asarray(L0_ffn_w_gating)
    w_l = np.asarray(L0_ffn_w_linear)

    xf = np.ascontiguousarray(x.T)
    woutT_full = np.ascontiguousarray(w_out.T)
    wg0 = np.ascontiguousarray(w_g[0])
    wg1 = np.ascontiguousarray(w_g[1])
    wl = np.ascontiguousarray(w_l)

    in_maps = []
    for c in range(NC):
        b0, b1 = c, 15 - c
        own = np.concatenate([x[b0 * TT:(b0 + 1) * TT], x[b1 * TT:(b1 + 1) * TT]], axis=0)
        xo_c = np.ascontiguousarray(own.T)
        h0 = 2 * c
        rows = np.concatenate([
            w_qkv[h0 * 128:(h0 + 2) * 128],
            w_qkv[2048 + h0 * 128:2048 + (h0 + 2) * 128],
            w_qkv[4096 + h0 * 128:4096 + (h0 + 2) * 128]], axis=0)
        wqkvT_c = np.ascontiguousarray(rows.T)
        bq_c = np.ascontiguousarray(np.concatenate([
            b_qkv[h0 * 128:(h0 + 2) * 128],
            b_qkv[2048 + h0 * 128:2048 + (h0 + 2) * 128],
            b_qkv[4096 + h0 * 128:4096 + (h0 + 2) * 128]], axis=0))
        woutT_c = np.ascontiguousarray(woutT_full[h0 * 128:(h0 + 2) * 128, :])
        in_maps.append({
            "xf": xf, "xo": xo_c, "wqkvT": wqkvT_c, "bqkv": bq_c,
            "woutT": woutT_c, "bout": b_out, "wg0": wg0, "wg1": wg1, "wl": wl,
        })

    if _NC_CACHE[0] is None:
        _NC_CACHE[0] = build_nc()
    res = run_bass_kernel_spmd(_NC_CACHE[0], in_maps, core_ids=list(range(NC)))

    out = np.empty((1, S, D), dtype=np.float32)
    for c in range(NC):
        o = np.asarray(res.results[c]["outp"]).T
        out[0, c * TT:(c + 1) * TT] = o[:TT]
        out[0, (15 - c) * TT:(16 - c) * TT] = o[TT:]
    return out


# version 08 (final)
# speedup vs baseline: 1.7300x; 1.0231x over previous
import contextlib
import numpy as np
import concourse.bass as bass
import concourse.mybir as mybir
from concourse.bass_utils import run_bass_kernel_spmd

F32 = mybir.dt.float32
BF16 = mybir.dt.bfloat16
F8 = mybir.dt.float8e4
AF = mybir.ActivationFunctionType
ALU = mybir.AluOpType
AX = mybir.AxisListType

D = 2048
S = 8192
NC = 8
TT = 512
NTT = S // TT
TOWN = 1024
HID = 8192
NDCH = D // 128
NHCH = HID // 128
SC = float(1.0 / np.sqrt(128.0))


class Ctr:
    def __init__(self, es, nc, name):
        self.sem = es.enter_context(nc.semaphore(name))
        self.v = 0

    def inc(self, instr, n=1):
        instr.then_inc(self.sem, n)
        self.v += n
        return self.v


def build_nc():
    nc = bass.Bass("TRN2", target_bir_lowering=False, debug=False)

    xf = nc.dram_tensor("xf", [D, S], F32, kind="ExternalInput").ap()
    xo = nc.dram_tensor("xo", [D, TOWN], F32, kind="ExternalInput").ap()
    wqkvT = nc.dram_tensor("wqkvT", [D, 768], F32, kind="ExternalInput").ap()
    bqkv = nc.dram_tensor("bqkv", [768], F32, kind="ExternalInput").ap()
    woutT = nc.dram_tensor("woutT", [256, D], F32, kind="ExternalInput").ap()
    bout = nc.dram_tensor("bout", [D], F32, kind="ExternalInput").ap()
    wg0 = nc.dram_tensor("wg0", [D, HID], F32, kind="ExternalInput").ap()
    wg1 = nc.dram_tensor("wg1", [D, HID], F32, kind="ExternalInput").ap()
    wl = nc.dram_tensor("wl", [HID, D], F32, kind="ExternalInput").ap()
    outp = nc.dram_tensor("outp", [D, TOWN], F32, kind="ExternalOutput").ap()

    qdr = nc.dram_tensor("qdr", [2, 2, 128, S], BF16, kind="Internal").ap()
    kdr = nc.dram_tensor("kdr", [2, 2, 128, S], BF16, kind="Internal").ap()
    vdr = nc.dram_tensor("vdr", [2, 2, S, 128], BF16, kind="Internal").ap()
    spart = nc.dram_tensor("spart", [NC, D, TOWN], F32, kind="Internal").ap()
    rows_dr = nc.dram_tensor("rows_dr", [2, 18, TT], F32, kind="Internal").ap()
    sred = nc.dram_tensor("sred", [D, TOWN], F32, kind="Internal").ap()

    mask_np = np.zeros((128, 4 * TT), dtype=np.float32)
    for off in range(4):
        mask_np[:, off * TT:(off + 1) * TT] = (
            np.arange(128)[:, None] + 128 * off <= np.arange(TT)[None, :])
    masks_dr = nc.inline_tensor(mask_np, name="masks_c")
    ones_dr = nc.inline_tensor(np.ones((128, 128), dtype=np.float32), name="ones_c")
    ident_dr = nc.inline_tensor(np.eye(128, dtype=np.float32), name="ident_c")
    import ml_dtypes as _mld
    onesb_dr = nc.inline_tensor(np.ones((128, 128), dtype=_mld.bfloat16), name="onesb_c")
    identb_dr = nc.inline_tensor(np.eye(128).astype(_mld.bfloat16), name="identb_c")
    eps_dr = nc.inline_tensor(np.full((128, 1), 1e-5, dtype=np.float32), name="eps_c")

    es = contextlib.ExitStack()
    with es:
        MASKS = es.enter_context(nc.sbuf_tensor("MASKS", [128, 4 * TT], F32))
        ONES = es.enter_context(nc.sbuf_tensor("ONES", [128, 128], F32))
        ONESB = es.enter_context(nc.sbuf_tensor("ONESB", [128, 128], BF16))
        IDENTB = es.enter_context(nc.sbuf_tensor("IDENTB", [128, 128], BF16))
        IDENT = es.enter_context(nc.sbuf_tensor("IDENT", [128, 128], F32))
        EPS = es.enter_context(nc.sbuf_tensor("EPS", [128, 1], F32))
        BQKV = es.enter_context(nc.sbuf_tensor("BQKV", [128, 6], F32))
        BOUT = es.enter_context(nc.sbuf_tensor("BOUT", [128, 16], F32))
        NRM = es.enter_context(nc.sbuf_tensor("NRM", [1, 64], F32))
        BB = es.enter_context(nc.sbuf_tensor("BB", [128, 32], F32))
        BROW = es.enter_context(nc.sbuf_tensor("BROW", [1, 64], F32))
        BBPAD = es.enter_context(nc.sbuf_tensor("BBPAD", [128, 32], F32))

        s_ld = Ctr(es, nc, "s_ld")
        s_x0 = Ctr(es, nc, "s_x0")
        s_x1 = Ctr(es, nc, "s_x1")
        s_pd = Ctr(es, nc, "s_pd")
        s_wq = Ctr(es, nc, "s_wq")
        s_kv = Ctr(es, nc, "s_kv")
        s_qt0 = Ctr(es, nc, "s_qt0")
        s_qt1 = Ctr(es, nc, "s_qt1")
        s_wo = Ctr(es, nc, "s_wo")
        s_w0 = Ctr(es, nc, "s_w0")
        s_w1 = Ctr(es, nc, "s_w1")
        s_d0 = Ctr(es, nc, "s_d0")
        s_d1 = Ctr(es, nc, "s_d1")
        g_sp0 = Ctr(es, nc, "g_sp0")
        g_sp1 = Ctr(es, nc, "g_sp1")
        g_o0 = Ctr(es, nc, "g_o0")
        g_o1 = Ctr(es, nc, "g_o1")
        g_cst = Ctr(es, nc, "g_cst")
        g_sp = Ctr(es, nc, "g_sp")
        g_rows = Ctr(es, nc, "g_rows")
        g_out = Ctr(es, nc, "g_out")
        cc = Ctr(es, nc, "cc")
        t_p = Ctr(es, nc, "t_p")
        v_p = Ctr(es, nc, "v_p")
        a_p = Ctr(es, nc, "a_p")

        blk = es.enter_context(nc.Block())

        @blk.gpsimd
        def _(g):
            g_cst.inc(g.dma_start(out=MASKS[:], in_=masks_dr.ap()), 16)
            g_cst.inc(g.dma_start(out=ONES[:], in_=ones_dr.ap()), 16)
            g_cst.inc(g.dma_start(out=ONESB[:], in_=onesb_dr.ap()), 16)
            g_cst.inc(g.dma_start(out=IDENTB[:], in_=identb_dr.ap()), 16)
            g_cst.inc(g.dma_start(out=IDENT[:], in_=ident_dr.ap()), 16)
            g_cst.inc(g.dma_start(out=EPS[:], in_=eps_dr.ap()), 16)
            g_cst.inc(g.dma_start(out=BQKV[:], in_=bqkv.rearrange("(c p) -> p c", p=128), allow_slow_non_contiguous=True), 16)
            g_cst.inc(g.dma_start(out=BOUT[:], in_=bout.rearrange("(c p) -> p c", p=128), allow_slow_non_contiguous=True), 16)

        CST = g_cst.v

        def fence():
            snaps = [(c.sem, c.v) for c in
                     (t_p, v_p, a_p, g_sp, g_sp0, g_sp1, g_rows, g_out, g_cst,
                      s_x0, s_x1, s_pd, s_wq, s_kv, s_qt0, s_qt1, s_wo, s_w0,
                      s_w1, s_d0, s_d1)]

            def emit(eng):
                for sem, val in snaps:
                    if val:
                        eng.wait_ge(sem, val)

            @blk.tensor
            def _(t):
                emit(t)

            @blk.vector
            def _(v):
                emit(v)

            @blk.scalar
            def _(s):
                emit(s)

            @blk.sync
            def _(sy):
                emit(sy)

            @blk.gpsimd
            def _(g):
                emit(g)

        def emit_ln_stats(src, ntt, row_base, SUMS, SQ, XCH, X2):
            t0, v0, a0, gr0 = t_p.v, v_p.v, a_p.v, g_rows.v
            slots = [s_x0, s_x1]
            ld_vals = []

            def x2_idx(n):
                return 20 * (n // NDCH) + (n % NDCH) + 1

            @blk.sync
            def _(sy):
                for tt in range(ntt):
                    for dch in range(NDCH):
                        n = tt * NDCH + dch
                        b = n % 2
                        if n >= 2:
                            sy.wait_ge(t_p.sem, t0 + 2 * (n - 1))
                            sy.wait_ge(v_p.sem, v0 + x2_idx(n - 2))
                        else:
                            sy.wait_ge(t_p.sem, t0)
                            sy.wait_ge(v_p.sem, v0)
                        ld_vals.append(slots[b].inc(sy.dma_start(
                            out=XCH[:, b * TT:(b + 1) * TT],
                            in_=src[dch * 128:(dch + 1) * 128, tt * TT:(tt + 1) * TT]), 16))

            @blk.vector
            def _(v):
                for tt in range(ntt):
                    for dch in range(NDCH):
                        n = tt * NDCH + dch
                        b = n % 2
                        v.wait_ge(slots[b].sem, ld_vals[n])
                        if n >= 2:
                            v.wait_ge(t_p.sem, t0 + 2 * (n - 1))
                        v_p.inc(v.tensor_tensor(
                            out=X2[:, b * TT:(b + 1) * TT],
                            in0=XCH[:, b * TT:(b + 1) * TT],
                            in1=XCH[:, b * TT:(b + 1) * TT], op=ALU.mult))
                    v.wait_ge(a_p.sem, a0 + 2 * tt + 1)
                    v.wait_ge(v_p.sem, v0 + 20 * tt + 16)
                    v_p.inc(v.tensor_tensor(out=TMP[0:1, TT:2 * TT], in0=TMP[0:1, 0:TT],
                                            in1=TMP[0:1, 0:TT], op=ALU.mult))
                    v.wait_ge(v_p.sem, v0 + 20 * tt + 17)
                    v.wait_ge(t_p.sem, t0 + 2 * NDCH * (tt + 1))
                    v_p.inc(v.scalar_tensor_tensor(
                        out=TMP[0:1, TT:2 * TT], in0=SQ[0:1, :], scalar=1.0 / D,
                        in1=TMP[0:1, TT:2 * TT], op0=ALU.mult, op1=ALU.subtract))
                    v.wait_ge(a_p.sem, a0 + 2 * tt + 2)
                    v.wait_ge(v_p.sem, v0 + 20 * tt + 18)
                    v.wait_ge(g_rows.sem, gr0 + 32 * tt)
                    v_p.inc(v.reciprocal(TMPR[0:1, :], TMP[0:1, TT:2 * TT]))
                    v.wait_ge(v_p.sem, v0 + 20 * tt + 19)
                    v_p.inc(v.tensor_tensor(out=TMPM[0:1, :], in0=TMP[0:1, 0:TT],
                                            in1=TMPR[0:1, :], op=ALU.mult))

            @blk.gpsimd
            def _(g):
                for tt in range(ntt):
                    g.wait_ge(v_p.sem, v0 + 20 * tt + 20)
                    g_rows.inc(g.dma_start(out=rows_dr[0, row_base + tt, :],
                                           in_=TMPR[0:1, :]), 16)
                    g_rows.inc(g.dma_start(out=rows_dr[1, row_base + tt, :],
                                           in_=TMPM[0:1, :]), 16)

            @blk.tensor
            def _(t):
                t.wait_ge(g_cst.sem, CST)
                t.wait_ge(v_p.sem, v0)
                t.wait_ge(a_p.sem, a0)
                for tt in range(ntt):
                    if tt > 0:
                        t.wait_ge(a_p.sem, a0 + 2 * tt - 1)
                        t.wait_ge(v_p.sem, v0 + 20 * tt - 2)
                    for dch in range(NDCH):
                        n = tt * NDCH + dch
                        b = n % 2
                        t.wait_ge(slots[b].sem, ld_vals[n])
                        t_p.inc(t.matmul(SUMS[:], ONES[:], XCH[:, b * TT:(b + 1) * TT],
                                         start=(dch == 0), stop=(dch == NDCH - 1)))
                        t.wait_ge(v_p.sem, v0 + x2_idx(n))
                        t_p.inc(t.matmul(SQ[:], ONES[:], X2[:, b * TT:(b + 1) * TT],
                                         start=(dch == 0), stop=(dch == NDCH - 1)))

            @blk.scalar
            def _(sc_):
                sc_.wait_ge(g_cst.sem, CST)
                for tt in range(ntt):
                    sc_.wait_ge(t_p.sem, t0 + 2 * NDCH * (tt + 1))
                    sc_.wait_ge(v_p.sem, v0 + 20 * tt)
                    a_p.inc(sc_.activation(out=TMP[0:1, 0:TT], in_=SUMS[0:1, :],
                                           func=AF.Identity, scale=1.0 / D))
                    sc_.wait_ge(v_p.sem, v0 + 20 * tt + 18)
                    a_p.inc(sc_.activation(out=TMP[0:1, TT:2 * TT], in_=TMP[0:1, TT:2 * TT],
                                           func=AF.Sqrt, bias=EPS[0:1, 0:1]))

        def emit_ln_tile(src, tt, row_slot, rows_g, XCH, T1, XN2, XNH, XNL, RBS, MRBS,
                         RP1, RP2, AUX1, AUX2,
                         pads_free_t, pads_free_v, first_waits, prev, xn_consumer):
            @blk.sync
            def _(sy):
                sy.wait_ge(g_rows.sem, rows_g)
                sy.wait_ge(v_p.sem, pads_free_v)
                if pads_free_t:
                    sy.wait_ge(t_p.sem, pads_free_t)
                s_pd.inc(sy.dma_start(out=RP1[0:1, :], in_=rows_dr[0, row_slot, :]), 16)
                s_pd.inc(sy.dma_start(out=RP2[0:1, :], in_=rows_dr[1, row_slot, :]), 16)

            pad_ld = s_pd.v
            a_at = a_p.v

            @blk.tensor
            def _(t):
                t.wait_ge(s_pd.sem, pad_ld)
                t.wait_ge(v_p.sem, pads_free_v)
                t.wait_ge(a_p.sem, a_at)
                t_p.inc(t.matmul(AUX1[:], ONES[:], RP1[:], start=True, stop=True))
                t_p.inc(t.matmul(AUX2[:], ONES[:], RP2[:], start=True, stop=True))

            t_bc = t_p.v

            @blk.vector
            def _(v):
                v.wait_ge(t_p.sem, t_bc)
                v.wait_ge(v_p.sem, v_p.v)
                v_p.inc(v.tensor_copy(RBS[:], AUX1[:]))
                v_p.inc(v.tensor_copy(MRBS[:], AUX2[:]))

            v_bc = v_p.v
            xn_done = []
            free_marks = []
            ld_vals = []
            slots = [s_x0, s_x1]

            for dch in range(NDCH):
                b = dch % 2

                @blk.sync
                def _(sy, dch=dch, b=b):
                    if dch >= 2:
                        sy.wait_ge(v_p.sem, xn_done[dch - 2])
                    else:
                        for sem, val in first_waits:
                            sy.wait_ge(sem, val)
                        if prev is not None:
                            sy.wait_ge(v_p.sem, prev[0][NDCH - 2 + dch])
                    ld_vals.append(slots[b].inc(sy.dma_start(
                        out=XCH[:, b * TT:(b + 1) * TT],
                        in_=src[dch * 128:(dch + 1) * 128, tt * TT:(tt + 1) * TT]), 16))

                @blk.vector
                def _(v, dch=dch, b=b):
                    v.wait_ge(slots[b].sem, ld_vals[dch])
                    v.wait_ge(v_p.sem, v_bc)
                    if dch >= 2:
                        sem, val = free_marks[dch - 2]
                        v.wait_ge(sem, val)
                        v.wait_ge(v_p.sem, xn_done[dch - 2])
                    elif prev is not None:
                        sem, val = prev[1][NDCH - 2 + dch]
                        v.wait_ge(sem, val)
                        v.wait_ge(v_p.sem, prev[0][NDCH - 2 + dch])
                    v_p.inc(v.tensor_tensor(out=T1[:, b * TT:(b + 1) * TT],
                                            in0=XCH[:, b * TT:(b + 1) * TT],
                                            in1=RBS[:], op=ALU.mult))
                    v.wait_ge(v_p.sem, v_p.v)
                    v_p.inc(v.tensor_tensor(out=XN2[:, b * TT:(b + 1) * TT],
                                            in0=T1[:, b * TT:(b + 1) * TT],
                                            in1=MRBS[:], op=ALU.subtract))
                    if XNH is not None:
                        v.wait_ge(v_p.sem, v_p.v)
                        v_p.inc(v.tensor_copy(XNH[:, b * TT:(b + 1) * TT],
                                              XN2[:, b * TT:(b + 1) * TT]))
                        v.wait_ge(v_p.sem, v_p.v)
                        v_p.inc(v.tensor_tensor(out=XNL[:, b * TT:(b + 1) * TT],
                                                in0=XN2[:, b * TT:(b + 1) * TT],
                                                in1=XNH[:, b * TT:(b + 1) * TT],
                                                op=ALU.subtract))

                xn_done.append(v_p.v)
                free_marks.append(xn_consumer(dch, b, v_p.v))

            return xn_done, free_marks

        XF8 = es.enter_context(nc.sbuf_tensor("XF8", [128, NDCH * TOWN], F8))
        XF8V = [0]

        with contextlib.ExitStack() as ea:
            TMP = ea.enter_context(nc.sbuf_tensor("TMP", [1, 2 * TT], F32))
            TMPR = ea.enter_context(nc.sbuf_tensor("TMPR", [1, TT], F32))
            TMPM = ea.enter_context(nc.sbuf_tensor("TMPM", [1, TT], F32))
            WQ = ea.enter_context(nc.sbuf_tensor("WQ", [128, 6 * NDCH * 128], F32))
            WQH = ea.enter_context(nc.sbuf_tensor("WQH", [128, 6 * NDCH * 128], BF16))
            WQL = ea.enter_context(nc.sbuf_tensor("WQL", [128, 6 * NDCH * 128], BF16))
            XCH = ea.enter_context(nc.sbuf_tensor("XCH", [128, 2 * TT], F32))
            X2 = ea.enter_context(nc.sbuf_tensor("X2", [128, 2 * TT], F32))
            T1 = ea.enter_context(nc.sbuf_tensor("T1", [128, 2 * TT], F32))
            XN2 = ea.enter_context(nc.sbuf_tensor("XN2", [128, 2 * TT], F32))
            XNH = ea.enter_context(nc.sbuf_tensor("XNH", [128, 2 * TT], BF16))
            XNL = ea.enter_context(nc.sbuf_tensor("XNL", [128, 2 * TT], BF16))
            RBS = ea.enter_context(nc.sbuf_tensor("RBS", [128, TT], F32))
            MRBS = ea.enter_context(nc.sbuf_tensor("MRBS", [128, TT], F32))
            RP1 = ea.enter_context(nc.sbuf_tensor("RP1", [128, TT], F32))
            RP2 = ea.enter_context(nc.sbuf_tensor("RP2", [128, TT], F32))
            EV = ea.enter_context(nc.sbuf_tensor("EV", [128, 6 * TT], F32))
            EVH = ea.enter_context(nc.sbuf_tensor("EVH", [128, 6 * TT], BF16))
            EVL = ea.enter_context(nc.sbuf_tensor("EVL", [128, 6 * TT], BF16))
            VTT = ea.enter_context(nc.sbuf_tensor("VTT", [128, 4 * TT], BF16))
            KSQ = ea.enter_context(nc.sbuf_tensor("KSQ", [128, TT], F32))

            @blk.sync
            def _(sy):
                src = wqkvT.rearrange("(dch p) (ech j) -> p ech dch j", p=128, j=128)
                s_wq.inc(sy.dma_start(out=WQ.ap().rearrange(
                    "p (ech dch j) -> p ech dch j", ech=6, dch=NDCH), in_=src), 16)

            WQ_LD = s_wq.v

            @blk.vector
            def _(v):
                v.wait_ge(s_wq.sem, WQ_LD)
                v.wait_ge(v_p.sem, v_p.v)
                v_p.inc(v.tensor_copy(WQH[:], WQ[:]))
                v.wait_ge(v_p.sem, v_p.v)
                v_p.inc(v.tensor_tensor(out=WQL[:], in0=WQ[:], in1=WQH[:],
                                        op=ALU.subtract))

            WQHL_V = v_p.v

            with contextlib.ExitStack() as ea1:
                SUMS = ea1.enter_context(nc.psum_tensor("SUMS", [128, TT], F32))
                SQ = ea1.enter_context(nc.psum_tensor("SQ", [128, TT], F32))
                emit_ln_stats(xf, NTT, 0, SUMS, SQ, XCH, X2)
                emit_ln_stats(xo, 2, 16, SUMS, SQ, XCH, X2)

            STATS_T, STATS_V = t_p.v, v_p.v
            ROWS_A = g_rows.v

            QKV = ea.enter_context(nc.psum_tensor("QKV", [128, 6 * TT], F32))
            AUX1 = ea.enter_context(nc.psum_tensor("AUX1", [128, TT], F32))
            AUX2 = ea.enter_context(nc.psum_tensor("AUX2", [128, TT], F32))

            @blk.vector
            def _(v):
                v.wait_ge(v_p.sem, v_p.v)
                v_p.inc(v.memset(RP1[:], 0.0))
                v_p.inc(v.memset(RP2[:], 0.0))

            pads_free_t = 0
            pads_free_v = v_p.v
            first_waits = [(v_p.sem, STATS_V), (t_p.sem, STATS_T)]
            prevA = None
            ev_free = []
            vdr_g = 0

            for tt in range(NTT):
                def qkv_consume(dch, b, xn_v, a_pre=a_p.v):
                    @blk.tensor
                    def _(t, dch=dch, b=b, xn_v=xn_v, a_pre=a_pre):
                        t.wait_ge(v_p.sem, max(xn_v, WQHL_V))
                        if dch == 0:
                            t.wait_ge(a_p.sem, a_pre)
                        bs = b * TT
                        for ech in range(6):
                            w0 = (ech * NDCH + dch) * 128
                            t_p.inc(t.matmul(
                                QKV[:, ech * TT:(ech + 1) * TT],
                                WQH[:, w0:w0 + 128], XNH[:, bs:bs + TT],
                                start=(dch == 0), stop=False))
                            t_p.inc(t.matmul(
                                QKV[:, ech * TT:(ech + 1) * TT],
                                WQH[:, w0:w0 + 128], XNL[:, bs:bs + TT],
                                start=False, stop=False))
                            t_p.inc(t.matmul(
                                QKV[:, ech * TT:(ech + 1) * TT],
                                WQL[:, w0:w0 + 128], XNH[:, bs:bs + TT],
                                start=False, stop=(dch == NDCH - 1)))
                    return (t_p.sem, t_p.v)

                prevA = emit_ln_tile(xf, tt, tt, ROWS_A, XCH, T1, XN2, XNH, XNL, RBS, MRBS,
                                     RP1, RP2, AUX1, AUX2, pads_free_t, pads_free_v,
                                     first_waits, prevA, qkv_consume)
                first_waits = []
                t_qkv_done = t_p.v

                @blk.scalar
                def _(sc_, t_qkv_done=t_qkv_done, ev_free=list(ev_free)):
                    sc_.wait_ge(t_p.sem, t_qkv_done)
                    sc_.wait_ge(g_cst.sem, CST)
                    for sem, val in ev_free:
                        sc_.wait_ge(sem, val)
                    for ech in range(6):
                        a_p.inc(sc_.activation(out=EV[:, ech * TT:(ech + 1) * TT],
                                               in_=QKV[:, ech * TT:(ech + 1) * TT],
                                               func=AF.Identity, bias=BQKV[:, ech:ech + 1],
                                               scale=1.0))

                a_ev = a_p.v

                @blk.vector
                def _(v, a_ev=a_ev, gprev=g_sp.v, tprev=t_p.v):
                    v.wait_ge(a_p.sem, a_ev)
                    v.wait_ge(v_p.sem, v_p.v)
                    v.wait_ge(g_sp.sem, gprev)
                    v.wait_ge(t_p.sem, tprev)
                    v_p.inc(v.tensor_copy(EVH[:], EV[:]))
                    v.wait_ge(v_p.sem, v_p.v)
                    v_p.inc(v.tensor_tensor(out=EVL[:], in0=EV[:], in1=EVH[:],
                                            op=ALU.subtract))

                ev_hl = v_p.v

                @blk.gpsimd
                def _(g, tt=tt, ev_hl=ev_hl):
                    g.wait_ge(v_p.sem, ev_hl)
                    for h in range(2):
                        for lv, EVX in ((0, EVH), (1, EVL)):
                            g_sp.inc(g.dma_start(
                                out=qdr[h, lv, :, tt * TT:(tt + 1) * TT],
                                in_=EVX[:, h * TT:(h + 1) * TT]), 16)
                            g_sp.inc(g.dma_start(
                                out=kdr[h, lv, :, tt * TT:(tt + 1) * TT],
                                in_=EVX[:, (2 + h) * TT:(3 + h) * TT]), 16)

                for i, (ech, slot) in enumerate(
                        [(0, 0), (1, 16), (2, 32), (3, 48)]):
                    @blk.vector
                    def _(v, ech=ech, a_ev=a_ev, tprev=t_p.v, vprev=v_p.v):
                        v.wait_ge(a_p.sem, a_ev)
                        v.wait_ge(t_p.sem, tprev)
                        v.wait_ge(v_p.sem, vprev)
                        v_p.inc(v.tensor_tensor(out=KSQ[:],
                                                in0=EV[:, ech * TT:(ech + 1) * TT],
                                                in1=EV[:, ech * TT:(ech + 1) * TT],
                                                op=ALU.mult))

                    sq_v = v_p.v

                    @blk.tensor
                    def _(t, sq_v=sq_v, vfree=v_p.v):
                        t.wait_ge(v_p.sem, sq_v)
                        t_p.inc(t.matmul(AUX1[:], ONES[:], KSQ[:], start=True, stop=True))

                    mm_t = t_p.v

                    @blk.vector
                    def _(v, mm_t=mm_t, slot=slot, tt=tt):
                        v.wait_ge(t_p.sem, mm_t)
                        v.wait_ge(v_p.sem, v_p.v)
                        v_p.inc(v.reduce_max(NRM[0:1, slot + tt:slot + tt + 1],
                                             AUX1[0:1, :], axis=AX.X))

                for h in range(2):
                    @blk.tensor
                    def _(t, h=h, a_ev=a_ev, vfree=v_p.v):
                        t.wait_ge(a_p.sem, a_ev)
                        t.wait_ge(v_p.sem, vfree)
                        for i in range(4):
                            t_p.inc(t.transpose(
                                AUX2[:, i * 128:(i + 1) * 128],
                                EV[:, (4 + h) * TT + i * 128:(4 + h) * TT + (i + 1) * 128],
                                IDENT[:]))

                    tp_t = t_p.v

                    @blk.vector
                    def _(v, h=h, tp_t=tp_t, vdr_g=vdr_g):
                        v.wait_ge(t_p.sem, tp_t)
                        v.wait_ge(v_p.sem, v_p.v)
                        v.wait_ge(g_sp.sem, vdr_g)
                        hi = slice((2 * h) * TT, (2 * h + 1) * TT)
                        lo = slice((2 * h + 1) * TT, (2 * h + 2) * TT)
                        v_p.inc(v.tensor_copy(VTT[:, hi], AUX2[:]))
                        v.wait_ge(v_p.sem, v_p.v)
                        v_p.inc(v.tensor_tensor(out=VTT[:, lo], in0=AUX2[:],
                                                in1=VTT[:, hi], op=ALU.subtract))

                vtt_v = v_p.v

                @blk.gpsimd
                def _(g, tt=tt, vtt_v=vtt_v):
                    g.wait_ge(v_p.sem, vtt_v)
                    for hl in range(4):
                        h, lv = hl // 2, hl % 2
                        for i in range(4):
                            tk = tt * TT + i * 128
                            g_sp.inc(g.dma_start(
                                out=vdr[h, lv, tk:tk + 128, :],
                                in_=VTT[:, hl * TT + i * 128:hl * TT + (i + 1) * 128]), 16)

                vdr_g = g_sp.v
                pads_free_t = t_p.v
                pads_free_v = v_p.v
                ev_free = [(v_p.sem, v_p.v), (t_p.sem, t_p.v), (g_sp.sem, g_sp.v)]

            for tt in range(2):
                def f8_consume(dch, b, xn_v, tt=tt):
                    @blk.vector
                    def _(v, dch=dch, b=b, tt=tt):
                        v.wait_ge(v_p.sem, v_p.v)
                        v_p.inc(v.tensor_copy(
                            XF8[:, dch * TOWN + tt * TT:dch * TOWN + (tt + 1) * TT],
                            XN2[:, b * TT:(b + 1) * TT]))
                    return (v_p.sem, v_p.v)

                prevA = emit_ln_tile(xo, tt, 16 + tt, ROWS_A, XCH, T1, XN2,
                                     None, None, RBS, MRBS, RP1, RP2, AUX1, AUX2,
                                     pads_free_t, pads_free_v, [], prevA, f8_consume)
                pads_free_t = t_p.v
                pads_free_v = v_p.v

            XF8V[0] = v_p.v

        A_T, A_V, A_A, A_G = t_p.v, v_p.v, a_p.v, g_sp.v

        with contextlib.ExitStack() as eb0:
            BBPS = eb0.enter_context(nc.psum_tensor("BBPS", [128, TT], F32))

            @blk.vector
            def _(v):
                v.wait_ge(v_p.sem, A_V)
                v.wait_ge(g_cst.sem, CST)
                v_p.inc(v.memset(BBPAD[:], 0.0))
                for h in range(2):
                    v.wait_ge(v_p.sem, v_p.v)
                    v.wait_ge(a_p.sem, A_A + 2 * h)
                    v_p.inc(v.tensor_copy(BROW[0:1, 16:32], NRM[0:1, 32 + h * 16:48 + h * 16]))
                    for qt in range(1, NTT):
                        v.wait_ge(v_p.sem, v_p.v)
                        v_p.inc(v.tensor_tensor(out=BROW[0:1, 16 + qt:17 + qt],
                                                in0=BROW[0:1, 16 + qt:17 + qt],
                                                in1=BROW[0:1, 15 + qt:16 + qt], op=ALU.max))
                    v.wait_ge(v_p.sem, v_p.v)
                    v_p.inc(v.tensor_tensor(out=BROW[0:1, 32:48],
                                            in0=NRM[0:1, h * 16:(h + 1) * 16],
                                            in1=BROW[0:1, 16:32], op=ALU.mult))
                    v.wait_ge(a_p.sem, A_A + 2 * (h + 1))
                    v_p.inc(v.tensor_copy(BBPAD[0:1, h * 16:(h + 1) * 16],
                                          BROW[0:1, 32:48]))

            @blk.scalar
            def _(sc_):
                vv = A_V + 1
                for h in range(2):
                    vv += 17
                    sc_.wait_ge(v_p.sem, vv)
                    a_p.inc(sc_.activation(out=BROW[0:1, 32:48], in_=BROW[0:1, 32:48],
                                           func=AF.Sqrt))
                    sc_.wait_ge(a_p.sem, a_p.v)
                    a_p.inc(sc_.activation(out=BROW[0:1, 32:48], in_=BROW[0:1, 32:48],
                                           func=AF.Identity, scale=-SC))
                    vv += 1

            @blk.tensor
            def _(t, a_at=a_p.v):
                t.wait_ge(v_p.sem, v_p.v)
                t.wait_ge(a_p.sem, a_at)
                t_p.inc(t.matmul(BBPS[:, 0:32], ONES[:], BBPAD[:, 0:32],
                                 start=True, stop=True))

            @blk.vector
            def _(v):
                v.wait_ge(t_p.sem, t_p.v)
                v.wait_ge(v_p.sem, v_p.v)
                v_p.inc(v.tensor_copy(BB[:], BBPS[:, 0:32]))

        BB_V = v_p.v
        fence()

        with contextlib.ExitStack() as eb:
            KHH = eb.enter_context(nc.sbuf_tensor("KHH", [128, S], BF16))
            KHL = eb.enter_context(nc.sbuf_tensor("KHL", [128, S], BF16))
            VHH = eb.enter_context(nc.sbuf_tensor("VHH", [128, S], BF16))
            VHL = eb.enter_context(nc.sbuf_tensor("VHL", [128, S], BF16))
            QTH = eb.enter_context(nc.sbuf_tensor("QTH", [128, 2 * TT], BF16))
            QTL = eb.enter_context(nc.sbuf_tensor("QTL", [128, 2 * TT], BF16))
            P = eb.enter_context(nc.sbuf_tensor("P", [128, 2 * TT], F32))
            PH = eb.enter_context(nc.sbuf_tensor("PH", [128, 2 * TT], BF16))
            PL = eb.enter_context(nc.sbuf_tensor("PL", [128, 2 * TT], BF16))
            RBN = eb.enter_context(nc.sbuf_tensor("RBN", [128, TT], F32))
            AO = eb.enter_context(nc.sbuf_tensor("AO", [128, 2 * NTT * TT], F32))
            AOH = eb.enter_context(nc.sbuf_tensor("AOH", [128, 2 * TT], BF16))
            AOL = eb.enter_context(nc.sbuf_tensor("AOL", [128, 2 * TT], BF16))
            WO = eb.enter_context(nc.sbuf_tensor("WO", [128, NDCH * 128], F32))
            WOH = eb.enter_context(nc.sbuf_tensor("WOH", [128, 2 * NDCH * 128], BF16))
            WOL = eb.enter_context(nc.sbuf_tensor("WOL", [128, 2 * NDCH * 128], BF16))
            SP = eb.enter_context(nc.sbuf_tensor("SP", [128, 2 * TT], F32))
            STP = eb.enter_context(nc.psum_tensor("STP", [128, 2 * TT], F32))
            AVP = eb.enter_context(nc.psum_tensor("AVP", [128, TT], F32))
            SMP = eb.enter_context(nc.psum_tensor("SMP", [128, TT], F32))
            OPP = eb.enter_context(nc.psum_tensor("OPP", [128, 2 * TT], F32))

            wsrc = woutT.rearrange("(dch p) (ech j) -> p dch ech j", p=128, j=128)
            for dch in range(2):
                @blk.sync
                def _(sy, dch=dch, vprev=v_p.v):
                    if dch == 1:
                        sy.wait_ge(v_p.sem, vprev)
                    s_wo.inc(sy.dma_start(out=WO.ap().rearrange(
                        "p (ech j) -> p ech j", ech=NDCH), in_=wsrc[:, dch, :, :]), 16)

                wo_ld = s_wo.v

                @blk.vector
                def _(v, dch=dch, wo_ld=wo_ld):
                    v.wait_ge(s_wo.sem, wo_ld)
                    v.wait_ge(v_p.sem, v_p.v)
                    half = slice(dch * NDCH * 128, (dch + 1) * NDCH * 128)
                    v_p.inc(v.tensor_copy(WOH[:, half], WO[:]))
                    v.wait_ge(v_p.sem, v_p.v)
                    v_p.inc(v.tensor_tensor(out=WOL[:, half], in0=WO[:],
                                            in1=WOH[:, half], op=ALU.subtract))

            WOHL_V = v_p.v
            ao_done_v = max(BB_V, WOHL_V)
            AB_A = a_p.v
            prev_pl_tail = []
            prev_ae_tail = []
            prev_tsum_tail = []
            qt_last_s = {}

            for h in range(2):
                @blk.sync
                def _(sy, h=h, t_all=t_p.v):
                    sy.wait_ge(g_sp.sem, A_G)
                    if h == 1:
                        sy.wait_ge(t_p.sem, t_all)
                    for lv, KHX in ((0, KHH), (1, KHL)):
                        for i in range(2):
                            s_kv.inc(sy.dma_start(
                                out=KHX[:, i * 4096:(i + 1) * 4096],
                                in_=kdr[h, lv, :, i * 4096:(i + 1) * 4096]), 16)
                    vsrc = vdr.rearrange("h lv (tb p) j -> h lv p tb j", p=128)
                    for lv, VHX in ((0, VHH), (1, VHL)):
                        for i in range(2):
                            s_kv.inc(sy.dma_start(
                                out=VHX.ap().rearrange("p (tb j) -> p tb j", j=128)[
                                    :, i * 32:(i + 1) * 32, :],
                                in_=vsrc[h, lv, :, i * 32:(i + 1) * 32, :]), 16)

                KV_LD = s_kv.v

                for qt in range(NTT):
                    nkt = (qt + 1) * 4
                    qb = qt % 2
                    gk = (h, qt)
                    qsem = s_qt0 if qb == 0 else s_qt1

                    @blk.sync
                    def _(sy, h=h, qt=qt, qb=qb, qsem=qsem):
                        key = (h, qt - 2)
                        if key in qt_last_s:
                            sy.wait_ge(t_p.sem, qt_last_s[key])
                        qsem.inc(sy.dma_start(out=QTH[:, qb * TT:(qb + 1) * TT],
                                              in_=qdr[h, 0, :, qt * TT:(qt + 1) * TT]), 16)
                        qsem.inc(sy.dma_start(out=QTL[:, qb * TT:(qb + 1) * TT],
                                              in_=qdr[h, 1, :, qt * TT:(qt + 1) * TT]), 16)

                    q_ld = qsem.v

                    T0, A0, V0 = t_p.v, a_p.v, v_p.v
                    ts_vals = [0] * nkt
                    tsum_vals = [0] * nkt
                    ae_vals = [0] * nkt
                    pl_vals = [0] * nkt
                    av = A0
                    for kt in range(nkt):
                        av += 1 if kt >= nkt - 4 else 2
                        ae_vals[kt] = av
                    tv = T0
                    for kt in range(nkt):
                        tv += 3
                        ts_vals[kt] = tv
                        if kt >= 1:
                            tv += 4
                            tsum_vals[kt - 1] = tv
                    tv += 4
                    tsum_vals[nkt - 1] = tv
                    vv = V0
                    for kt in range(nkt):
                        vv += 3 if kt >= nkt - 4 else 1
                        pl_vals[kt] = vv
                    qt_last_s[gk] = ts_vals[nkt - 1]

                    @blk.tensor
                    def _(t, h=h, qt=qt, qb=qb, nkt=nkt, q_ld=q_ld, KV_LD=KV_LD,
                          qsem=qsem, av0=ao_done_v, pae=list(prev_ae_tail)):
                        t.wait_ge(qsem.sem, q_ld)
                        t.wait_ge(s_kv.sem, KV_LD)
                        qs = slice(qb * TT, (qb + 1) * TT)

                        def emit_s(kt):
                            if kt >= 2:
                                t.wait_ge(a_p.sem, ae_vals[kt - 2])
                            elif len(pae) == 2:
                                t.wait_ge(a_p.sem, pae[kt])
                            else:
                                t.wait_ge(a_p.sem, AB_A)
                            ks = slice(kt * 128, (kt + 1) * 128)
                            pb = kt % 2
                            t_p.inc(t.matmul(STP[:, pb * TT:(pb + 1) * TT],
                                             KHH[:, ks], QTH[:, qs], start=True, stop=False))
                            t_p.inc(t.matmul(STP[:, pb * TT:(pb + 1) * TT],
                                             KHH[:, ks], QTL[:, qs], start=False, stop=False))
                            t_p.inc(t.matmul(STP[:, pb * TT:(pb + 1) * TT],
                                             KHL[:, ks], QTH[:, qs], start=False, stop=True))

                        def emit_av(kt):
                            t.wait_ge(v_p.sem, pl_vals[kt])
                            t.wait_ge(a_p.sem, ae_vals[kt])
                            if kt == 0:
                                t.wait_ge(v_p.sem, av0)
                            ks = slice(kt * 128, (kt + 1) * 128)
                            pb = kt % 2
                            ps = slice(pb * TT, (pb + 1) * TT)
                            t_p.inc(t.matmul(AVP[:], VHH[:, ks], PH[:, ps],
                                             start=(kt == 0), stop=False))
                            t_p.inc(t.matmul(AVP[:], VHH[:, ks], PL[:, ps],
                                             start=False, stop=False))
                            t_p.inc(t.matmul(AVP[:], VHL[:, ks], PH[:, ps],
                                             start=False, stop=(kt == nkt - 1)))
                            t_p.inc(t.matmul(SMP[:], ONESB[:], PH[:, ps],
                                             start=(kt == 0), stop=(kt == nkt - 1)))

                        for kt in range(nkt):
                            emit_s(kt)
                            if kt >= 1:
                                emit_av(kt - 1)
                        emit_av(nkt - 1)
                        assert t_p.v == tsum_vals[nkt - 1], (t_p.v, tsum_vals[nkt - 1])

                    @blk.scalar
                    def _(sc_, h=h, qt=qt, nkt=nkt, ppl=list(prev_pl_tail),
                          ptt=list(prev_tsum_tail)):
                        for kt in range(nkt):
                            pb = kt % 2
                            ps = slice(pb * TT, (pb + 1) * TT)
                            sc_.wait_ge(t_p.sem, ts_vals[kt])
                            sc_.wait_ge(v_p.sem, BB_V)
                            if kt >= 2:
                                sc_.wait_ge(v_p.sem, pl_vals[kt - 2])
                            elif len(ppl) == 2:
                                sc_.wait_ge(v_p.sem, ppl[kt])
                            a_p.inc(sc_.activation(out=P[:, ps],
                                                   in_=STP[:, ps],
                                                   func=AF.Exp,
                                                   bias=BB[:, h * 16 + qt:h * 16 + qt + 1],
                                                   scale=SC))
                            if kt < nkt - 4:
                                if kt >= 2:
                                    sc_.wait_ge(t_p.sem, tsum_vals[kt - 2])
                                elif len(ptt) == 2:
                                    sc_.wait_ge(t_p.sem, ptt[kt])
                                a_p.inc(sc_.activation(out=PH[:, ps],
                                                       in_=STP[:, ps],
                                                       func=AF.Exp,
                                                       bias=BB[:, h * 16 + qt:h * 16 + qt + 1],
                                                       scale=SC))
                        assert a_p.v == ae_vals[-1]

                    @blk.vector
                    def _(v, h=h, qt=qt, nkt=nkt, ptt=list(prev_tsum_tail)):
                        for kt in range(nkt):
                            pb = kt % 2
                            ps = slice(pb * TT, (pb + 1) * TT)
                            v.wait_ge(a_p.sem, ae_vals[kt])
                            if kt >= nkt - 4:
                                off = kt - (nkt - 4)
                                v.wait_ge(v_p.sem, v_p.v)
                                v_p.inc(v.tensor_tensor(
                                    out=P[:, ps], in0=P[:, ps],
                                    in1=MASKS[:, off * TT:(off + 1) * TT], op=ALU.mult))
                                v.wait_ge(v_p.sem, v_p.v)
                                if kt >= 2:
                                    v.wait_ge(t_p.sem, tsum_vals[kt - 2])
                                elif len(ptt) == 2:
                                    v.wait_ge(t_p.sem, ptt[kt])
                                v_p.inc(v.tensor_copy(PH[:, ps], P[:, ps]))
                            v.wait_ge(v_p.sem, v_p.v)
                            if kt >= 2:
                                v.wait_ge(t_p.sem, tsum_vals[kt - 2])
                            elif len(ptt) == 2:
                                v.wait_ge(t_p.sem, ptt[kt])
                            v_p.inc(v.tensor_tensor(out=PL[:, ps], in0=P[:, ps],
                                                    in1=PH[:, ps], op=ALU.subtract))
                        assert v_p.v == pl_vals[-1]
                        v.wait_ge(t_p.sem, tsum_vals[nkt - 1])
                        v.wait_ge(v_p.sem, v_p.v)
                        v_p.inc(v.reciprocal(RBN[:], SMP[:]))
                        v.wait_ge(v_p.sem, v_p.v)
                        v_p.inc(v.tensor_tensor(
                            out=AO[:, (h * NTT + qt) * TT:(h * NTT + qt + 1) * TT],
                            in0=AVP[:], in1=RBN[:], op=ALU.mult))

                    ao_done_v = v_p.v
                    prev_ae_tail = ae_vals[-2:]
                    prev_tsum_tail = tsum_vals[-2:]
                    prev_pl_tail = pl_vals[-2:]

            AO_DONE = v_p.v
            op_evac_a = {}
            sp_dma_g = {}
            ao_dec_v = {}
            qt_mm_t = {}

            for qt in range(NTT):
                @blk.vector
                def _(v, qt=qt, tprev=qt_mm_t.get(qt - 1, 0)):
                    v.wait_ge(v_p.sem, AO_DONE)
                    v.wait_ge(v_p.sem, v_p.v)
                    if tprev:
                        v.wait_ge(t_p.sem, tprev)
                    for dch in range(2):
                        ds = slice(dch * TT, (dch + 1) * TT)
                        aos = slice((dch * NTT + qt) * TT, (dch * NTT + qt + 1) * TT)
                        v_p.inc(v.tensor_copy(AOH[:, ds], AO[:, aos]))
                        v.wait_ge(v_p.sem, v_p.v)
                        v_p.inc(v.tensor_tensor(out=AOL[:, ds], in0=AO[:, aos],
                                                in1=AOH[:, ds], op=ALU.subtract))

                ao_dec_v[qt] = v_p.v

                for ech in range(NDCH):
                    n = qt * NDCH + ech
                    ob = n % 2

                    @blk.tensor
                    def _(t, qt=qt, ech=ech, ob=ob, n=n, a_at=a_p.v):
                        t.wait_ge(v_p.sem, ao_dec_v[qt])
                        if n >= 2:
                            t.wait_ge(a_p.sem, op_evac_a[n - 2])
                        else:
                            t.wait_ge(a_p.sem, a_at)
                        for dch in range(2):
                            ws = slice((dch * NDCH + ech) * 128, (dch * NDCH + ech + 1) * 128)
                            ds = slice(dch * TT, (dch + 1) * TT)
                            t_p.inc(t.matmul(OPP[:, ob * TT:(ob + 1) * TT],
                                             WOH[:, ws], AOH[:, ds],
                                             start=(dch == 0), stop=False))
                            t_p.inc(t.matmul(OPP[:, ob * TT:(ob + 1) * TT],
                                             WOH[:, ws], AOL[:, ds],
                                             start=False, stop=False))
                            t_p.inc(t.matmul(OPP[:, ob * TT:(ob + 1) * TT],
                                             WOL[:, ws], AOH[:, ds],
                                             start=False, stop=(dch == 1)))

                    mm_t = t_p.v
                    qt_mm_t[qt] = t_p.v

                    @blk.scalar
                    def _(sc_, mm_t=mm_t, n=n, ob=ob):
                        sc_.wait_ge(t_p.sem, mm_t)
                        if n >= 2:
                            sc_.wait_ge(sp_dma_g[n - 2][0].sem, sp_dma_g[n - 2][1])
                        a_p.inc(sc_.activation(out=SP[:, ob * TT:(ob + 1) * TT],
                                               in_=OPP[:, ob * TT:(ob + 1) * TT],
                                               func=AF.Identity))

                    op_evac_a[n] = a_p.v

                    @blk.gpsimd
                    def _(g, qt=qt, ech=ech, ob=ob, n=n):
                        g.wait_ge(a_p.sem, op_evac_a[n])
                        rank = qt if qt < 8 else 15 - qt
                        half = 0 if qt < 8 else 1
                        gsem = g_sp0 if ob == 0 else g_sp1
                        gsem.inc(g.dma_start(
                            out=spart[rank, ech * 128:(ech + 1) * 128,
                                      half * TT:(half + 1) * TT],
                            in_=SP[:, ob * TT:(ob + 1) * TT]), 16)

                    sp_dma_g[n] = (g_sp0 if ob == 0 else g_sp1, (g_sp0 if ob == 0 else g_sp1).v)

            fence()

            @blk.gpsimd
            def _(g, a0=g_sp0.v, b0=g_sp1.v):
                g.wait_ge(g_sp0.sem, a0)
                g.wait_ge(g_sp1.sem, b0)
                cc.inc(g.collective_compute(
                    "ReduceScatter", ALU.add, replica_groups=[list(range(NC))],
                    ins=[spart.opt()], outs=[sred.opt()]))

        G = es.enter_context(nc.sbuf_tensor("G", [128, NDCH * TOWN], BF16))
        with contextlib.ExitStack() as ec:
            WST = ec.enter_context(nc.sbuf_tensor("WST", [128, 4 * 2048], F32))
            W8 = ec.enter_context(nc.sbuf_tensor("W8", [128, 4 * 2048], F8))
            GV = ec.enter_context(nc.sbuf_tensor("GV", [128, 2 * TT], BF16))
            F1 = ec.enter_context(nc.sbuf_tensor("F1", [128, 2 * TT], BF16))
            ACTB = ec.enter_context(nc.sbuf_tensor("ACTB", [128, 2 * TT], BF16))
            ACT8 = ec.enter_context(nc.sbuf_tensor("ACT8", [128, NHCH * TOWN], F8))

            FFP = ec.enter_context(nc.psum_tensor("FFP", [128, 4 * TT], F32))
            XF8_V = XF8V[0]

            st_ld, q8_v, ev_scl, act_v, ff_mm_t = {}, {}, {}, {}, {}
            for hch in range(NHCH):
                sb = hch % 2

                wsem = s_w0 if sb == 0 else s_w1

                @blk.sync
                def _(sy, hch=hch, sb=sb, wsem=wsem):
                    if hch >= 2:
                        sy.wait_ge(v_p.sem, q8_v[hch - 2])
                    for m, w in ((0, wg0), (1, wg1)):
                        src = w.rearrange("(dch p) (hch j) -> p hch dch j", p=128, j=128)
                        wsem.inc(sy.dma_start(
                            out=WST.ap().rearrange("p (s dch j) -> p s dch j", s=4, j=128)[
                                :, 2 * sb + m, :, :],
                            in_=src[:, hch, :, :]), 16)

                st_ld[hch] = wsem.v

                @blk.vector
                def _(v, hch=hch, sb=sb, wsem=wsem):
                    v.wait_ge(wsem.sem, st_ld[hch])
                    v.wait_ge(v_p.sem, v_p.v)
                    if hch >= 2:
                        v.wait_ge(t_p.sem, ff_mm_t[hch - 2])
                    v_p.inc(v.tensor_copy(W8[:, (2 * sb) * 2048:(2 * sb + 2) * 2048],
                                          WST[:, (2 * sb) * 2048:(2 * sb + 2) * 2048]))

                q8_v[hch] = v_p.v

                @blk.tensor
                def _(t, hch=hch, sb=sb, a_at=a_p.v):
                    t.wait_ge(v_p.sem, max(q8_v[hch], XF8_V))
                    if hch >= 1:
                        t.wait_ge(a_p.sem, ev_scl[hch - 1])
                    else:
                        t.wait_ge(a_p.sem, a_at)
                    for m in range(2):
                        for th in range(2):
                            for c2 in range(NDCH // 2):
                                w0 = (2 * sb + m) * 2048 + (2 * c2) * 128
                                lhs = W8[:, w0:w0 + 256].rearrange(
                                    "p (j m2) -> p j m2", j=2)
                                rhs = XF8[:, (2 * c2) * TOWN:(2 * c2 + 2) * TOWN].rearrange(
                                    "p (j t) -> p j t", j=2)[:, :, th * TT:(th + 1) * TT]
                                t_p.inc(t.matmul(
                                    FFP[:, (2 * m + th) * TT:(2 * m + th + 1) * TT],
                                    lhs, rhs,
                                    start=(c2 == 0), stop=(c2 == NDCH // 2 - 1),
                                    perf_mode=mybir.MatmulPerfMode.DoubleRow))

                ff_mm_t[hch] = t_p.v

                @blk.scalar
                def _(sc_, hch=hch):
                    sc_.wait_ge(t_p.sem, ff_mm_t[hch])
                    if hch >= 1:
                        sc_.wait_ge(v_p.sem, act_v[hch - 1])
                    for th in range(2):
                        a_p.inc(sc_.activation(out=GV[:, th * TT:(th + 1) * TT],
                                               in_=FFP[:, th * TT:(th + 1) * TT],
                                               func=AF.Gelu_apprx_tanh))

                ev_scl[hch] = a_p.v

                @blk.vector
                def _(v, hch=hch):
                    v.wait_ge(a_p.sem, ev_scl[hch])
                    v.wait_ge(t_p.sem, ff_mm_t[hch])
                    for th in range(2):
                        v.wait_ge(v_p.sem, v_p.v)
                        v_p.inc(v.tensor_copy(F1[:, th * TT:(th + 1) * TT],
                                              FFP[:, (2 + th) * TT:(3 + th) * TT]))
                        v.wait_ge(v_p.sem, v_p.v)
                        v_p.inc(v.tensor_tensor(out=ACTB[:, th * TT:(th + 1) * TT],
                                                in0=GV[:, th * TT:(th + 1) * TT],
                                                in1=F1[:, th * TT:(th + 1) * TT],
                                                op=ALU.mult))
                        v.wait_ge(v_p.sem, v_p.v)
                        v_p.inc(v.tensor_copy(
                            ACT8[:, hch * TOWN + th * TT:hch * TOWN + (th + 1) * TT],
                            ACTB[:, th * TT:(th + 1) * TT]))

                act_v[hch] = v_p.v

            ACT_V = v_p.v
            FF_LAST_T = ff_mm_t[NHCH - 1]
            FF_LAST_Q = q8_v[NHCH - 1]
            FF_LAST_E = ev_scl[NHCH - 1]

            wl_ld, wl_q8, wl_mm, g_evac = {}, {}, {}, {}
            for ech in range(NDCH):
                for grp in range(4):
                    n = ech * 4 + grp
                    sb = n % 2

                    wsem = s_w0 if sb == 0 else s_w1

                    @blk.sync
                    def _(sy, ech=ech, grp=grp, sb=sb, n=n, wsem=wsem):
                        if n >= 2:
                            sy.wait_ge(v_p.sem, wl_q8[n - 2])
                        else:
                            sy.wait_ge(v_p.sem, FF_LAST_Q)
                        src = wl.rearrange("(hch p) (ech j) -> p hch ech j", p=128, j=128)
                        wsem.inc(sy.dma_start(
                            out=WST.ap().rearrange("p (s dch j) -> p s dch j", s=4, j=128)[
                                :, 2 * sb, :, :],
                            in_=src[:, grp * 16:(grp + 1) * 16, ech, :]), 16)

                    wl_ld[n] = wsem.v

                    @blk.vector
                    def _(v, n=n, sb=sb, wsem=wsem):
                        v.wait_ge(wsem.sem, wl_ld[n])
                        v.wait_ge(v_p.sem, v_p.v)
                        if n >= 2:
                            v.wait_ge(t_p.sem, wl_mm[n - 2])
                        else:
                            v.wait_ge(t_p.sem, FF_LAST_T)
                        v_p.inc(v.tensor_copy(W8[:, (2 * sb) * 2048:(2 * sb) * 2048 + 2048],
                                              WST[:, (2 * sb) * 2048:(2 * sb) * 2048 + 2048]))

                    wl_q8[n] = v_p.v

                    @blk.tensor
                    def _(t, ech=ech, grp=grp, n=n, sb=sb):
                        t.wait_ge(v_p.sem, max(wl_q8[n], ACT_V))
                        if grp == 0:
                            if ech >= 1:
                                t.wait_ge(a_p.sem, g_evac[ech - 1])
                            else:
                                t.wait_ge(a_p.sem, FF_LAST_E)
                        for i2 in range(8):
                            hch = grp * 16 + 2 * i2
                            for th in range(2):
                                w0 = (2 * sb) * 2048 + (2 * i2) * 128
                                lhs = W8[:, w0:w0 + 256].rearrange(
                                    "p (j m2) -> p j m2", j=2)
                                rhs = ACT8[:, hch * TOWN:(hch + 2) * TOWN].rearrange(
                                    "p (j t) -> p j t", j=2)[:, :, th * TT:(th + 1) * TT]
                                t_p.inc(t.matmul(
                                    FFP[:, th * TT:(th + 1) * TT],
                                    lhs, rhs,
                                    start=(hch == 0), stop=(hch == NHCH - 2),
                                    perf_mode=mybir.MatmulPerfMode.DoubleRow))

                    wl_mm[n] = t_p.v

                @blk.scalar
                def _(sc_, ech=ech):
                    sc_.wait_ge(t_p.sem, wl_mm[ech * 4 + 3])
                    for th in range(2):
                        a_p.inc(sc_.activation(
                            out=G[:, ech * TOWN + th * TT:ech * TOWN + (th + 1) * TT],
                            in_=FFP[:, th * TT:(th + 1) * TT], func=AF.Silu))

                g_evac[ech] = a_p.v

        G_A = a_p.v
        fence()

        with contextlib.ExitStack() as ed:
            SR = ed.enter_context(nc.sbuf_tensor("SR", [128, 2 * TT], F32))
            XOT = ed.enter_context(nc.sbuf_tensor("XOT", [128, 2 * TT], F32))
            SBT = ed.enter_context(nc.sbuf_tensor("SBT", [128, 2 * TT], F32))
            OT = ed.enter_context(nc.sbuf_tensor("OT", [128, 2 * TT], F32))

            d_ld, d_sc, d_ot, d_st = {}, {}, {}, {}
            for th in range(2):
                for ech in range(NDCH):
                    n = th * NDCH + ech
                    b = n % 2

                    dsem = s_d0 if b == 0 else s_d1

                    @blk.sync
                    def _(sy, th=th, ech=ech, b=b, n=n, dsem=dsem):
                        sy.wait_ge(cc.sem, 1)
                        if n >= 2:
                            sy.wait_ge(a_p.sem, d_sc[n - 2])
                            sy.wait_ge(v_p.sem, d_ot[n - 2])
                        dsem.inc(sy.dma_start(
                            out=SR[:, b * TT:(b + 1) * TT],
                            in_=sred[ech * 128:(ech + 1) * 128, th * TT:(th + 1) * TT]), 16)
                        dsem.inc(sy.dma_start(
                            out=XOT[:, b * TT:(b + 1) * TT],
                            in_=xo[ech * 128:(ech + 1) * 128, th * TT:(th + 1) * TT]), 16)

                    d_ld[n] = (dsem, dsem.v)

                    @blk.scalar
                    def _(sc_, ech=ech, b=b, n=n):
                        sc_.wait_ge(d_ld[n][0].sem, d_ld[n][1])
                        sc_.wait_ge(a_p.sem, G_A)
                        if n >= 2:
                            sc_.wait_ge(v_p.sem, d_ot[n - 2])
                        a_p.inc(sc_.activation(out=SBT[:, b * TT:(b + 1) * TT],
                                               in_=SR[:, b * TT:(b + 1) * TT],
                                               func=AF.Identity, bias=BOUT[:, ech:ech + 1]))

                    d_sc[n] = a_p.v

                    @blk.vector
                    def _(v, th=th, ech=ech, b=b, n=n):
                        v.wait_ge(a_p.sem, d_sc[n])
                        v.wait_ge(v_p.sem, v_p.v)
                        if n >= 2:
                            gsem, gval = d_st[n - 2]
                            v.wait_ge(gsem.sem, gval)
                        v_p.inc(v.tensor_tensor(
                            out=SBT[:, b * TT:(b + 1) * TT],
                            in0=SBT[:, b * TT:(b + 1) * TT],
                            in1=G[:, ech * TOWN + th * TT:ech * TOWN + (th + 1) * TT],
                            op=ALU.mult))
                        v.wait_ge(v_p.sem, v_p.v)
                        v.wait_ge(d_ld[n][0].sem, d_ld[n][1])
                        v_p.inc(v.tensor_tensor(out=OT[:, b * TT:(b + 1) * TT],
                                                in0=SBT[:, b * TT:(b + 1) * TT],
                                                in1=XOT[:, b * TT:(b + 1) * TT],
                                                op=ALU.add))

                    d_ot[n] = v_p.v

                    gosem = g_o0 if b == 0 else g_o1

                    @blk.gpsimd
                    def _(g, th=th, ech=ech, b=b, n=n, gosem=gosem):
                        g.wait_ge(v_p.sem, d_ot[n])
                        gosem.inc(g.dma_start(
                            out=outp[ech * 128:(ech + 1) * 128, th * TT:(th + 1) * TT],
                            in_=OT[:, b * TT:(b + 1) * TT]), 16)

                    d_st[n] = (gosem, gosem.v)

            @blk.gpsimd
            def _(g, a0=g_o0.v, b0=g_o1.v):
                g.wait_ge(g_o0.sem, a0)
                g.wait_ge(g_o1.sem, b0)

    return nc


_NC_CACHE = [None]


def kernel(L0_attn_x, L0_attn_w_qkv, L0_attn_b_qkv, L0_attn_w_out, L0_attn_b_out,
           L0_ffn_w_gating, L0_ffn_w_linear):
    x = np.asarray(L0_attn_x)[0]
    w_qkv = np.asarray(L0_attn_w_qkv)
    b_qkv = np.asarray(L0_attn_b_qkv)
    w_out = np.asarray(L0_attn_w_out)
    b_out = np.asarray(L0_attn_b_out)
    w_g = np.asarray(L0_ffn_w_gating)
    w_l = np.asarray(L0_ffn_w_linear)

    xf = np.ascontiguousarray(x.T)
    woutT_full = np.ascontiguousarray(w_out.T)
    wg0 = np.ascontiguousarray(w_g[0])
    wg1 = np.ascontiguousarray(w_g[1])
    wl = np.ascontiguousarray(w_l)

    in_maps = []
    for c in range(NC):
        b0, b1 = c, 15 - c
        own = np.concatenate([x[b0 * TT:(b0 + 1) * TT], x[b1 * TT:(b1 + 1) * TT]], axis=0)
        xo_c = np.ascontiguousarray(own.T)
        h0 = 2 * c
        rows = np.concatenate([
            w_qkv[h0 * 128:(h0 + 2) * 128],
            w_qkv[2048 + h0 * 128:2048 + (h0 + 2) * 128],
            w_qkv[4096 + h0 * 128:4096 + (h0 + 2) * 128]], axis=0)
        wqkvT_c = np.ascontiguousarray(rows.T)
        bq_c = np.ascontiguousarray(np.concatenate([
            b_qkv[h0 * 128:(h0 + 2) * 128],
            b_qkv[2048 + h0 * 128:2048 + (h0 + 2) * 128],
            b_qkv[4096 + h0 * 128:4096 + (h0 + 2) * 128]], axis=0))
        woutT_c = np.ascontiguousarray(woutT_full[h0 * 128:(h0 + 2) * 128, :])
        in_maps.append({
            "xf": xf, "xo": xo_c, "wqkvT": wqkvT_c, "bqkv": bq_c,
            "woutT": woutT_c, "bout": b_out, "wg0": wg0, "wg1": wg1, "wl": wl,
        })

    if _NC_CACHE[0] is None:
        _NC_CACHE[0] = build_nc()
    res = run_bass_kernel_spmd(_NC_CACHE[0], in_maps, core_ids=list(range(NC)))

    out = np.empty((1, S, D), dtype=np.float32)
    for c in range(NC):
        o = np.asarray(res.results[c]["outp"]).T
        out[0, c * TT:(c + 1) * TT] = o[:TT]
        out[0, (15 - c) * TT:(16 - c) * TT] = o[TT:]
    return out
